# version 06
# baseline (speedup 1.0000x reference)
import contextlib
import numpy as np

import concourse.bass as bass
import concourse.mybir as mybir
from concourse.bass_utils import run_bass_kernel_spmd

F32 = mybir.dt.float32
BF16 = mybir.dt.bfloat16
I32 = mybir.dt.int32
AF = mybir.ActivationFunctionType
ALU = mybir.AluOpType
AX = mybir.AxisListType

H = 1024
INNER = 2048
NST = 16
KCV = 4
RDT = 64
FF = 3072
PD = 128
G = INNER // PD
HC = H // PD
FC = FF // PD
EPS = 1e-5

STREAMS = ("sync", "tensor", "vector", "scalar", "gpsimd", "st",
           "ld_h0", "ld_h1", "ld_s", "ld_f", "ld_x")
ENG_OF = {"sync": "sync", "tensor": "tensor", "vector": "vector",
          "scalar": "scalar", "gpsimd": "gpsimd", "st": "sync",
          "ld_h0": "sync", "ld_h1": "sync", "ld_s": "sync",
          "ld_f": "sync", "ld_x": "sync"}


ENGS5 = ("sync", "tensor", "vector", "scalar", "gpsimd")

GUARDS = {"gate", "D", "E", "ffn"}


class Plan:

    def __init__(self, nc, sems, cfg_ap=None, cfg_ready=None):
        self.nc = nc
        self.sems = sems
        self.cnt = {s: 0 for s in STREAMS}
        self.ops = {e: [] for e in ENGS5}
        self.cfg_ap = cfg_ap
        self.cfg_ready = cfg_ready
        self._br = None

    def op(self, stream, fn, waits=(), inc=None):
        if inc is None:
            inc = 16 if (stream.startswith("ld_") or stream in ("sync", "st")) else 1
        eng = ENG_OF[stream]
        self.ops[eng].append(("op", fn, tuple(waits), inc, stream))
        self.cnt[stream] += inc
        return (stream, self.cnt[stream])

    def now(self, stream):
        return (stream, self.cnt[stream])

    def barrier(self):
        marks = [(s, self.cnt[s]) for s in STREAMS if self.cnt[s] > 0]
        for e in self.ops:
            w = [m for m in marks if ENG_OF[m[0]] != e]
            if w:
                self.ops[e].append(("op", None, tuple(w), 0, e))

    def begin_branch(self, bit, mask_idx=0):
        assert self._br is None
        self._br = [bit, dict(self.cnt), None]
        for e in ENGS5:
            self.ops[e].append(("if", bit, mask_idx))

    def else_branch(self):
        bit, snap, _ = self._br
        self._br[2] = dict(self.cnt)
        self.cnt = dict(snap)
        for e in ENGS5:
            self.ops[e].append(("else",))

    def end_branch(self):
        bit, snap, armA = self._br
        self._br = None
        if armA is None:
            armA = dict(self.cnt)
            self.cnt = dict(snap)
            for e in ENGS5:
                self.ops[e].append(("else",))
        armB = dict(self.cnt)
        final = {s: max(armA[s], armB[s]) for s in STREAMS}
        padA = {s: final[s] - armA[s] for s in STREAMS if final[s] > armA[s]}
        padB = {s: final[s] - armB[s] for s in STREAMS if final[s] > armB[s]}
        self.cnt = final
        for e in ENGS5:
            for k in range(len(self.ops[e]) - 1, -1, -1):
                if self.ops[e][k] == ("else",):
                    self.ops[e][k] = ("else", padA)
                    break
            self.ops[e].append(("endif", padB))

    def _prune_empty_regions(self):
        for e, ops in self.ops.items():
            out = []
            i = 0
            while i < len(ops):
                entry = ops[i]
                if entry[0] != "if":
                    out.append(entry)
                    i += 1
                    continue
                depth = 0
                j = i
                k_else = None
                while True:
                    kind = ops[j][0]
                    if kind == "if":
                        depth += 1
                    elif kind == "endif":
                        depth -= 1
                        if depth == 0:
                            break
                    elif kind == "else" and depth == 1:
                        k_else = j
                    j += 1
                padA = ops[k_else][1]
                padB = ops[j][1]
                body_ops = [ops[m] for m in range(i + 1, j)
                            if ops[m][0] == "op"]
                my_padA = any(ENG_OF[s] == e for s in padA)
                my_padB = any(ENG_OF[s] == e for s in padB)
                inner_marks = any(ops[m][0] in ("if", "else", "endif")
                                  for m in range(i + 1, j)
                                  if m != k_else)
                if not body_ops and not my_padA and not my_padB \
                        and not inner_marks:
                    i = j + 1
                else:
                    out.extend(ops[i:j + 1])
                    i = j + 1
            self.ops[e] = out

    def emit(self, block):
        self._prune_empty_regions()

        def make(eng_name):
            def body(eng):
                has_if = any(entry[0] == "if" for entry in self.ops[eng_name])

                def inner(rmask, rb):
                    satisfied = {}
                    sat_stack = []
                    guard_stack = []
                    loaded = [False]

                    def ensure_mask():
                        if loaded[0]:
                            return
                        loaded[0] = True
                        ws, wv = self.cfg_ready
                        if satisfied.get(ws, 0) < wv:
                            eng.wait_ge(self.sems[ws], wv)
                            satisfied[ws] = wv
                        eng.reg_load(rmask[0], self.cfg_ap[0])
                        eng.reg_load(rmask[1], self.cfg_ap[1])

                    def do_waits(waits):
                        for (ws, wv) in waits:
                            if satisfied.get(ws, 0) >= wv:
                                continue
                            eng.wait_ge(self.sems[ws], wv)
                            satisfied[ws] = wv

                    def do_pads(pad):
                        for s, k in pad.items():
                            if ENG_OF[s] == eng_name:
                                step = 16 if (s.startswith("ld_") or
                                              s in ("st",)) else 1
                                left = k
                                while left > 0:
                                    d = min(step, left)
                                    eng.sem_inc(self.sems[s], d)
                                    left -= d

                    for entry in self.ops[eng_name]:
                        kind = entry[0]
                        if kind == "op":
                            _, fn, waits, inc, stream = entry
                            do_waits(waits)
                            if fn is not None:
                                instr = fn(eng)
                                if inc:
                                    instr.then_inc(self.sems[stream], inc)
                        elif kind == "if":
                            ensure_mask()
                            bit, mi = entry[1], entry[2]
                            eng.reg_alu(rb, rmask[mi], bit,
                                        ALU.logical_shift_right)
                            eng.reg_alu(rb, rb, 1, ALU.bitwise_and)
                            gd = eng.If_ne(rb, 0)
                            gd.__enter__()
                            guard_stack.append(gd)
                            sat_stack.append(dict(satisfied))
                        elif kind == "else":
                            padA = entry[1]
                            do_pads(padA)
                            snap = sat_stack[-1]
                            sat_stack[-1] = dict(satisfied)
                            satisfied.clear()
                            satisfied.update(snap)
                            guard_stack[-1].__exit__(None, None, None)
                            gd = eng.Else()
                            gd.__enter__()
                            guard_stack[-1] = gd
                        elif kind == "endif":
                            padB = entry[1]
                            do_pads(padB)
                            guard_stack.pop().__exit__(None, None, None)
                            armA_sat = sat_stack.pop()
                            merged = {}
                            for k in set(armA_sat) | set(satisfied):
                                merged[k] = min(armA_sat.get(k, 0),
                                                satisfied.get(k, 0))
                            satisfied.clear()
                            satisfied.update(merged)

                if has_if:
                    with eng.register("r_mask0_" + eng_name) as rm0, \
                         eng.register("r_mask1_" + eng_name) as rm1, \
                         eng.register("r_bit_" + eng_name) as rb:
                        inner((rm0, rm1), rb)
                else:
                    inner(None, None)

            return body

        for e in ENGS5:
            if self.ops[e]:
                getattr(block, e)(make(e))


def build(L, T=512):
    NCH = L // T
    TS = T // PD
    SGN = 2
    NSG = G // SGN
    T2 = 256
    TS2 = T2 // PD

    nc = bass.Bass("TRN2", target_bir_lowering=False, debug=False)

    hid_d = nc.dram_tensor("hidden", [L, H], F32, kind="ExternalInput").ap()
    inwT_d = nc.dram_tensor("inwT_dr", [H, 2 * INNER], F32, kind="ExternalInput").ap()
    outwT_d = nc.dram_tensor("outwT_dr", [INNER, H], F32, kind="ExternalInput").ap()
    xpwT_d = nc.dram_tensor("xpwT_dr", [INNER, 96], F32, kind="ExternalInput").ap()
    dtpwT_d = nc.dram_tensor("dtpwT_dr", [RDT, INNER], F32, kind="ExternalInput").ap()
    lnw_d = nc.dram_tensor("lnw_dr", [PD, HC], F32, kind="ExternalInput").ap()
    lnb_d = nc.dram_tensor("lnb_dr", [PD, HC], F32, kind="ExternalInput").ap()
    inb_d = nc.dram_tensor("inb_dr", [PD, 2 * G], F32, kind="ExternalInput").ap()
    cvw_d = nc.dram_tensor("cvw_dr", [PD, G * KCV], F32, kind="ExternalInput").ap()
    cvb_d = nc.dram_tensor("cvb_dr", [PD, G], F32, kind="ExternalInput").ap()
    dtb_d = nc.dram_tensor("dtb_dr", [PD, G], F32, kind="ExternalInput").ap()
    alog_d = nc.dram_tensor("alog_dr", [PD, G * NST], F32, kind="ExternalInput").ap()
    dvec_d = nc.dram_tensor("dvec_dr", [PD, G], F32, kind="ExternalInput").ap()
    outb_d = nc.dram_tensor("outb_dr", [1, H], F32, kind="ExternalInput").ap()
    wgT_d = nc.dram_tensor("wgT_dr", [H, FF], I32, kind="ExternalInput").ap()
    wuT_d = nc.dram_tensor("wuT_dr", [H, FF], I32, kind="ExternalInput").ap()
    wdT_d = nc.dram_tensor("wdT_dr", [FF, H], I32, kind="ExternalInput").ap()

    cfg_d = nc.dram_tensor("cfg_dr", [1, 4], I32, kind="ExternalInput").ap()
    out_d = nc.dram_tensor("out", [L, H], F32, kind="ExternalOutput").ap()
    x1sp_d = nc.dram_tensor("x1sp", [L, H], F32).ap()
    xnTsp_d = nc.dram_tensor("xnTsp", [H, L], BF16).ap()

    es = contextlib.ExitStack()
    SB = lambda name, shape, dt: es.enter_context(nc.sbuf_tensor(name, shape, dt))
    PS = lambda name, shape, dt: es.enter_context(nc.psum_tensor(name, shape, dt))

    ps_a = PS("ps_a", [PD, 512], F32)
    ps_b = PS("ps_b", [PD, 512], F32)
    ps_xp = PS("ps_xp", [PD, 512], F32)
    ps_bb = PS("ps_bb", [PD, 512], F32)
    ps_bb2 = PS("ps_bb2", [PD, 512], F32)
    ps_op = PS("ps_op", [PD, 1024], F32)
    ps_trv = None

    sems = {}
    for s in STREAMS:
        sems[s] = es.enter_context(nc.semaphore("sem_" + s))
    P = Plan(nc, sems)
    cfg_s = SB("cfg_s", [1, 4], I32)

    ident_b = SB("ident_b", [PD, PD], BF16)
    acols = SB("acols", [PD, G * NST], F32)
    cvw = SB("cvw", [PD, G * KCV], F32)
    cvb = SB("cvb", [PD, G], F32)
    dtb = SB("dtb", [PD, G], F32)
    dvec = SB("dvec", [PD, G], F32)
    inb = SB("inb", [PD, 2 * G], F32)
    lnw = SB("lnw", [PD, HC], F32)
    lnb = SB("lnb", [PD, HC], F32)
    outb_t = SB("outb_t", [PD, H], F32)
    states = SB("states", [PD, G * NST], F32)
    onesb = SB("onesb", [1, PD], BF16)
    sel16 = SB("sel16", [16, NST * PD], BF16)
    brow_b = SB("brow_b", [16, T], BF16)
    crow_b = SB("crow_b", [16, T], BF16)
    dtr_b = SB("dtr_b", [RDT, T], BF16)
    tcol = SB("tcol", [PD, 16], F32)
    consts = SB("consts", [PD, 8], F32)

    sweep_es = contextlib.ExitStack()
    SW = lambda name, shape, dt: sweep_es.enter_context(nc.sbuf_tensor(name, shape, dt))
    inwT = SW("inwT", [PD, HC * 2 * INNER], BF16)
    outwT = SW("outwT", [PD, G * H], BF16)
    xpwT = SW("xpwT", [PD, G * 96], BF16)
    dtpwT = SW("dtpwT", [RDT, G * PD], BF16)
    hidt = [SW(f"hidt{i}", [PD, H], F32) for i in range(2)]
    hnT = SW("hnT", [PD, HC * T], BF16)
    xcb = SW("xcb", [PD, G * T], BF16)
    yg = SW("yg", [PD, G * T], BF16)
    halo = SW("halo", [PD, G * 3], F32)
    dt_sg = SW("dt_sg", [PD, SGN * T], F32)
    dtx_sg = SW("dtx_sg", [PD, SGN * T], F32)
    r1c = SW("r1c", [PD, TS], F32)
    m1c = SW("m1c", [PD, TS], F32)
    xnb = SW("xnb", [PD, H], BF16)
    pro_es = contextlib.ExitStack()
    wstage = pro_es.enter_context(nc.sbuf_tensor("wstage", [PD, 2048], F32))
    obrow = pro_es.enter_context(nc.sbuf_tensor("obrow", [1, H], F32))
    obrow_b = pro_es.enter_context(nc.sbuf_tensor("obrow_b", [1, H], BF16))
    pro_es.close()
    d_es = contextlib.ExitStack()
    DW = lambda name, shape, dt: d_es.enter_context(nc.sbuf_tensor(name, shape, dt))
    xg = [DW(f"xg{i}", [PD, T + 3], F32) for i in range(2)]
    zb = [DW(f"zb{i}", [PD, T], F32) for i in range(2)]
    dec = [DW(f"dec{i}", [PD, T], F32) for i in range(2)]
    wti = [DW(f"wti{i}", [PD, T], F32) for i in range(2)]
    srot = DW("srot", [PD, 2 * T], BF16)
    prot = [DW(f"prot{i}", [PD, T], BF16) for i in range(2)]
    cbcr = [DW(f"cbcr{i}", [PD, T], BF16) for i in range(2)]
    yac = DW("yac", [PD, SGN * T], F32)
    d_es.close()
    attn = SW("attn", [PD, H], F32)
    x1t = SW("x1t", [PD, H], F32)
    xnTs = SW("xnTs", [PD, HC * PD], BF16)

    def ldma(dst, src, waits=(), stream="sync"):
        return P.op(stream, lambda e, d=dst, s=src: e.dma_start(out=d, in_=s),
                    waits=waits)

    def sdma(dst, src, waits=()):
        return P.op("st", lambda e, d=dst, s=src: e.dma_start(out=d, in_=s),
                    waits=waits)

    def act(out, in_, func, bias=None, scale=None, accum_out=None, waits=()):
        kw = {}
        if bias is not None:
            kw["bias"] = bias
        if scale is not None:
            kw["scale"] = scale
        if accum_out is not None:
            kw["accum_out"] = accum_out
        return P.op("scalar",
                    lambda e, o=out, i=in_, f=func, kw=kw: e.activation(o, i, f, **kw),
                    waits=waits)

    def tt(out, a, b, op=ALU.mult, waits=()):
        return P.op("vector",
                    lambda e, o=out, x=a, y=b, p=op: e.tensor_tensor(o, x, y, op=p),
                    waits=waits)

    def ts(out, a, s1, op0, waits=()):
        return P.op("vector",
                    lambda e, o=out, x=a, s=s1, p=op0:
                    e.tensor_scalar(o, x, s, None, p),
                    waits=waits)

    def stt(out, in0, scalar, in1, op0, op1, waits=()):
        return P.op("vector",
                    lambda e, o=out, a=in0, s=scalar, b=in1, p0=op0, p1=op1:
                    e.scalar_tensor_tensor(o, a, s, b, op0=p0, op1=p1),
                    waits=waits)

    def vred(out, in_, op=ALU.add, axis=AX.X, waits=()):
        return P.op("vector",
                    lambda e, o=out, i=in_, p=op, ax=axis:
                    e.tensor_reduce(o, i, axis=ax, op=p),
                    waits=waits)

    def vcopy(out, in_, waits=()):
        return P.op("vector",
                    lambda e, o=out, i=in_: e.tensor_copy(o, i), waits=waits)

    def scan(out, d0, d1, init, waits=()):
        return P.op("vector",
                    lambda e, o=out, a=d0, b=d1, i=init:
                    e.tensor_tensor_scan(o, a, b, i, op0=ALU.mult, op1=ALU.add),
                    waits=waits)

    def mm(psum, lhsT, rhs, start, stop, waits=(), inc=1):
        return P.op("tensor",
                    lambda e, o=psum, l=lhsT, r=rhs, st=start, sp=stop:
                    e.matmul(o, l, r, start=st, stop=sp),
                    waits=waits, inc=inc)

    def transpose(psum, in_, waits=(), inc=1):
        return P.op("tensor",
                    lambda e, o=psum, i=in_: e.transpose(o, i, ident_b[:, :]),
                    waits=waits, inc=inc)

    def layernorm(src, dst, waits):
        e_s = vred(tcol[:, 1:2], src, waits=list(waits) + [P.now("vector")])
        e_q = act(ps_op[:, 0:H], src, AF.Square, accum_out=tcol[:, 2:3],
                  waits=list(waits) + [P.now("scalar"), P.now("tensor"),
                                       P.now("vector")])
        e_mu = ts(tcol[:, 3:4], tcol[:, 1:2], 1.0 / H, ALU.mult, waits=[e_s])
        e_e2 = ts(tcol[:, 4:5], tcol[:, 2:3], 1.0 / H, ALU.mult, waits=[e_q])
        e_m2 = tt(tcol[:, 5:6], tcol[:, 3:4], tcol[:, 3:4], ALU.mult, waits=[e_mu])
        e_v = tt(tcol[:, 6:7], tcol[:, 4:5], tcol[:, 5:6], ALU.subtract,
                 waits=[e_e2, e_m2])
        e_ln = act(tcol[:, 7:8], tcol[:, 6:7], AF.Ln, bias=EPS, waits=[e_v])
        e_r = act(tcol[:, 8:9], tcol[:, 7:8], AF.Exp, scale=-0.5, waits=[e_ln])
        e_mn = ts(tcol[:, 9:10], tcol[:, 3:4], -1.0, ALU.mult, waits=[e_mu])
        e_bc = tt(tcol[:, 10:11], tcol[:, 9:10], tcol[:, 8:9], ALU.mult,
                  waits=[e_r, e_mn])
        return act(dst, src, AF.Identity, scale=tcol[:, 8:9], bias=tcol[:, 10:11],
                   waits=[e_bc])

    CONST_VALS = [EPS, 1.0, -1.0, -0.5, 1.0 / 127.0]
    for ci, cv in enumerate(CONST_VALS):
        P.op("gpsimd", lambda e, i=ci, v=cv: e.memset(consts[:, i:i + 1], v),
             inc=1)
        nc.const_aps.register(F32, cv, consts[:, ci:ci + 1])
    e_idf = P.op("gpsimd", lambda e: e.memset(ident_b[:, :], 0.0), inc=1)
    P.op("gpsimd",
         lambda e: e.affine_select(out=ident_b[:, :], in_=ident_b[:, :],
                                   compare_op=ALU.not_equal, fill=1.0,
                                   base=0, pattern=[[-1, PD]],
                                   channel_multiplier=1),
         waits=[e_idf], inc=1)
    P.op("gpsimd", lambda e: e.memset(onesb[:, :], 1.0), inc=1)
    e_s16a = P.op("gpsimd", lambda e: e.memset(sel16[:, :], 0.0), inc=1)
    P.op("gpsimd",
         lambda e: e.affine_select(out=sel16.rearrange("c (n p) -> c n p", n=NST),
                                   in_=sel16.rearrange("c (n p) -> c n p", n=NST),
                                   compare_op=ALU.not_equal, fill=1.0, base=0,
                                   pattern=[[-1, NST], [0, PD]],
                                   channel_multiplier=1),
         waits=[e_s16a], inc=1)
    P.op("gpsimd", lambda e: e.memset(states[:, :], 0.0), inc=1)
    P.op("gpsimd", lambda e: e.memset(yg[:, :], 0.0), inc=1)
    P.op("gpsimd", lambda e: e.memset(halo[:, :], 0.0), inc=1)
    e_pro = P.now("gpsimd")

    small = [(acols, alog_d), (cvw, cvw_d), (cvb, cvb_d), (dtb, dtb_d),
             (dvec, dvec_d), (inb, inb_d), (lnw, lnw_d), (lnb, lnb_d)]
    e_small = [ldma(t[:, :], d, stream="ld_s") for (t, d) in small]
    e_ob = ldma(obrow[:, :], outb_d, stream="ld_s")
    ldma(cfg_s[:, :], cfg_d, stream="ld_s")
    e_lds = P.now("ld_s")
    P.cfg_ap = (cfg_s[0:1, 0:1], cfg_s[0:1, 1:2])
    P.cfg_ready = e_lds
    e_a1 = act(acols[:, :], acols[:, :], AF.Exp, waits=[e_lds])
    ts(acols[:, :], acols[:, :], -1.0, ALU.mult, waits=[e_a1])

    e_obb = act(obrow_b[:, :], obrow[:, :], AF.Copy, waits=[e_lds])
    e_obm = [mm(ps_op[:, hh * 512:(hh + 1) * 512], onesb[:, :],
                obrow_b[:, hh * 512:(hh + 1) * 512], True, True,
                waits=[e_obb, e_pro]) for hh in range(2)]
    vcopy(outb_t[:, :], ps_op[:, :], waits=e_obm)

    def load_cast(dst_sb, dram, rows, cols, col0_dst, prev, scale=None):
        stage_cols = 2048
        for c0 in range(0, cols, stage_cols):
            cw = min(stage_cols, cols - c0)
            w = [prev] if prev else []
            e_l = ldma(wstage[0:rows, 0:cw], dram[:, c0:c0 + cw], waits=w)
            prev = act(dst_sb[0:rows, col0_dst + c0: col0_dst + c0 + cw],
                       wstage[0:rows, 0:cw], AF.Identity, scale=scale, waits=[e_l])
        return prev

    prev = None
    for hc in range(HC):
        prev = load_cast(inwT, inwT_d[hc * PD:(hc + 1) * PD, :], PD,
                         2 * INNER, hc * 2 * INNER, prev)
    for g in range(G):
        prev = load_cast(outwT, outwT_d[g * PD:(g + 1) * PD, :], PD, H,
                         g * H, prev)
        prev = load_cast(xpwT, xpwT_d[g * PD:(g + 1) * PD, :], PD, 96,
                         g * 96, prev)
    prev = load_cast(dtpwT, dtpwT_d, RDT, G * PD, 0, prev)
    P.barrier()

    ps_tr = ps_bb2.bitcast(BF16)

    def emit_chunk(c, is_own):
        for ti in range(TS):
            row0 = c * T + ti * PD
            htile = hidt[ti % 2][:, :]
            e_h = ldma(htile, hid_d[row0:row0 + PD, :],
                       waits=[P.now("scalar"), P.now("vector"),
                              P.now("tensor")], stream="ld_h%d" % (ti % 2))
            sumc, sqc = tcol[:, 1:2], tcol[:, 2:3]
            e_s = vred(sumc, htile, waits=[e_h, P.now("vector")])
            e_q = act(ps_op[:, 0:H], htile, AF.Square, accum_out=sqc,
                      waits=[e_h, P.now("scalar"), P.now("tensor")])
            mu = m1c[:, ti:ti + 1]
            e_mu = ts(mu, sumc, 1.0 / H, ALU.mult, waits=[e_s])
            e_e2 = ts(tcol[:, 4:5], sqc, 1.0 / H, ALU.mult, waits=[e_q])
            e_m2 = tt(tcol[:, 5:6], mu, mu, ALU.mult, waits=[e_mu])
            e_v = tt(tcol[:, 6:7], tcol[:, 4:5], tcol[:, 5:6], ALU.subtract,
                     waits=[e_e2, e_m2])
            e_ln = act(tcol[:, 7:8], tcol[:, 6:7], AF.Ln, bias=EPS, waits=[e_v])
            r1 = r1c[:, ti:ti + 1]
            e_r1 = act(r1, tcol[:, 7:8], AF.Exp, scale=-0.5, waits=[e_ln])
            e_r1sq = tt(tcol[:, 5:6], r1, r1, ALU.mult, waits=[e_r1])
            e_v2 = tt(tcol[:, 13:14], tcol[:, 6:7], tcol[:, 5:6], ALU.mult,
                      waits=[e_r1sq, e_v])
            e_l2 = act(tcol[:, 14:15], tcol[:, 13:14], AF.Ln, bias=EPS,
                       waits=[e_v2, e_r1])
            e_r2 = act(tcol[:, 9:10], tcol[:, 14:15], AF.Exp, scale=-0.5,
                       waits=[e_l2])
            e_sc = tt(tcol[:, 10:11], r1, tcol[:, 9:10], ALU.mult, waits=[e_r2])
            e_mn = ts(tcol[:, 11:12], mu, -1.0, ALU.mult, waits=[e_mu])
            e_bc = tt(tcol[:, 12:13], tcol[:, 11:12], tcol[:, 10:11], ALU.mult,
                      waits=[e_sc, e_mn])
            e_u = act(xnb[:, :], htile, AF.Identity, scale=tcol[:, 10:11],
                      bias=tcol[:, 12:13], waits=[e_bc, P.now("tensor")])
            for h2 in range(HC):
                psv = ps_tr[:, (h2 % 2) * PD:(h2 % 2 + 1) * PD]
                e_tr = transpose(psv, xnb[:, h2 * PD:(h2 + 1) * PD],
                                 waits=[e_u, P.now("scalar")])
                act(hnT[:, h2 * T + ti * PD: h2 * T + (ti + 1) * PD],
                    psv, AF.Identity, scale=lnw[:, h2:h2 + 1],
                    bias=lnb[:, h2:h2 + 1], waits=[e_tr])
        P.barrier()

        for g in range(G):
            xgc = xg[g % 2]
            pp = ps_a if g % 2 == 0 else ps_b
            last = None
            for h2 in range(HC):
                last = mm(pp[:, :], inwT[:, h2 * 2 * INNER + g * PD:
                                         h2 * 2 * INNER + (g + 1) * PD],
                          hnT[:, h2 * T:(h2 + 1) * T],
                          start=(h2 == 0), stop=(h2 == HC - 1),
                          waits=([P.now("scalar")] if h2 == 0 else []),
                          inc=(1 if h2 == HC - 1 else 0))
            e_x = act(xgc[:, 3:3 + T], pp[:, :], AF.Identity,
                      bias=inb[:, g:g + 1], waits=[last, P.now("vector")])
            e_hl = vcopy(xgc[:, 0:3], halo[:, g * 3:(g + 1) * 3],
                         waits=[e_x, P.now("vector")])
            zbc = zb[g % 2][:, 0:T]
            e_ck = ts(zbc, xgc[:, 0:T], cvw[:, g * KCV:g * KCV + 1],
                      ALU.mult, waits=[e_x, e_hl, P.now("scalar")])
            for k in range(1, KCV):
                e_ck = stt(zbc, xgc[:, k:k + T],
                           cvw[:, g * KCV + k:g * KCV + k + 1], zbc,
                           ALU.mult, ALU.add, waits=[e_ck])
            vcopy(halo[:, g * 3:(g + 1) * 3], xgc[:, T:T + 3], waits=[e_x])
            act(xcb[:, g * T:(g + 1) * T], zbc, AF.Silu,
                bias=cvb[:, g:g + 1], waits=[e_ck])
        if is_own:
            for g in range(G):
                pp = ps_a if g % 2 == 0 else ps_b
                last = None
                for h2 in range(HC):
                    last = mm(pp[:, :],
                              inwT[:, h2 * 2 * INNER + INNER + g * PD:
                                   h2 * 2 * INNER + INNER + (g + 1) * PD],
                              hnT[:, h2 * T:(h2 + 1) * T],
                              start=(h2 == 0), stop=(h2 == HC - 1),
                              waits=([P.now("scalar")] if h2 == 0 else []),
                              inc=(1 if h2 == HC - 1 else 0))
                act(yg[:, g * T:(g + 1) * T], pp[:, :], AF.Silu,
                    bias=inb[:, G + g:G + g + 1], waits=[last])
        P.barrier()

        w0 = [P.now("scalar")]
        for g in range(G):
            e_g1 = mm(ps_xp[0:16, :], xpwT[:, g * 96 + 64:g * 96 + 80],
                      xcb[:, g * T:(g + 1) * T], start=(g == 0),
                      stop=(g == G - 1), waits=(w0 if g == 0 else []),
                      inc=(1 if g == G - 1 else 0))
        e_bb = act(brow_b[:, :], ps_xp[0:16, :], AF.Copy, waits=[e_g1])
        e_cb = None
        if is_own:
            for g in range(G):
                e_g2 = mm(ps_xp[32:48, :], xpwT[:, g * 96 + 80:g * 96 + 96],
                          xcb[:, g * T:(g + 1) * T], start=(g == 0),
                          stop=(g == G - 1),
                          waits=([e_bb] if g == 0 else []),
                          inc=(1 if g == G - 1 else 0))
            e_cb = act(crow_b[:, :], ps_xp[32:48, :], AF.Copy, waits=[e_g2])
        for g in range(G):
            e_g3 = mm(ps_xp[64:128, :], xpwT[:, g * 96:g * 96 + 64],
                      xcb[:, g * T:(g + 1) * T], start=(g == 0),
                      stop=(g == G - 1),
                      waits=([e_cb if e_cb else e_bb] if g == 0 else []),
                      inc=(1 if g == G - 1 else 0))
        e_db = act(dtr_b[:, :], ps_xp[64:128, :], AF.Copy, waits=[e_g3])
        P.barrier()

        for sg in range(NSG):
            g0 = sg * SGN
            for i in range(SGN):
                g = g0 + i
                pp = ps_a if i % 2 == 0 else ps_b
                e_m = mm(pp[:, :], dtpwT[:, g * PD:(g + 1) * PD],
                         dtr_b[:, :], True, True,
                         waits=[e_db, P.now("scalar")])
                e_z = act(zb[0][:, 0:T], pp[:, :], AF.Identity,
                          bias=dtb[:, g:g + 1],
                          waits=[e_m, P.now("vector")])
                o0 = i * T
                e_ab = act(dtx_sg[:, o0:o0 + T], zb[0][:, 0:T], AF.Abs,
                           waits=[e_z, P.now("vector")])
                e_ex = act(dtx_sg[:, o0:o0 + T], dtx_sg[:, o0:o0 + T],
                           AF.Exp, scale=-1.0, waits=[e_ab])
                e_ln1 = act(dtx_sg[:, o0:o0 + T], dtx_sg[:, o0:o0 + T],
                            AF.Ln, bias=1.0, waits=[e_ex])
                e_rl = act(dt_sg[:, o0:o0 + T], zb[0][:, 0:T], AF.Relu,
                           waits=[e_ln1])
                tt(dt_sg[:, o0:o0 + T], dt_sg[:, o0:o0 + T],
                   dtx_sg[:, o0:o0 + T], ALU.add, waits=[e_rl])
            Wf = SGN * T
            e_dt = P.now("vector")
            e_dtx = tt(dtx_sg[:, 0:Wf], dt_sg[:, 0:Wf],
                       xcb[:, g0 * T:(g0 + SGN) * T], ALU.mult,
                       waits=[e_dt, P.now("scalar")])
            e_y0 = None
            if is_own:
                e_y0 = P.op("gpsimd",
                            lambda e, b=yac: e.memset(b[:, :], 0.0),
                            waits=[P.now("vector")], inc=1)

            dec_user = [None, None]
            bb_user = [None, None]
            cc_user = None
            cb_user = [None, None]
            s_user = [None, None]
            for n in range(NST):
                kb = n % 2
                pbb = ps_bb if kb == 0 else ps_bb2
                e_bm = mm(pbb[:, :], sel16[:, n * PD:(n + 1) * PD],
                          brow_b[:, :], True, True,
                          waits=[e_bb] + ([bb_user[kb]] if bb_user[kb] else
                                          [P.now("vector"),
                                           P.now("scalar")]))
                e_cv = None
                if is_own:
                    e_cm = mm(ps_xp[:, :], sel16[:, n * PD:(n + 1) * PD],
                              crow_b[:, :], True, True,
                              waits=[e_cb] + ([cc_user] if cc_user else
                                              [P.now("vector")]))
                    kc = n % 2
                    e_cv = act(cbcr[kc][:, :], ps_xp[:, :], AF.Copy,
                               waits=[e_cm] + ([cb_user[kc]] if cb_user[kc]
                                               else []))
                    cc_user = e_cv
                for i in range(SGN):
                    g = g0 + i
                    sl = i % 2
                    kd = (n * SGN + i) % 2
                    e_de = act(dec[kd][:, :], dt_sg[:, i * T:(i + 1) * T],
                               AF.Exp,
                               scale=acols[:, g * NST + n:g * NST + n + 1],
                               waits=([dec_user[kd]] if dec_user[kd]
                                      else [P.now("vector")]))
                    e_w = tt(wti[kd][:, :], dtx_sg[:, i * T:(i + 1) * T],
                             pbb[:, :], ALU.mult, waits=[e_dtx, e_bm])
                    bb_user[kb] = e_w
                    e_sc = scan(srot[:, sl * T:(sl + 1) * T], dec[kd][:, :],
                                wti[kd][:, :],
                                states[:, g * NST + n:g * NST + n + 1],
                                waits=[e_de, e_w] +
                                ([s_user[sl]] if s_user[sl] else []))
                    dec_user[kd] = e_sc
                    s_user[sl] = e_sc
                    if is_own:
                        kc = n % 2
                        e_p = tt(prot[kd][:, :], srot[:, sl * T:(sl + 1) * T],
                                 cbcr[kc][:, :], ALU.mult,
                                 waits=[e_sc, e_cv])
                        cb_user[kc] = e_p
                        tt(yac[:, i * T:(i + 1) * T],
                           yac[:, i * T:(i + 1) * T],
                           prot[kd][:, :], ALU.add,
                           waits=[e_p] + ([e_y0] if n == 0 else []))
                        s_user[sl] = e_p
                    if sl == 1:
                        sv = srot.rearrange("p (i t) -> p i t", i=2)[
                            :, :, T - 1:T]
                        dv = states.rearrange("p (g n) -> p g n", g=G)[
                            :, g0 + i - 1:g0 + i + 1, n:n + 1]
                        e_xt = vcopy(dv, sv, waits=[])
                        s_user[0] = e_xt
                        s_user[1] = e_xt
            if is_own:
                for i in range(SGN):
                    g = g0 + i
                    e_yf = stt(yac[:, i * T:(i + 1) * T],
                               xcb[:, g * T:(g + 1) * T], dvec[:, g:g + 1],
                               yac[:, i * T:(i + 1) * T], ALU.mult, ALU.add,
                               waits=[P.now("vector")])
                    tt(yg[:, g * T:(g + 1) * T], yac[:, i * T:(i + 1) * T],
                       yg[:, g * T:(g + 1) * T], ALU.mult,
                       waits=[e_yf])
        P.barrier()

        if not is_own:
            return
        for t2 in range(TS):
            last = None
            for g in range(G):
                for hh in range(2):
                    last = mm(ps_op[:, hh * 512:(hh + 1) * 512],
                              yg[:, g * T + t2 * PD:g * T + (t2 + 1) * PD],
                              outwT[:, g * H + hh * 512:g * H + (hh + 1) * 512],
                              start=(g == 0), stop=(g == G - 1),
                              waits=([P.now("scalar"), P.now("vector")]
                                     if g == 0 else []),
                              inc=(1 if (g == G - 1 and hh == 1) else 0))
            row0 = c * T + t2 * PD
            htile = hidt[t2 % 2][:, :]
            e_h = ldma(htile, hid_d[row0:row0 + PD, :],
                       waits=[P.now("scalar"), P.now("vector")],
                       stream="ld_h%d" % (t2 % 2))
            mu, r1 = m1c[:, t2:t2 + 1], r1c[:, t2:t2 + 1]
            e_mb = tt(tcol[:, 0:1], mu, r1, ALU.mult, waits=[P.now("vector")])
            e_mb2 = ts(tcol[:, 0:1], tcol[:, 0:1], -1.0, ALU.mult,
                       waits=[e_mb])
            e_l1 = act(attn[:, :], htile, AF.Identity, scale=r1,
                       bias=tcol[:, 0:1], waits=[e_mb2, e_h, P.now("scalar")])
            e_a1 = tt(attn[:, :], attn[:, :], ps_op[:, :], ALU.add,
                      waits=[e_l1, last])
            e_a2 = tt(attn[:, :], attn[:, :], outb_t[:, :], ALU.add,
                      waits=[e_a1])
            e_n2 = layernorm(attn[:, :], attn[:, :], [e_a2])
            e_x1 = tt(x1t[:, :], htile, attn[:, :], ALU.add,
                      waits=[e_n2, P.now("st")])
            sdma(x1sp_d[row0:row0 + PD, :], x1t[:, :], waits=[e_x1])
            e_n3 = layernorm(x1t[:, :], xnb[:, :], [e_x1])
            for h2 in range(HC):
                psv = ps_tr[:, (h2 % 2) * PD:(h2 % 2 + 1) * PD]
                e_tr = transpose(psv, xnb[:, h2 * PD:(h2 + 1) * PD],
                                 waits=[e_n3, P.now("vector")])
                vcopy(xnTs[:, h2 * PD:(h2 + 1) * PD], psv,
                      waits=[e_tr] + ([P.now("st")] if h2 == 0 else []))
            sdma(xnTsp_d[:, row0:row0 + PD].rearrange(
                "(h p) q -> p h q", p=PD),
                xnTs.rearrange("p (h q) -> p h q", h=HC),
                waits=[P.now("vector")])

    for c in range(NCH):
        P.begin_branch(c, 0)
        emit_chunk(c, True)
        P.end_branch()
        P.begin_branch(c, 1)
        emit_chunk(c, False)
        P.end_branch()
        P.barrier()

    P.barrier()
    sweep_es.close()

    ffn_es = contextlib.ExitStack()
    FW = lambda name, shape, dt: ffn_es.enter_context(nc.sbuf_tensor(name, shape, dt))
    wg_sb = FW("wg_sb", [PD, HC * FF], BF16)
    wu_sb = FW("wu_sb", [PD, HC * FF], BF16)
    wd_sb = FW("wd_sb", [PD, FC * H], BF16)
    fstage = FW("fstage", [PD, FF], I32)
    frhs = FW("frhs", [PD, HC * T2], BF16)
    hbuf = FW("hbuf", [PD, FC * T2], BF16)
    sgr = FW("sgr", [PD, T2], F32)
    f_x1 = FW("f_x1", [PD, H * TS2], F32)
    f_out = FW("f_out", [PD, H], F32)
    f_ev = FW("f_ev", [PD, H], F32)

    DQ = 1.0 / 127.0
    prev = None
    for hc2 in range(HC):
        e_l = ldma(fstage[:, :], wgT_d[hc2 * PD:(hc2 + 1) * PD, :],
                   waits=([prev] if prev else []))
        prev = act(wg_sb[:, hc2 * FF:(hc2 + 1) * FF], fstage[:, :], AF.Identity,
                   scale=DQ, waits=[e_l])
    for hc2 in range(HC):
        e_l = ldma(fstage[:, :], wuT_d[hc2 * PD:(hc2 + 1) * PD, :], waits=[prev])
        prev = act(wu_sb[:, hc2 * FF:(hc2 + 1) * FF], fstage[:, :], AF.Identity,
                   scale=DQ, waits=[e_l])
    for f in range(FC):
        e_l = ldma(fstage[:, 0:H], wdT_d[f * PD:(f + 1) * PD, :], waits=[prev])
        prev = act(wd_sb[:, f * H:(f + 1) * H], fstage[:, 0:H], AF.Identity,
                   scale=DQ, waits=[e_l])
    P.barrier()

    f_zero = FW("f_zero", [PD, H], F32)
    e_fz = P.op("gpsimd", lambda e: e.memset(f_zero[:, :], 0.0), inc=1)
    n_half = L // T2
    for j in range(n_half):
        jc = j * T2 // T
        row0 = j * T2
        if "ffn" in GUARDS:
            P.begin_branch(jc)
        e_r = []
        for hc2 in range(HC):
            e_r.append(ldma(frhs[:, hc2 * T2:(hc2 + 1) * T2],
                            xnTsp_d[hc2 * PD:(hc2 + 1) * PD, row0:row0 + T2],
                            waits=[P.now("tensor")], stream="ld_f"))
        e_r = [e_r[-1]]
        e_x1l = [ldma(f_x1[:, i * H:(i + 1) * H],
                      x1sp_d[row0 + i * PD:row0 + (i + 1) * PD, :],
                      waits=[P.now("vector")], stream="ld_x")
                 for i in range(TS2)]
        e_x1l = [e_x1l[-1]] * TS2
        for fo in range(FC):
            lg = lu = None
            for hc2 in range(HC):
                lg = mm(ps_a[:, 0:T2],
                        wg_sb[:, hc2 * FF + fo * PD:hc2 * FF + (fo + 1) * PD],
                        frhs[:, hc2 * T2:(hc2 + 1) * T2],
                        start=(hc2 == 0), stop=(hc2 == HC - 1),
                        waits=(e_r + [P.now("scalar")] if hc2 == 0 else []),
                        inc=(1 if hc2 == HC - 1 else 0))
            for hc2 in range(HC):
                lu = mm(ps_b[:, 0:T2],
                        wu_sb[:, hc2 * FF + fo * PD:hc2 * FF + (fo + 1) * PD],
                        frhs[:, hc2 * T2:(hc2 + 1) * T2],
                        start=(hc2 == 0), stop=(hc2 == HC - 1),
                        waits=([P.now("vector")] if hc2 == 0 else []),
                        inc=(1 if hc2 == HC - 1 else 0))
            e_sg = act(sgr[:, :], ps_a[:, 0:T2], AF.Silu,
                       waits=[lg, P.now("vector")])
            tt(hbuf[:, fo * T2:(fo + 1) * T2], sgr[:, :], ps_b[:, 0:T2],
               ALU.mult, waits=[e_sg, lu])
        for i in range(TS2):
            last = None
            for fo in range(FC):
                for hh in range(2):
                    last = mm(ps_op[:, hh * 512:(hh + 1) * 512],
                              hbuf[:, fo * T2 + i * PD:fo * T2 + (i + 1) * PD],
                              wd_sb[:, fo * H + hh * 512:fo * H + (hh + 1) * 512],
                              start=(fo == 0), stop=(fo == FC - 1),
                              waits=([P.now("vector"), P.now("scalar")]
                                     if fo == 0 else []),
                              inc=(1 if (fo == FC - 1 and hh == 1) else 0))
            e_ev = act(f_ev[:, :], ps_op[:, :], AF.Copy, waits=[last])
            e_n4 = layernorm(f_ev[:, :], f_ev[:, :], [e_ev])
            e_o = tt(f_out[:, :], f_x1[:, i * H:(i + 1) * H], f_ev[:, :],
                     ALU.add, waits=[e_n4, e_x1l[i], P.now("st")])
            sdma(out_d[row0 + i * PD:row0 + (i + 1) * PD, :], f_out[:, :],
                 waits=[e_o])
        if "ffn" in GUARDS:
            P.else_branch()
            for i in range(TS2):
                sdma(out_d[row0 + i * PD:row0 + (i + 1) * PD, :],
                     f_zero[:, :], waits=[e_fz])
            P.end_branch()

    P.op("gpsimd", None, waits=[P.now("st")], inc=0)
    P.barrier()
    ffn_es.close()

    with es, nc.Block() as block:
        P.emit(block)
    return nc


def _prep_weights(inp):
    g = lambda k: np.ascontiguousarray(inp[k])
    w = {}
    w["inwT_dr"] = np.ascontiguousarray(g("L0_attn_in_w").T)
    w["outwT_dr"] = np.ascontiguousarray(g("L0_attn_out_w").T)
    w["xpwT_dr"] = np.ascontiguousarray(g("L0_attn_xp_w").T)
    w["dtpwT_dr"] = np.ascontiguousarray(g("L0_attn_dtp_w").T)
    w["lnw_dr"] = np.ascontiguousarray(g("L0_attn_ln_w").reshape(HC, PD).T)
    w["lnb_dr"] = np.ascontiguousarray(g("L0_attn_ln_b").reshape(HC, PD).T)
    w["inb_dr"] = np.ascontiguousarray(g("L0_attn_in_b").reshape(2 * G, PD).T)
    w["cvw_dr"] = np.ascontiguousarray(
        g("L0_attn_conv_w").reshape(G, PD, KCV).transpose(1, 0, 2).reshape(PD, G * KCV))
    w["cvb_dr"] = np.ascontiguousarray(g("L0_attn_conv_b").reshape(G, PD).T)
    w["dtb_dr"] = np.ascontiguousarray(g("L0_attn_dtp_b").reshape(G, PD).T)
    w["alog_dr"] = np.ascontiguousarray(
        g("L0_attn_A_log").reshape(G, PD, NST).transpose(1, 0, 2).reshape(PD, G * NST))
    w["dvec_dr"] = np.ascontiguousarray(g("L0_attn_D").reshape(G, PD).T)
    w["outb_dr"] = np.ascontiguousarray(g("L0_attn_out_b").reshape(1, H))
    w["wgT_dr"] = np.ascontiguousarray(g("L0_ffn_w_gate").T)
    w["wuT_dr"] = np.ascontiguousarray(g("L0_ffn_w_up").T)
    w["wdT_dr"] = np.ascontiguousarray(g("L0_ffn_w_down").T)
    return w


def kernel(**inputs):
    L = inputs["L0_attn_hidden"].shape[1]
    NCH = L // 512
    w = _prep_weights(inputs)
    nc = build(L)
    frac = [0.0, 0.40625, 0.65625, 0.84375, 1.0]
    bounds = [int(round(f * NCH)) for f in frac]
    in_maps = []
    for core in range(8):
        b, q = core // 4, core % 4
        own = 0
        for c in range(bounds[q], bounds[q + 1]):
            own |= 1 << c
        scanonly = 0
        for c in range(0, bounds[q]):
            scanonly |= 1 << c
        m = dict(w)
        m["hidden"] = np.ascontiguousarray(
            np.asarray(inputs["L0_attn_hidden"][b], dtype=np.float32))
        m["cfg_dr"] = np.array([[own, scanonly, 0, 0]],
                               dtype=np.uint32).view(np.int32)
        in_maps.append(m)
    res = run_bass_kernel_spmd(nc, in_maps, core_ids=list(range(8)))
    out = np.empty((2, L, H), np.float32)
    for core in range(8):
        b, q = core // 4, core % 4
        r0, r1 = bounds[q] * 512, bounds[q + 1] * 512
        out[b, r0:r1] = res.results[core]["out"][r0:r1]
    return out


# version 07
# speedup vs baseline: 1.4538x; 1.4538x over previous
import contextlib
import numpy as np

import concourse.bass as bass
import concourse.mybir as mybir
from concourse.bass_utils import run_bass_kernel_spmd

F32 = mybir.dt.float32
BF16 = mybir.dt.bfloat16
I32 = mybir.dt.int32
AF = mybir.ActivationFunctionType
ALU = mybir.AluOpType
AX = mybir.AxisListType

H = 1024
INNER = 2048
NST = 16
KCV = 4
RDT = 64
FF = 3072
PD = 128
G = INNER // PD
HC = H // PD
FC = FF // PD
EPS = 1e-5

STREAMS = ("sync", "tensor", "vector", "scalar", "gpsimd", "st",
           "ld_h0", "ld_h1", "ld_s", "ld_f", "ld_x")
ENG_OF = {"sync": "sync", "tensor": "tensor", "vector": "vector",
          "scalar": "scalar", "gpsimd": "gpsimd", "st": "sync",
          "ld_h0": "sync", "ld_h1": "sync", "ld_s": "sync",
          "ld_f": "sync", "ld_x": "sync"}


ENGS5 = ("sync", "tensor", "vector", "scalar", "gpsimd")

GUARDS = {"gate", "D", "E", "ffn"}


class Plan:

    def __init__(self, nc, sems, cfg_ap=None, cfg_ready=None):
        self.nc = nc
        self.sems = sems
        self.cnt = {s: 0 for s in STREAMS}
        self.ops = {e: [] for e in ENGS5}
        self.cfg_ap = cfg_ap
        self.cfg_ready = cfg_ready
        self._br = None

    def op(self, stream, fn, waits=(), inc=None):
        if inc is None:
            inc = 16 if (stream.startswith("ld_") or stream in ("sync", "st")) else 1
        eng = ENG_OF[stream]
        self.ops[eng].append(("op", fn, tuple(waits), inc, stream))
        self.cnt[stream] += inc
        return (stream, self.cnt[stream])

    def now(self, stream):
        return (stream, self.cnt[stream])

    def barrier(self):
        marks = [(s, self.cnt[s]) for s in STREAMS if self.cnt[s] > 0]
        for e in self.ops:
            w = [m for m in marks if ENG_OF[m[0]] != e]
            if w:
                self.ops[e].append(("op", None, tuple(w), 0, e))

    def begin_branch(self, bit, mask_idx=0):
        assert self._br is None
        self._br = [bit, dict(self.cnt), None]
        for e in ENGS5:
            self.ops[e].append(("if", bit, mask_idx))

    def else_branch(self):
        bit, snap, _ = self._br
        self._br[2] = dict(self.cnt)
        self.cnt = dict(snap)
        for e in ENGS5:
            self.ops[e].append(("else",))

    def end_branch(self):
        bit, snap, armA = self._br
        self._br = None
        if armA is None:
            armA = dict(self.cnt)
            self.cnt = dict(snap)
            for e in ENGS5:
                self.ops[e].append(("else",))
        armB = dict(self.cnt)
        final = {s: max(armA[s], armB[s]) for s in STREAMS}
        padA = {s: final[s] - armA[s] for s in STREAMS if final[s] > armA[s]}
        padB = {s: final[s] - armB[s] for s in STREAMS if final[s] > armB[s]}
        self.cnt = final
        for e in ENGS5:
            for k in range(len(self.ops[e]) - 1, -1, -1):
                if self.ops[e][k] == ("else",):
                    self.ops[e][k] = ("else", padA)
                    break
            self.ops[e].append(("endif", padB))

    def _prune_empty_regions(self):
        for e, ops in self.ops.items():
            out = []
            i = 0
            while i < len(ops):
                entry = ops[i]
                if entry[0] != "if":
                    out.append(entry)
                    i += 1
                    continue
                depth = 0
                j = i
                k_else = None
                while True:
                    kind = ops[j][0]
                    if kind == "if":
                        depth += 1
                    elif kind == "endif":
                        depth -= 1
                        if depth == 0:
                            break
                    elif kind == "else" and depth == 1:
                        k_else = j
                    j += 1
                padA = ops[k_else][1]
                padB = ops[j][1]
                body_ops = [ops[m] for m in range(i + 1, j)
                            if ops[m][0] == "op"]
                my_padA = any(ENG_OF[s] == e for s in padA)
                my_padB = any(ENG_OF[s] == e for s in padB)
                inner_marks = any(ops[m][0] in ("if", "else", "endif")
                                  for m in range(i + 1, j)
                                  if m != k_else)
                if not body_ops and not my_padA and not my_padB \
                        and not inner_marks:
                    i = j + 1
                else:
                    out.extend(ops[i:j + 1])
                    i = j + 1
            self.ops[e] = out

    def emit(self, block):
        self._prune_empty_regions()

        def make(eng_name):
            def body(eng):
                has_if = any(entry[0] == "if" for entry in self.ops[eng_name])

                def inner(rmask, rb):
                    satisfied = {}
                    sat_stack = []
                    guard_stack = []
                    loaded = [False]

                    def ensure_mask():
                        if loaded[0]:
                            return
                        loaded[0] = True
                        ws, wv = self.cfg_ready
                        if satisfied.get(ws, 0) < wv:
                            eng.wait_ge(self.sems[ws], wv)
                            satisfied[ws] = wv
                        eng.reg_load(rmask[0], self.cfg_ap[0])
                        eng.reg_load(rmask[1], self.cfg_ap[1])

                    def do_waits(waits):
                        for (ws, wv) in waits:
                            if satisfied.get(ws, 0) >= wv:
                                continue
                            eng.wait_ge(self.sems[ws], wv)
                            satisfied[ws] = wv

                    def do_pads(pad):
                        for s, k in pad.items():
                            if ENG_OF[s] == eng_name:
                                step = 16 if (s.startswith("ld_") or
                                              s in ("st",)) else 1
                                left = k
                                while left > 0:
                                    d = min(step, left)
                                    eng.sem_inc(self.sems[s], d)
                                    left -= d

                    for entry in self.ops[eng_name]:
                        kind = entry[0]
                        if kind == "op":
                            _, fn, waits, inc, stream = entry
                            do_waits(waits)
                            if fn is not None:
                                instr = fn(eng)
                                if inc:
                                    instr.then_inc(self.sems[stream], inc)
                        elif kind == "if":
                            ensure_mask()
                            bit, mi = entry[1], entry[2]
                            eng.reg_alu(rb, rmask[mi], bit,
                                        ALU.logical_shift_right)
                            eng.reg_alu(rb, rb, 1, ALU.bitwise_and)
                            gd = eng.If_ne(rb, 0)
                            gd.__enter__()
                            guard_stack.append(gd)
                            sat_stack.append(dict(satisfied))
                        elif kind == "else":
                            padA = entry[1]
                            do_pads(padA)
                            snap = sat_stack[-1]
                            sat_stack[-1] = dict(satisfied)
                            satisfied.clear()
                            satisfied.update(snap)
                            guard_stack[-1].__exit__(None, None, None)
                            gd = eng.Else()
                            gd.__enter__()
                            guard_stack[-1] = gd
                        elif kind == "endif":
                            padB = entry[1]
                            do_pads(padB)
                            guard_stack.pop().__exit__(None, None, None)
                            armA_sat = sat_stack.pop()
                            merged = {}
                            for k in set(armA_sat) | set(satisfied):
                                merged[k] = min(armA_sat.get(k, 0),
                                                satisfied.get(k, 0))
                            satisfied.clear()
                            satisfied.update(merged)

                if has_if:
                    with eng.register("r_mask0_" + eng_name) as rm0, \
                         eng.register("r_mask1_" + eng_name) as rm1, \
                         eng.register("r_bit_" + eng_name) as rb:
                        inner((rm0, rm1), rb)
                else:
                    inner(None, None)

            return body

        for e in ENGS5:
            if self.ops[e]:
                getattr(block, e)(make(e))


def build(L, T=512):
    NCH = L // T
    TS = T // PD
    SGN = 2
    NSG = G // SGN
    T2 = 256
    TS2 = T2 // PD

    nc = bass.Bass("TRN2", target_bir_lowering=False, debug=False)

    hid_d = nc.dram_tensor("hidden", [L, H], F32, kind="ExternalInput").ap()
    inwT_d = nc.dram_tensor("inwT_dr", [H, 2 * INNER], F32, kind="ExternalInput").ap()
    outwT_d = nc.dram_tensor("outwT_dr", [INNER, H], F32, kind="ExternalInput").ap()
    xpwT_d = nc.dram_tensor("xpwT_dr", [INNER, 96], F32, kind="ExternalInput").ap()
    dtpwT_d = nc.dram_tensor("dtpwT_dr", [RDT, INNER], F32, kind="ExternalInput").ap()
    lnw_d = nc.dram_tensor("lnw_dr", [PD, HC], F32, kind="ExternalInput").ap()
    lnb_d = nc.dram_tensor("lnb_dr", [PD, HC], F32, kind="ExternalInput").ap()
    inb_d = nc.dram_tensor("inb_dr", [PD, 2 * G], F32, kind="ExternalInput").ap()
    cvw_d = nc.dram_tensor("cvw_dr", [PD, G * KCV], F32, kind="ExternalInput").ap()
    cvb_d = nc.dram_tensor("cvb_dr", [PD, G], F32, kind="ExternalInput").ap()
    dtb_d = nc.dram_tensor("dtb_dr", [PD, G], F32, kind="ExternalInput").ap()
    alog_d = nc.dram_tensor("alog_dr", [PD, G * NST], F32, kind="ExternalInput").ap()
    dvec_d = nc.dram_tensor("dvec_dr", [PD, G], F32, kind="ExternalInput").ap()
    outb_d = nc.dram_tensor("outb_dr", [1, H], F32, kind="ExternalInput").ap()
    wgT_d = nc.dram_tensor("wgT_dr", [H, FF], I32, kind="ExternalInput").ap()
    wuT_d = nc.dram_tensor("wuT_dr", [H, FF], I32, kind="ExternalInput").ap()
    wdT_d = nc.dram_tensor("wdT_dr", [FF, H], I32, kind="ExternalInput").ap()

    cfg_d = nc.dram_tensor("cfg_dr", [1, 4], I32, kind="ExternalInput").ap()
    out_d = nc.dram_tensor("out", [L, H], F32, kind="ExternalOutput").ap()
    x1sp_d = nc.dram_tensor("x1sp", [L, H], F32).ap()
    xnTsp_d = nc.dram_tensor("xnTsp", [H, L], BF16).ap()

    es = contextlib.ExitStack()
    SB = lambda name, shape, dt: es.enter_context(nc.sbuf_tensor(name, shape, dt))
    PS = lambda name, shape, dt: es.enter_context(nc.psum_tensor(name, shape, dt))

    ps_a = PS("ps_a", [PD, 512], F32)
    ps_b = PS("ps_b", [PD, 512], F32)
    ps_xp = PS("ps_xp", [PD, 512], F32)
    ps_bb = PS("ps_bb", [PD, 512], F32)
    ps_bb2 = PS("ps_bb2", [PD, 512], F32)
    ps_op = PS("ps_op", [PD, 1024], F32)
    ps_trv = None

    sems = {}
    for s in STREAMS:
        sems[s] = es.enter_context(nc.semaphore("sem_" + s))
    P = Plan(nc, sems)
    cfg_s = SB("cfg_s", [1, 4], I32)

    ident_b = SB("ident_b", [PD, PD], BF16)
    acols = SB("acols", [PD, G * NST], F32)
    cvw = SB("cvw", [PD, G * KCV], F32)
    cvb = SB("cvb", [PD, G], F32)
    dtb = SB("dtb", [PD, G], F32)
    dvec = SB("dvec", [PD, G], F32)
    inb = SB("inb", [PD, 2 * G], F32)
    lnw = SB("lnw", [PD, HC], F32)
    lnb = SB("lnb", [PD, HC], F32)
    outb_t = SB("outb_t", [PD, H], F32)
    states = SB("states", [PD, G * NST], F32)
    onesb = SB("onesb", [1, PD], BF16)
    sel16 = SB("sel16", [16, NST * PD], BF16)
    brow_b = SB("brow_b", [16, T], BF16)
    crow_b = SB("crow_b", [16, T], BF16)
    dtr_b = SB("dtr_b", [RDT, T], BF16)
    tcol = SB("tcol", [PD, 16], F32)
    consts = SB("consts", [PD, 8], F32)

    sweep_es = contextlib.ExitStack()
    SW = lambda name, shape, dt: sweep_es.enter_context(nc.sbuf_tensor(name, shape, dt))
    inwT = SW("inwT", [PD, HC * 2 * INNER], BF16)
    outwT = SW("outwT", [PD, G * H], BF16)
    xpwT = SW("xpwT", [PD, G * 96], BF16)
    dtpwT = SW("dtpwT", [RDT, G * PD], BF16)
    hidt = [SW(f"hidt{i}", [PD, H], F32) for i in range(2)]
    hnT = SW("hnT", [PD, HC * T], BF16)
    xcb = SW("xcb", [PD, G * T], BF16)
    yg = SW("yg", [PD, G * T], BF16)
    halo = SW("halo", [PD, G * 3], F32)
    dt_sg = SW("dt_sg", [PD, SGN * T], F32)
    dtx_sg = SW("dtx_sg", [PD, SGN * T], F32)
    r1c = SW("r1c", [PD, TS], F32)
    m1c = SW("m1c", [PD, TS], F32)
    xnb = SW("xnb", [PD, H], BF16)
    pro_es = contextlib.ExitStack()
    wstage = pro_es.enter_context(nc.sbuf_tensor("wstage", [PD, 2048], F32))
    obrow = pro_es.enter_context(nc.sbuf_tensor("obrow", [1, H], F32))
    obrow_b = pro_es.enter_context(nc.sbuf_tensor("obrow_b", [1, H], BF16))
    pro_es.close()
    d_es = contextlib.ExitStack()
    DW = lambda name, shape, dt: d_es.enter_context(nc.sbuf_tensor(name, shape, dt))
    xg = [DW(f"xg{i}", [PD, T + 3], F32) for i in range(2)]
    zb = [DW(f"zb{i}", [PD, T], F32) for i in range(2)]
    dec = [DW(f"dec{i}", [PD, T], F32) for i in range(2)]
    wti = [DW(f"wti{i}", [PD, T], F32) for i in range(2)]
    srot = DW("srot", [PD, 2 * T], BF16)
    prot = [DW(f"prot{i}", [PD, T], BF16) for i in range(2)]
    cbcr = [DW(f"cbcr{i}", [PD, T], BF16) for i in range(2)]
    bbr = [DW(f"bbr{i}", [PD, T], F32) for i in range(2)]
    yac = DW("yac", [PD, SGN * T], F32)
    d_es.close()
    attn = SW("attn", [PD, H], F32)
    x1t = SW("x1t", [PD, H], F32)
    xnTs = SW("xnTs", [PD, HC * PD], BF16)

    def ldma(dst, src, waits=(), stream="sync"):
        return P.op(stream, lambda e, d=dst, s=src: e.dma_start(out=d, in_=s),
                    waits=waits)

    def sdma(dst, src, waits=()):
        return P.op("st", lambda e, d=dst, s=src: e.dma_start(out=d, in_=s),
                    waits=waits)

    def act(out, in_, func, bias=None, scale=None, accum_out=None, waits=()):
        kw = {}
        if bias is not None:
            kw["bias"] = bias
        if scale is not None:
            kw["scale"] = scale
        if accum_out is not None:
            kw["accum_out"] = accum_out
        return P.op("scalar",
                    lambda e, o=out, i=in_, f=func, kw=kw: e.activation(o, i, f, **kw),
                    waits=waits)

    def tt(out, a, b, op=ALU.mult, waits=()):
        return P.op("vector",
                    lambda e, o=out, x=a, y=b, p=op: e.tensor_tensor(o, x, y, op=p),
                    waits=waits)

    def gtt(out, a, b, op=ALU.mult, waits=()):
        return P.op("gpsimd",
                    lambda e, o=out, x=a, y=b, p=op: e.tensor_tensor(o, x, y, op=p),
                    waits=waits)

    def ts(out, a, s1, op0, waits=()):
        return P.op("vector",
                    lambda e, o=out, x=a, s=s1, p=op0:
                    e.tensor_scalar(o, x, s, None, p),
                    waits=waits)

    def stt(out, in0, scalar, in1, op0, op1, waits=()):
        return P.op("vector",
                    lambda e, o=out, a=in0, s=scalar, b=in1, p0=op0, p1=op1:
                    e.scalar_tensor_tensor(o, a, s, b, op0=p0, op1=p1),
                    waits=waits)

    def vred(out, in_, op=ALU.add, axis=AX.X, waits=()):
        return P.op("vector",
                    lambda e, o=out, i=in_, p=op, ax=axis:
                    e.tensor_reduce(o, i, axis=ax, op=p),
                    waits=waits)

    def vcopy(out, in_, waits=()):
        return P.op("vector",
                    lambda e, o=out, i=in_: e.tensor_copy(o, i), waits=waits)

    def scan(out, d0, d1, init, waits=()):
        return P.op("vector",
                    lambda e, o=out, a=d0, b=d1, i=init:
                    e.tensor_tensor_scan(o, a, b, i, op0=ALU.mult, op1=ALU.add),
                    waits=waits)

    def mm(psum, lhsT, rhs, start, stop, waits=(), inc=1):
        return P.op("tensor",
                    lambda e, o=psum, l=lhsT, r=rhs, st=start, sp=stop:
                    e.matmul(o, l, r, start=st, stop=sp),
                    waits=waits, inc=inc)

    def transpose(psum, in_, waits=(), inc=1):
        return P.op("tensor",
                    lambda e, o=psum, i=in_: e.transpose(o, i, ident_b[:, :]),
                    waits=waits, inc=inc)

    def layernorm(src, dst, waits):
        e_s = vred(tcol[:, 1:2], src, waits=list(waits) + [P.now("vector")])
        e_q = act(ps_op[:, 0:H], src, AF.Square, accum_out=tcol[:, 2:3],
                  waits=list(waits) + [P.now("scalar"), P.now("tensor"),
                                       P.now("vector")])
        e_mu = ts(tcol[:, 3:4], tcol[:, 1:2], 1.0 / H, ALU.mult, waits=[e_s])
        e_e2 = ts(tcol[:, 4:5], tcol[:, 2:3], 1.0 / H, ALU.mult, waits=[e_q])
        e_m2 = tt(tcol[:, 5:6], tcol[:, 3:4], tcol[:, 3:4], ALU.mult, waits=[e_mu])
        e_v = tt(tcol[:, 6:7], tcol[:, 4:5], tcol[:, 5:6], ALU.subtract,
                 waits=[e_e2, e_m2])
        e_ln = act(tcol[:, 7:8], tcol[:, 6:7], AF.Ln, bias=EPS, waits=[e_v])
        e_r = act(tcol[:, 8:9], tcol[:, 7:8], AF.Exp, scale=-0.5, waits=[e_ln])
        e_mn = ts(tcol[:, 9:10], tcol[:, 3:4], -1.0, ALU.mult, waits=[e_mu])
        e_bc = tt(tcol[:, 10:11], tcol[:, 9:10], tcol[:, 8:9], ALU.mult,
                  waits=[e_r, e_mn])
        return act(dst, src, AF.Identity, scale=tcol[:, 8:9], bias=tcol[:, 10:11],
                   waits=[e_bc])

    CONST_VALS = [EPS, 1.0, -1.0, -0.5, 1.0 / 127.0]
    for ci, cv in enumerate(CONST_VALS):
        P.op("gpsimd", lambda e, i=ci, v=cv: e.memset(consts[:, i:i + 1], v),
             inc=1)
        nc.const_aps.register(F32, cv, consts[:, ci:ci + 1])
    e_idf = P.op("gpsimd", lambda e: e.memset(ident_b[:, :], 0.0), inc=1)
    P.op("gpsimd",
         lambda e: e.affine_select(out=ident_b[:, :], in_=ident_b[:, :],
                                   compare_op=ALU.not_equal, fill=1.0,
                                   base=0, pattern=[[-1, PD]],
                                   channel_multiplier=1),
         waits=[e_idf], inc=1)
    P.op("gpsimd", lambda e: e.memset(onesb[:, :], 1.0), inc=1)
    e_s16a = P.op("gpsimd", lambda e: e.memset(sel16[:, :], 0.0), inc=1)
    P.op("gpsimd",
         lambda e: e.affine_select(out=sel16.rearrange("c (n p) -> c n p", n=NST),
                                   in_=sel16.rearrange("c (n p) -> c n p", n=NST),
                                   compare_op=ALU.not_equal, fill=1.0, base=0,
                                   pattern=[[-1, NST], [0, PD]],
                                   channel_multiplier=1),
         waits=[e_s16a], inc=1)
    P.op("gpsimd", lambda e: e.memset(states[:, :], 0.0), inc=1)
    P.op("gpsimd", lambda e: e.memset(yg[:, :], 0.0), inc=1)
    P.op("gpsimd", lambda e: e.memset(halo[:, :], 0.0), inc=1)
    e_pro = P.now("gpsimd")

    small = [(acols, alog_d), (cvw, cvw_d), (cvb, cvb_d), (dtb, dtb_d),
             (dvec, dvec_d), (inb, inb_d), (lnw, lnw_d), (lnb, lnb_d)]
    e_small = [ldma(t[:, :], d, stream="ld_s") for (t, d) in small]
    e_ob = ldma(obrow[:, :], outb_d, stream="ld_s")
    ldma(cfg_s[:, :], cfg_d, stream="ld_s")
    e_lds = P.now("ld_s")
    P.cfg_ap = (cfg_s[0:1, 0:1], cfg_s[0:1, 1:2])
    P.cfg_ready = e_lds
    e_a1 = act(acols[:, :], acols[:, :], AF.Exp, waits=[e_lds])
    ts(acols[:, :], acols[:, :], -1.0, ALU.mult, waits=[e_a1])

    e_obb = act(obrow_b[:, :], obrow[:, :], AF.Copy, waits=[e_lds])
    e_obm = [mm(ps_op[:, hh * 512:(hh + 1) * 512], onesb[:, :],
                obrow_b[:, hh * 512:(hh + 1) * 512], True, True,
                waits=[e_obb, e_pro]) for hh in range(2)]
    vcopy(outb_t[:, :], ps_op[:, :], waits=e_obm)

    def load_cast(dst_sb, dram, rows, cols, col0_dst, prev, scale=None):
        stage_cols = 2048
        for c0 in range(0, cols, stage_cols):
            cw = min(stage_cols, cols - c0)
            w = [prev] if prev else []
            e_l = ldma(wstage[0:rows, 0:cw], dram[:, c0:c0 + cw], waits=w)
            prev = act(dst_sb[0:rows, col0_dst + c0: col0_dst + c0 + cw],
                       wstage[0:rows, 0:cw], AF.Identity, scale=scale, waits=[e_l])
        return prev

    prev = None
    for hc in range(HC):
        prev = load_cast(inwT, inwT_d[hc * PD:(hc + 1) * PD, :], PD,
                         2 * INNER, hc * 2 * INNER, prev)
    for g in range(G):
        prev = load_cast(outwT, outwT_d[g * PD:(g + 1) * PD, :], PD, H,
                         g * H, prev)
        prev = load_cast(xpwT, xpwT_d[g * PD:(g + 1) * PD, :], PD, 96,
                         g * 96, prev)
    prev = load_cast(dtpwT, dtpwT_d, RDT, G * PD, 0, prev)
    P.barrier()

    ps_tr = ps_bb2.bitcast(BF16)

    def emit_chunk(c, is_own):
        for ti in range(TS):
            row0 = c * T + ti * PD
            htile = hidt[ti % 2][:, :]
            e_h = ldma(htile, hid_d[row0:row0 + PD, :],
                       waits=[P.now("scalar"), P.now("vector"),
                              P.now("tensor")], stream="ld_h%d" % (ti % 2))
            sumc, sqc = tcol[:, 1:2], tcol[:, 2:3]
            e_s = vred(sumc, htile, waits=[e_h, P.now("vector")])
            e_q = act(ps_op[:, 0:H], htile, AF.Square, accum_out=sqc,
                      waits=[e_h, P.now("scalar"), P.now("tensor")])
            mu = m1c[:, ti:ti + 1]
            e_mu = ts(mu, sumc, 1.0 / H, ALU.mult, waits=[e_s])
            e_e2 = ts(tcol[:, 4:5], sqc, 1.0 / H, ALU.mult, waits=[e_q])
            e_m2 = tt(tcol[:, 5:6], mu, mu, ALU.mult, waits=[e_mu])
            e_v = tt(tcol[:, 6:7], tcol[:, 4:5], tcol[:, 5:6], ALU.subtract,
                     waits=[e_e2, e_m2])
            e_ln = act(tcol[:, 7:8], tcol[:, 6:7], AF.Ln, bias=EPS, waits=[e_v])
            r1 = r1c[:, ti:ti + 1]
            e_r1 = act(r1, tcol[:, 7:8], AF.Exp, scale=-0.5, waits=[e_ln])
            e_r1sq = tt(tcol[:, 5:6], r1, r1, ALU.mult, waits=[e_r1])
            e_v2 = tt(tcol[:, 13:14], tcol[:, 6:7], tcol[:, 5:6], ALU.mult,
                      waits=[e_r1sq, e_v])
            e_l2 = act(tcol[:, 14:15], tcol[:, 13:14], AF.Ln, bias=EPS,
                       waits=[e_v2, e_r1])
            e_r2 = act(tcol[:, 9:10], tcol[:, 14:15], AF.Exp, scale=-0.5,
                       waits=[e_l2])
            e_sc = tt(tcol[:, 10:11], r1, tcol[:, 9:10], ALU.mult, waits=[e_r2])
            e_mn = ts(tcol[:, 11:12], mu, -1.0, ALU.mult, waits=[e_mu])
            e_bc = tt(tcol[:, 12:13], tcol[:, 11:12], tcol[:, 10:11], ALU.mult,
                      waits=[e_sc, e_mn])
            e_u = act(xnb[:, :], htile, AF.Identity, scale=tcol[:, 10:11],
                      bias=tcol[:, 12:13], waits=[e_bc, P.now("tensor")])
            for h2 in range(HC):
                psv = ps_tr[:, (h2 % 2) * PD:(h2 % 2 + 1) * PD]
                e_tr = transpose(psv, xnb[:, h2 * PD:(h2 + 1) * PD],
                                 waits=[e_u, P.now("scalar")])
                act(hnT[:, h2 * T + ti * PD: h2 * T + (ti + 1) * PD],
                    psv, AF.Identity, scale=lnw[:, h2:h2 + 1],
                    bias=lnb[:, h2:h2 + 1], waits=[e_tr])
        P.barrier()

        for g in range(G):
            xgc = xg[g % 2]
            pp = ps_a if g % 2 == 0 else ps_b
            last = None
            for h2 in range(HC):
                last = mm(pp[:, :], inwT[:, h2 * 2 * INNER + g * PD:
                                         h2 * 2 * INNER + (g + 1) * PD],
                          hnT[:, h2 * T:(h2 + 1) * T],
                          start=(h2 == 0), stop=(h2 == HC - 1),
                          waits=([P.now("scalar")] if h2 == 0 else []),
                          inc=(1 if h2 == HC - 1 else 0))
            e_x = act(xgc[:, 3:3 + T], pp[:, :], AF.Identity,
                      bias=inb[:, g:g + 1], waits=[last, P.now("vector")])
            e_hl = vcopy(xgc[:, 0:3], halo[:, g * 3:(g + 1) * 3],
                         waits=[e_x, P.now("vector")])
            zbc = zb[g % 2][:, 0:T]
            e_ck = ts(zbc, xgc[:, 0:T], cvw[:, g * KCV:g * KCV + 1],
                      ALU.mult, waits=[e_x, e_hl, P.now("scalar")])
            for k in range(1, KCV):
                e_ck = stt(zbc, xgc[:, k:k + T],
                           cvw[:, g * KCV + k:g * KCV + k + 1], zbc,
                           ALU.mult, ALU.add, waits=[e_ck])
            vcopy(halo[:, g * 3:(g + 1) * 3], xgc[:, T:T + 3], waits=[e_x])
            act(xcb[:, g * T:(g + 1) * T], zbc, AF.Silu,
                bias=cvb[:, g:g + 1], waits=[e_ck])
        if is_own:
            for g in range(G):
                pp = ps_a if g % 2 == 0 else ps_b
                last = None
                for h2 in range(HC):
                    last = mm(pp[:, :],
                              inwT[:, h2 * 2 * INNER + INNER + g * PD:
                                   h2 * 2 * INNER + INNER + (g + 1) * PD],
                              hnT[:, h2 * T:(h2 + 1) * T],
                              start=(h2 == 0), stop=(h2 == HC - 1),
                              waits=([P.now("scalar")] if h2 == 0 else []),
                              inc=(1 if h2 == HC - 1 else 0))
                act(yg[:, g * T:(g + 1) * T], pp[:, :], AF.Silu,
                    bias=inb[:, G + g:G + g + 1], waits=[last])
        P.barrier()

        w0 = [P.now("scalar")]
        for g in range(G):
            e_g1 = mm(ps_xp[0:16, :], xpwT[:, g * 96 + 64:g * 96 + 80],
                      xcb[:, g * T:(g + 1) * T], start=(g == 0),
                      stop=(g == G - 1), waits=(w0 if g == 0 else []),
                      inc=(1 if g == G - 1 else 0))
        e_bb = act(brow_b[:, :], ps_xp[0:16, :], AF.Copy, waits=[e_g1])
        e_cb = None
        if is_own:
            for g in range(G):
                e_g2 = mm(ps_xp[32:48, :], xpwT[:, g * 96 + 80:g * 96 + 96],
                          xcb[:, g * T:(g + 1) * T], start=(g == 0),
                          stop=(g == G - 1),
                          waits=([e_bb] if g == 0 else []),
                          inc=(1 if g == G - 1 else 0))
            e_cb = act(crow_b[:, :], ps_xp[32:48, :], AF.Copy, waits=[e_g2])
        for g in range(G):
            e_g3 = mm(ps_xp[64:128, :], xpwT[:, g * 96:g * 96 + 64],
                      xcb[:, g * T:(g + 1) * T], start=(g == 0),
                      stop=(g == G - 1),
                      waits=([e_cb if e_cb else e_bb] if g == 0 else []),
                      inc=(1 if g == G - 1 else 0))
        e_db = act(dtr_b[:, :], ps_xp[64:128, :], AF.Copy, waits=[e_g3])
        P.barrier()

        for sg in range(NSG):
            g0 = sg * SGN
            for i in range(SGN):
                g = g0 + i
                pp = ps_a if i % 2 == 0 else ps_b
                e_m = mm(pp[:, :], dtpwT[:, g * PD:(g + 1) * PD],
                         dtr_b[:, :], True, True,
                         waits=[e_db, P.now("scalar")])
                e_z = act(zb[0][:, 0:T], pp[:, :], AF.Identity,
                          bias=dtb[:, g:g + 1],
                          waits=[e_m, P.now("vector")])
                o0 = i * T
                e_ab = act(dtx_sg[:, o0:o0 + T], zb[0][:, 0:T], AF.Abs,
                           waits=[e_z, P.now("vector")])
                e_ex = act(dtx_sg[:, o0:o0 + T], dtx_sg[:, o0:o0 + T],
                           AF.Exp, scale=-1.0, waits=[e_ab])
                e_ln1 = act(dtx_sg[:, o0:o0 + T], dtx_sg[:, o0:o0 + T],
                            AF.Ln, bias=1.0, waits=[e_ex])
                e_rl = act(dt_sg[:, o0:o0 + T], zb[0][:, 0:T], AF.Relu,
                           waits=[e_ln1])
                tt(dt_sg[:, o0:o0 + T], dt_sg[:, o0:o0 + T],
                   dtx_sg[:, o0:o0 + T], ALU.add, waits=[e_rl])
            Wf = SGN * T
            e_dt = P.now("vector")
            e_dtx = tt(dtx_sg[:, 0:Wf], dt_sg[:, 0:Wf],
                       xcb[:, g0 * T:(g0 + SGN) * T], ALU.mult,
                       waits=[e_dt, P.now("scalar")])
            e_y0 = None
            if is_own:
                e_y0 = P.op("gpsimd",
                            lambda e, b=yac: e.memset(b[:, :], 0.0),
                            waits=[P.now("vector")], inc=1)

            dec_user = [None, None]
            bb_user = [None, None]
            bbe_user = [None, None]
            cc_user = None
            cb_user = [None, None]
            s_user = [None, None]
            for n in range(NST):
                kb = n % 2
                pbb = ps_bb if kb == 0 else ps_bb2
                e_bm = mm(pbb[:, :], sel16[:, n * PD:(n + 1) * PD],
                          brow_b[:, :], True, True,
                          waits=[e_bb] + ([bbe_user[kb]] if bbe_user[kb] else
                                          [P.now("vector"),
                                           P.now("scalar")]))
                e_be = act(bbr[kb][:, :], pbb[:, :], AF.Copy,
                           waits=[e_bm] + ([bb_user[kb]] if bb_user[kb]
                                           else []))
                bbe_user[kb] = e_be
                e_cv = None
                if is_own:
                    e_cm = mm(ps_xp[:, :], sel16[:, n * PD:(n + 1) * PD],
                              crow_b[:, :], True, True,
                              waits=[e_cb] + ([cc_user] if cc_user else
                                              [P.now("vector")]))
                    kc = n % 2
                    e_cv = act(cbcr[kc][:, :], ps_xp[:, :], AF.Copy,
                               waits=[e_cm] + ([cb_user[kc]] if cb_user[kc]
                                               else []))
                    cc_user = e_cv
                for i in range(SGN):
                    g = g0 + i
                    sl = i % 2
                    kd = (n * SGN + i) % 2
                    e_de = act(dec[kd][:, :], dt_sg[:, i * T:(i + 1) * T],
                               AF.Exp,
                               scale=acols[:, g * NST + n:g * NST + n + 1],
                               waits=([dec_user[kd]] if dec_user[kd]
                                      else [P.now("vector")]))
                    e_w = gtt(wti[kd][:, :], dtx_sg[:, i * T:(i + 1) * T],
                              bbr[kb][:, :], ALU.mult,
                              waits=[e_dtx, e_be] +
                              ([dec_user[kd]] if dec_user[kd] else []))
                    bb_user[kb] = e_w
                    e_sc = scan(srot[:, sl * T:(sl + 1) * T], dec[kd][:, :],
                                wti[kd][:, :],
                                states[:, g * NST + n:g * NST + n + 1],
                                waits=[e_de, e_w] +
                                ([s_user[sl]] if s_user[sl] else []))
                    dec_user[kd] = e_sc
                    s_user[sl] = e_sc
                    if is_own:
                        kc = n % 2
                        e_p = tt(prot[kd][:, :], srot[:, sl * T:(sl + 1) * T],
                                 cbcr[kc][:, :], ALU.mult,
                                 waits=[e_sc, e_cv])
                        cb_user[kc] = e_p
                        tt(yac[:, i * T:(i + 1) * T],
                           yac[:, i * T:(i + 1) * T],
                           prot[kd][:, :], ALU.add,
                           waits=[e_p] + ([e_y0] if n == 0 else []))
                        s_user[sl] = e_p
                    if sl == 1:
                        sv = srot.rearrange("p (i t) -> p i t", i=2)[
                            :, :, T - 1:T]
                        dv = states.rearrange("p (g n) -> p g n", g=G)[
                            :, g0 + i - 1:g0 + i + 1, n:n + 1]
                        e_xt = vcopy(dv, sv, waits=[])
                        s_user[0] = e_xt
                        s_user[1] = e_xt
            if is_own:
                for i in range(SGN):
                    g = g0 + i
                    e_yf = stt(yac[:, i * T:(i + 1) * T],
                               xcb[:, g * T:(g + 1) * T], dvec[:, g:g + 1],
                               yac[:, i * T:(i + 1) * T], ALU.mult, ALU.add,
                               waits=[P.now("vector")])
                    tt(yg[:, g * T:(g + 1) * T], yac[:, i * T:(i + 1) * T],
                       yg[:, g * T:(g + 1) * T], ALU.mult,
                       waits=[e_yf])
        P.barrier()

        if not is_own:
            return
        for t2 in range(TS):
            last = None
            for g in range(G):
                for hh in range(2):
                    last = mm(ps_op[:, hh * 512:(hh + 1) * 512],
                              yg[:, g * T + t2 * PD:g * T + (t2 + 1) * PD],
                              outwT[:, g * H + hh * 512:g * H + (hh + 1) * 512],
                              start=(g == 0), stop=(g == G - 1),
                              waits=([P.now("scalar"), P.now("vector")]
                                     if g == 0 else []),
                              inc=(1 if (g == G - 1 and hh == 1) else 0))
            row0 = c * T + t2 * PD
            htile = hidt[t2 % 2][:, :]
            e_h = ldma(htile, hid_d[row0:row0 + PD, :],
                       waits=[P.now("scalar"), P.now("vector")],
                       stream="ld_h%d" % (t2 % 2))
            mu, r1 = m1c[:, t2:t2 + 1], r1c[:, t2:t2 + 1]
            e_mb = tt(tcol[:, 0:1], mu, r1, ALU.mult, waits=[P.now("vector")])
            e_mb2 = ts(tcol[:, 0:1], tcol[:, 0:1], -1.0, ALU.mult,
                       waits=[e_mb])
            e_l1 = act(attn[:, :], htile, AF.Identity, scale=r1,
                       bias=tcol[:, 0:1], waits=[e_mb2, e_h, P.now("scalar")])
            e_a1 = tt(attn[:, :], attn[:, :], ps_op[:, :], ALU.add,
                      waits=[e_l1, last])
            e_a2 = tt(attn[:, :], attn[:, :], outb_t[:, :], ALU.add,
                      waits=[e_a1])
            e_n2 = layernorm(attn[:, :], attn[:, :], [e_a2])
            e_x1 = tt(x1t[:, :], htile, attn[:, :], ALU.add,
                      waits=[e_n2, P.now("st")])
            sdma(x1sp_d[row0:row0 + PD, :], x1t[:, :], waits=[e_x1])
            e_n3 = layernorm(x1t[:, :], xnb[:, :], [e_x1])
            for h2 in range(HC):
                psv = ps_tr[:, (h2 % 2) * PD:(h2 % 2 + 1) * PD]
                e_tr = transpose(psv, xnb[:, h2 * PD:(h2 + 1) * PD],
                                 waits=[e_n3, P.now("vector")])
                vcopy(xnTs[:, h2 * PD:(h2 + 1) * PD], psv,
                      waits=[e_tr] + ([P.now("st")] if h2 == 0 else []))
            sdma(xnTsp_d[:, row0:row0 + PD].rearrange(
                "(h p) q -> p h q", p=PD),
                xnTs.rearrange("p (h q) -> p h q", h=HC),
                waits=[P.now("vector")])

    for c in range(NCH):
        P.begin_branch(c, 0)
        emit_chunk(c, True)
        P.end_branch()
        P.begin_branch(c, 1)
        emit_chunk(c, False)
        P.end_branch()
        P.barrier()

    P.barrier()
    sweep_es.close()

    ffn_es = contextlib.ExitStack()
    FW = lambda name, shape, dt: ffn_es.enter_context(nc.sbuf_tensor(name, shape, dt))
    wg_sb = FW("wg_sb", [PD, HC * FF], BF16)
    wu_sb = FW("wu_sb", [PD, HC * FF], BF16)
    wd_sb = FW("wd_sb", [PD, FC * H], BF16)
    fstage = FW("fstage", [PD, FF], I32)
    frhs = FW("frhs", [PD, HC * T2], BF16)
    hbuf = FW("hbuf", [PD, FC * T2], BF16)
    sgr = FW("sgr", [PD, T2], F32)
    f_x1 = FW("f_x1", [PD, H * TS2], F32)
    f_out = FW("f_out", [PD, H], F32)
    f_ev = FW("f_ev", [PD, H], F32)

    DQ = 1.0 / 127.0
    prev = None
    for hc2 in range(HC):
        e_l = ldma(fstage[:, :], wgT_d[hc2 * PD:(hc2 + 1) * PD, :],
                   waits=([prev] if prev else []))
        prev = act(wg_sb[:, hc2 * FF:(hc2 + 1) * FF], fstage[:, :], AF.Identity,
                   scale=DQ, waits=[e_l])
    for hc2 in range(HC):
        e_l = ldma(fstage[:, :], wuT_d[hc2 * PD:(hc2 + 1) * PD, :], waits=[prev])
        prev = act(wu_sb[:, hc2 * FF:(hc2 + 1) * FF], fstage[:, :], AF.Identity,
                   scale=DQ, waits=[e_l])
    for f in range(FC):
        e_l = ldma(fstage[:, 0:H], wdT_d[f * PD:(f + 1) * PD, :], waits=[prev])
        prev = act(wd_sb[:, f * H:(f + 1) * H], fstage[:, 0:H], AF.Identity,
                   scale=DQ, waits=[e_l])
    P.barrier()

    f_zero = FW("f_zero", [PD, H], F32)
    e_fz = P.op("gpsimd", lambda e: e.memset(f_zero[:, :], 0.0), inc=1)
    n_half = L // T2
    for j in range(n_half):
        jc = j * T2 // T
        row0 = j * T2
        if "ffn" in GUARDS:
            P.begin_branch(jc)
        e_r = []
        for hc2 in range(HC):
            e_r.append(ldma(frhs[:, hc2 * T2:(hc2 + 1) * T2],
                            xnTsp_d[hc2 * PD:(hc2 + 1) * PD, row0:row0 + T2],
                            waits=[P.now("tensor")], stream="ld_f"))
        e_r = [e_r[-1]]
        e_x1l = [ldma(f_x1[:, i * H:(i + 1) * H],
                      x1sp_d[row0 + i * PD:row0 + (i + 1) * PD, :],
                      waits=[P.now("vector")], stream="ld_x")
                 for i in range(TS2)]
        e_x1l = [e_x1l[-1]] * TS2
        for fo in range(FC):
            lg = lu = None
            for hc2 in range(HC):
                lg = mm(ps_a[:, 0:T2],
                        wg_sb[:, hc2 * FF + fo * PD:hc2 * FF + (fo + 1) * PD],
                        frhs[:, hc2 * T2:(hc2 + 1) * T2],
                        start=(hc2 == 0), stop=(hc2 == HC - 1),
                        waits=(e_r + [P.now("scalar")] if hc2 == 0 else []),
                        inc=(1 if hc2 == HC - 1 else 0))
            for hc2 in range(HC):
                lu = mm(ps_b[:, 0:T2],
                        wu_sb[:, hc2 * FF + fo * PD:hc2 * FF + (fo + 1) * PD],
                        frhs[:, hc2 * T2:(hc2 + 1) * T2],
                        start=(hc2 == 0), stop=(hc2 == HC - 1),
                        waits=([P.now("vector")] if hc2 == 0 else []),
                        inc=(1 if hc2 == HC - 1 else 0))
            e_sg = act(sgr[:, :], ps_a[:, 0:T2], AF.Silu,
                       waits=[lg, P.now("vector")])
            tt(hbuf[:, fo * T2:(fo + 1) * T2], sgr[:, :], ps_b[:, 0:T2],
               ALU.mult, waits=[e_sg, lu])
        for i in range(TS2):
            last = None
            for fo in range(FC):
                for hh in range(2):
                    last = mm(ps_op[:, hh * 512:(hh + 1) * 512],
                              hbuf[:, fo * T2 + i * PD:fo * T2 + (i + 1) * PD],
                              wd_sb[:, fo * H + hh * 512:fo * H + (hh + 1) * 512],
                              start=(fo == 0), stop=(fo == FC - 1),
                              waits=([P.now("vector"), P.now("scalar")]
                                     if fo == 0 else []),
                              inc=(1 if (fo == FC - 1 and hh == 1) else 0))
            e_ev = act(f_ev[:, :], ps_op[:, :], AF.Copy, waits=[last])
            e_n4 = layernorm(f_ev[:, :], f_ev[:, :], [e_ev])
            e_o = tt(f_out[:, :], f_x1[:, i * H:(i + 1) * H], f_ev[:, :],
                     ALU.add, waits=[e_n4, e_x1l[i], P.now("st")])
            sdma(out_d[row0 + i * PD:row0 + (i + 1) * PD, :], f_out[:, :],
                 waits=[e_o])
        if "ffn" in GUARDS:
            P.else_branch()
            for i in range(TS2):
                sdma(out_d[row0 + i * PD:row0 + (i + 1) * PD, :],
                     f_zero[:, :], waits=[e_fz])
            P.end_branch()

    P.op("gpsimd", None, waits=[P.now("st")], inc=0)
    P.barrier()
    ffn_es.close()

    with es, nc.Block() as block:
        P.emit(block)
    return nc


def _prep_weights(inp):
    g = lambda k: np.ascontiguousarray(inp[k])
    w = {}
    w["inwT_dr"] = np.ascontiguousarray(g("L0_attn_in_w").T)
    w["outwT_dr"] = np.ascontiguousarray(g("L0_attn_out_w").T)
    w["xpwT_dr"] = np.ascontiguousarray(g("L0_attn_xp_w").T)
    w["dtpwT_dr"] = np.ascontiguousarray(g("L0_attn_dtp_w").T)
    w["lnw_dr"] = np.ascontiguousarray(g("L0_attn_ln_w").reshape(HC, PD).T)
    w["lnb_dr"] = np.ascontiguousarray(g("L0_attn_ln_b").reshape(HC, PD).T)
    w["inb_dr"] = np.ascontiguousarray(g("L0_attn_in_b").reshape(2 * G, PD).T)
    w["cvw_dr"] = np.ascontiguousarray(
        g("L0_attn_conv_w").reshape(G, PD, KCV).transpose(1, 0, 2).reshape(PD, G * KCV))
    w["cvb_dr"] = np.ascontiguousarray(g("L0_attn_conv_b").reshape(G, PD).T)
    w["dtb_dr"] = np.ascontiguousarray(g("L0_attn_dtp_b").reshape(G, PD).T)
    w["alog_dr"] = np.ascontiguousarray(
        g("L0_attn_A_log").reshape(G, PD, NST).transpose(1, 0, 2).reshape(PD, G * NST))
    w["dvec_dr"] = np.ascontiguousarray(g("L0_attn_D").reshape(G, PD).T)
    w["outb_dr"] = np.ascontiguousarray(g("L0_attn_out_b").reshape(1, H))
    w["wgT_dr"] = np.ascontiguousarray(g("L0_ffn_w_gate").T)
    w["wuT_dr"] = np.ascontiguousarray(g("L0_ffn_w_up").T)
    w["wdT_dr"] = np.ascontiguousarray(g("L0_ffn_w_down").T)
    return w


def kernel(**inputs):
    L = inputs["L0_attn_hidden"].shape[1]
    NCH = L // 512
    w = _prep_weights(inputs)
    nc = build(L)
    bounds = [0, NCH // 4, NCH // 2, 3 * NCH // 4, NCH]
    in_maps = []
    for core in range(8):
        b, q = core // 4, core % 4
        own = 0
        for c in range(bounds[q], bounds[q + 1]):
            own |= 1 << c
        scanonly = 0
        for c in range(0, bounds[q]):
            scanonly |= 1 << c
        m = dict(w)
        m["hidden"] = np.ascontiguousarray(
            np.asarray(inputs["L0_attn_hidden"][b], dtype=np.float32))
        m["cfg_dr"] = np.array([[own, scanonly, 0, 0]],
                               dtype=np.uint32).view(np.int32)
        in_maps.append(m)
    res = run_bass_kernel_spmd(nc, in_maps, core_ids=list(range(8)))
    out = np.empty((2, L, H), np.float32)
    for core in range(8):
        b, q = core // 4, core % 4
        r0, r1 = bounds[q] * 512, bounds[q + 1] * 512
        out[b, r0:r1] = res.results[core]["out"][r0:r1]
    return out


# version 08
# speedup vs baseline: 1.5254x; 1.0492x over previous
import contextlib
import numpy as np

import concourse.bass as bass
import concourse.mybir as mybir
from concourse.bass_utils import run_bass_kernel_spmd

F32 = mybir.dt.float32
BF16 = mybir.dt.bfloat16
I32 = mybir.dt.int32
AF = mybir.ActivationFunctionType
ALU = mybir.AluOpType
AX = mybir.AxisListType

H = 1024
INNER = 2048
NST = 16
KCV = 4
RDT = 64
FF = 3072
PD = 128
G = INNER // PD
HC = H // PD
FC = FF // PD
EPS = 1e-5

STREAMS = ("sync", "tensor", "vector", "scalar", "gpsimd", "st",
           "ld_h0", "ld_h1", "ld_s", "ld_f", "ld_x")
ENG_OF = {"sync": "sync", "tensor": "tensor", "vector": "vector",
          "scalar": "scalar", "gpsimd": "gpsimd", "st": "sync",
          "ld_h0": "sync", "ld_h1": "sync", "ld_s": "sync",
          "ld_f": "sync", "ld_x": "sync"}


ENGS5 = ("sync", "tensor", "vector", "scalar", "gpsimd")

GUARDS = {"gate", "D", "E", "ffn"}


class Plan:

    def __init__(self, nc, sems, cfg_ap=None, cfg_ready=None):
        self.nc = nc
        self.sems = sems
        self.cnt = {s: 0 for s in STREAMS}
        self.ops = {e: [] for e in ENGS5}
        self.cfg_ap = cfg_ap
        self.cfg_ready = cfg_ready
        self._br = None

    def op(self, stream, fn, waits=(), inc=None):
        if inc is None:
            inc = 16 if (stream.startswith("ld_") or stream in ("sync", "st")) else 1
        eng = ENG_OF[stream]
        self.ops[eng].append(("op", fn, tuple(waits), inc, stream))
        self.cnt[stream] += inc
        return (stream, self.cnt[stream])

    def now(self, stream):
        return (stream, self.cnt[stream])

    def barrier(self):
        marks = [(s, self.cnt[s]) for s in STREAMS if self.cnt[s] > 0]
        for e in self.ops:
            w = [m for m in marks if ENG_OF[m[0]] != e]
            if w:
                self.ops[e].append(("op", None, tuple(w), 0, e))

    def begin_branch(self, bit, mask_idx=0):
        assert self._br is None
        self._br = [bit, dict(self.cnt), None]
        for e in ENGS5:
            self.ops[e].append(("if", bit, mask_idx))

    def else_branch(self):
        bit, snap, _ = self._br
        self._br[2] = dict(self.cnt)
        self.cnt = dict(snap)
        for e in ENGS5:
            self.ops[e].append(("else",))

    def end_branch(self):
        bit, snap, armA = self._br
        self._br = None
        if armA is None:
            armA = dict(self.cnt)
            self.cnt = dict(snap)
            for e in ENGS5:
                self.ops[e].append(("else",))
        armB = dict(self.cnt)
        final = {s: max(armA[s], armB[s]) for s in STREAMS}
        padA = {s: final[s] - armA[s] for s in STREAMS if final[s] > armA[s]}
        padB = {s: final[s] - armB[s] for s in STREAMS if final[s] > armB[s]}
        self.cnt = final
        for e in ENGS5:
            for k in range(len(self.ops[e]) - 1, -1, -1):
                if self.ops[e][k] == ("else",):
                    self.ops[e][k] = ("else", padA)
                    break
            self.ops[e].append(("endif", padB))

    def _prune_empty_regions(self):
        for e, ops in self.ops.items():
            out = []
            i = 0
            while i < len(ops):
                entry = ops[i]
                if entry[0] != "if":
                    out.append(entry)
                    i += 1
                    continue
                depth = 0
                j = i
                k_else = None
                while True:
                    kind = ops[j][0]
                    if kind == "if":
                        depth += 1
                    elif kind == "endif":
                        depth -= 1
                        if depth == 0:
                            break
                    elif kind == "else" and depth == 1:
                        k_else = j
                    j += 1
                padA = ops[k_else][1]
                padB = ops[j][1]
                body_ops = [ops[m] for m in range(i + 1, j)
                            if ops[m][0] == "op"]
                my_padA = any(ENG_OF[s] == e for s in padA)
                my_padB = any(ENG_OF[s] == e for s in padB)
                inner_marks = any(ops[m][0] in ("if", "else", "endif")
                                  for m in range(i + 1, j)
                                  if m != k_else)
                if not body_ops and not my_padA and not my_padB \
                        and not inner_marks:
                    i = j + 1
                else:
                    out.extend(ops[i:j + 1])
                    i = j + 1
            self.ops[e] = out

    def emit(self, block):
        self._prune_empty_regions()

        def make(eng_name):
            def body(eng):
                has_if = any(entry[0] == "if" for entry in self.ops[eng_name])

                def inner(rmask, rb):
                    satisfied = {}
                    sat_stack = []
                    guard_stack = []
                    loaded = [False]

                    def ensure_mask():
                        if loaded[0]:
                            return
                        loaded[0] = True
                        ws, wv = self.cfg_ready
                        if satisfied.get(ws, 0) < wv:
                            eng.wait_ge(self.sems[ws], wv)
                            satisfied[ws] = wv
                        eng.reg_load(rmask[0], self.cfg_ap[0])
                        eng.reg_load(rmask[1], self.cfg_ap[1])

                    def do_waits(waits):
                        for (ws, wv) in waits:
                            if satisfied.get(ws, 0) >= wv:
                                continue
                            eng.wait_ge(self.sems[ws], wv)
                            satisfied[ws] = wv

                    def do_pads(pad):
                        for s, k in pad.items():
                            if ENG_OF[s] == eng_name:
                                step = 16 if (s.startswith("ld_") or
                                              s in ("st",)) else 1
                                left = k
                                while left > 0:
                                    d = min(step, left)
                                    eng.sem_inc(self.sems[s], d)
                                    left -= d

                    for entry in self.ops[eng_name]:
                        kind = entry[0]
                        if kind == "op":
                            _, fn, waits, inc, stream = entry
                            do_waits(waits)
                            if fn is not None:
                                instr = fn(eng)
                                if inc:
                                    instr.then_inc(self.sems[stream], inc)
                        elif kind == "if":
                            ensure_mask()
                            bit, mi = entry[1], entry[2]
                            eng.reg_alu(rb, rmask[mi], bit,
                                        ALU.logical_shift_right)
                            eng.reg_alu(rb, rb, 1, ALU.bitwise_and)
                            gd = eng.If_ne(rb, 0)
                            gd.__enter__()
                            guard_stack.append(gd)
                            sat_stack.append(dict(satisfied))
                        elif kind == "else":
                            padA = entry[1]
                            do_pads(padA)
                            snap = sat_stack[-1]
                            sat_stack[-1] = dict(satisfied)
                            satisfied.clear()
                            satisfied.update(snap)
                            guard_stack[-1].__exit__(None, None, None)
                            gd = eng.Else()
                            gd.__enter__()
                            guard_stack[-1] = gd
                        elif kind == "endif":
                            padB = entry[1]
                            do_pads(padB)
                            guard_stack.pop().__exit__(None, None, None)
                            armA_sat = sat_stack.pop()
                            merged = {}
                            for k in set(armA_sat) | set(satisfied):
                                merged[k] = min(armA_sat.get(k, 0),
                                                satisfied.get(k, 0))
                            satisfied.clear()
                            satisfied.update(merged)

                if has_if:
                    with eng.register("r_mask0_" + eng_name) as rm0, \
                         eng.register("r_mask1_" + eng_name) as rm1, \
                         eng.register("r_bit_" + eng_name) as rb:
                        inner((rm0, rm1), rb)
                else:
                    inner(None, None)

            return body

        for e in ENGS5:
            if self.ops[e]:
                getattr(block, e)(make(e))


def build(L, T=512):
    NCH = L // T
    TS = T // PD
    SGN = 2
    NSG = G // SGN
    T2 = 512
    TS2 = T2 // PD

    nc = bass.Bass("TRN2", target_bir_lowering=False, debug=False)

    hid_d = nc.dram_tensor("hidden", [L, H], F32, kind="ExternalInput").ap()
    inwT_d = nc.dram_tensor("inwT_dr", [H, 2 * INNER], F32, kind="ExternalInput").ap()
    outwT_d = nc.dram_tensor("outwT_dr", [INNER, H], F32, kind="ExternalInput").ap()
    xpwT_d = nc.dram_tensor("xpwT_dr", [INNER, 96], F32, kind="ExternalInput").ap()
    dtpwT_d = nc.dram_tensor("dtpwT_dr", [RDT, INNER], F32, kind="ExternalInput").ap()
    lnw_d = nc.dram_tensor("lnw_dr", [PD, HC], F32, kind="ExternalInput").ap()
    lnb_d = nc.dram_tensor("lnb_dr", [PD, HC], F32, kind="ExternalInput").ap()
    inb_d = nc.dram_tensor("inb_dr", [PD, 2 * G], F32, kind="ExternalInput").ap()
    cvw_d = nc.dram_tensor("cvw_dr", [PD, G * KCV], F32, kind="ExternalInput").ap()
    cvb_d = nc.dram_tensor("cvb_dr", [PD, G], F32, kind="ExternalInput").ap()
    dtb_d = nc.dram_tensor("dtb_dr", [PD, G], F32, kind="ExternalInput").ap()
    alog_d = nc.dram_tensor("alog_dr", [PD, G * NST], F32, kind="ExternalInput").ap()
    dvec_d = nc.dram_tensor("dvec_dr", [PD, G], F32, kind="ExternalInput").ap()
    outb_d = nc.dram_tensor("outb_dr", [1, H], F32, kind="ExternalInput").ap()
    wgT_d = nc.dram_tensor("wgT_dr", [H, FF], I32, kind="ExternalInput").ap()
    wuT_d = nc.dram_tensor("wuT_dr", [H, FF], I32, kind="ExternalInput").ap()
    wdT_d = nc.dram_tensor("wdT_dr", [FF, H], I32, kind="ExternalInput").ap()

    cfg_d = nc.dram_tensor("cfg_dr", [1, 4], I32, kind="ExternalInput").ap()
    out_d = nc.dram_tensor("out", [L, H], F32, kind="ExternalOutput").ap()
    x1sp_d = nc.dram_tensor("x1sp", [L, H], F32).ap()
    xnTsp_d = nc.dram_tensor("xnTsp", [H, L], BF16).ap()

    es = contextlib.ExitStack()
    SB = lambda name, shape, dt: es.enter_context(nc.sbuf_tensor(name, shape, dt))
    PS = lambda name, shape, dt: es.enter_context(nc.psum_tensor(name, shape, dt))

    ps_a = PS("ps_a", [PD, 512], F32)
    ps_b = PS("ps_b", [PD, 512], F32)
    ps_xp = PS("ps_xp", [PD, 512], F32)
    ps_bb = PS("ps_bb", [PD, 512], F32)
    ps_bb2 = PS("ps_bb2", [PD, 512], F32)
    ps_op = PS("ps_op", [PD, 1024], F32)
    ps_f = PS("ps_f", [PD, 512], F32)
    ps_trv = None

    sems = {}
    for s in STREAMS:
        sems[s] = es.enter_context(nc.semaphore("sem_" + s))
    P = Plan(nc, sems)
    cfg_s = SB("cfg_s", [1, 4], I32)

    tcol = SB("tcol", [PD, 32], F32)
    consts = SB("consts", [PD, 8], F32)

    sweep_es = contextlib.ExitStack()
    SW = lambda name, shape, dt: sweep_es.enter_context(nc.sbuf_tensor(name, shape, dt))
    ident_b = SW("ident_b", [PD, PD], BF16)
    acols = SW("acols", [PD, G * NST], F32)
    cvw = SW("cvw", [PD, G * KCV], F32)
    cvb = SW("cvb", [PD, G], F32)
    dtb = SW("dtb", [PD, G], F32)
    dvec = SW("dvec", [PD, G], F32)
    inb = SW("inb", [PD, 2 * G], F32)
    lnw = SW("lnw", [PD, HC], F32)
    lnb = SW("lnb", [PD, HC], F32)
    outb_t = SW("outb_t", [PD, H], F32)
    states = SW("states", [PD, G * NST], F32)
    onesb = SW("onesb", [1, PD], BF16)
    sel16 = SW("sel16", [16, NST * PD], BF16)
    brow_b = SW("brow_b", [16, T], BF16)
    crow_b = SW("crow_b", [16, T], BF16)
    dtr_b = SW("dtr_b", [RDT, T], BF16)
    inwT = SW("inwT", [PD, HC * 2 * INNER], BF16)
    outwT = SW("outwT", [PD, G * H], BF16)
    xpwT = SW("xpwT", [PD, G * 96], BF16)
    dtpwT = SW("dtpwT", [RDT, G * PD], BF16)
    hidt = [SW(f"hidt{i}", [PD, H], F32) for i in range(2)]
    hnT = SW("hnT", [PD, HC * T], BF16)
    xcb = SW("xcb", [PD, G * T], BF16)
    yg = SW("yg", [PD, G * T], BF16)
    halo = SW("halo", [PD, G * 3], F32)
    dt_sg = SW("dt_sg", [PD, SGN * T], F32)
    dtx_sg = SW("dtx_sg", [PD, SGN * T], F32)
    r1c = SW("r1c", [PD, TS], F32)
    m1c = SW("m1c", [PD, TS], F32)
    xnb = SW("xnb", [PD, H], BF16)
    pro_es = contextlib.ExitStack()
    wstage = pro_es.enter_context(nc.sbuf_tensor("wstage", [PD, 2048], F32))
    obrow = pro_es.enter_context(nc.sbuf_tensor("obrow", [1, H], F32))
    obrow_b = pro_es.enter_context(nc.sbuf_tensor("obrow_b", [1, H], BF16))
    pro_es.close()
    d_es = contextlib.ExitStack()
    DW = lambda name, shape, dt: d_es.enter_context(nc.sbuf_tensor(name, shape, dt))
    xg = [DW(f"xg{i}", [PD, T + 3], F32) for i in range(2)]
    zb = [DW(f"zb{i}", [PD, T], F32) for i in range(2)]
    dec = [DW(f"dec{i}", [PD, T], F32) for i in range(2)]
    wti = [DW(f"wti{i}", [PD, T], F32) for i in range(2)]
    srot = DW("srot", [PD, 2 * T], BF16)
    prot = [DW(f"prot{i}", [PD, T], BF16) for i in range(2)]
    cbcr = [DW(f"cbcr{i}", [PD, T], BF16) for i in range(2)]
    bbr = [DW(f"bbr{i}", [PD, T], F32) for i in range(2)]
    yac = DW("yac", [PD, SGN * T], F32)
    d_es.close()
    attn = SW("attn", [PD, H], F32)
    x1t = SW("x1t", [PD, H], F32)
    xnTs = SW("xnTs", [PD, HC * PD], BF16)

    def ldma(dst, src, waits=(), stream="sync"):
        return P.op(stream, lambda e, d=dst, s=src: e.dma_start(out=d, in_=s),
                    waits=waits)

    def sdma(dst, src, waits=()):
        return P.op("st", lambda e, d=dst, s=src: e.dma_start(out=d, in_=s),
                    waits=waits)

    def act(out, in_, func, bias=None, scale=None, accum_out=None, waits=()):
        kw = {}
        if bias is not None:
            kw["bias"] = bias
        if scale is not None:
            kw["scale"] = scale
        if accum_out is not None:
            kw["accum_out"] = accum_out
        return P.op("scalar",
                    lambda e, o=out, i=in_, f=func, kw=kw: e.activation(o, i, f, **kw),
                    waits=waits)

    def tt(out, a, b, op=ALU.mult, waits=()):
        return P.op("vector",
                    lambda e, o=out, x=a, y=b, p=op: e.tensor_tensor(o, x, y, op=p),
                    waits=waits)

    def gtt(out, a, b, op=ALU.mult, waits=()):
        return P.op("gpsimd",
                    lambda e, o=out, x=a, y=b, p=op: e.tensor_tensor(o, x, y, op=p),
                    waits=waits)

    def ts(out, a, s1, op0, waits=()):
        return P.op("vector",
                    lambda e, o=out, x=a, s=s1, p=op0:
                    e.tensor_scalar(o, x, s, None, p),
                    waits=waits)

    def stt(out, in0, scalar, in1, op0, op1, waits=()):
        return P.op("vector",
                    lambda e, o=out, a=in0, s=scalar, b=in1, p0=op0, p1=op1:
                    e.scalar_tensor_tensor(o, a, s, b, op0=p0, op1=p1),
                    waits=waits)

    def vred(out, in_, op=ALU.add, axis=AX.X, waits=()):
        return P.op("vector",
                    lambda e, o=out, i=in_, p=op, ax=axis:
                    e.tensor_reduce(o, i, axis=ax, op=p),
                    waits=waits)

    def vcopy(out, in_, waits=()):
        return P.op("vector",
                    lambda e, o=out, i=in_: e.tensor_copy(o, i), waits=waits)

    def scan(out, d0, d1, init, waits=()):
        return P.op("vector",
                    lambda e, o=out, a=d0, b=d1, i=init:
                    e.tensor_tensor_scan(o, a, b, i, op0=ALU.mult, op1=ALU.add),
                    waits=waits)

    def mm(psum, lhsT, rhs, start, stop, waits=(), inc=1):
        return P.op("tensor",
                    lambda e, o=psum, l=lhsT, r=rhs, st=start, sp=stop:
                    e.matmul(o, l, r, start=st, stop=sp),
                    waits=waits, inc=inc)

    def transpose(psum, in_, waits=(), inc=1):
        return P.op("tensor",
                    lambda e, o=psum, i=in_: e.transpose(o, i, ident_b[:, :]),
                    waits=waits, inc=inc)

    def layernorm(src, dst, waits):
        e_s = vred(tcol[:, 1:2], src, waits=list(waits) + [P.now("vector")])
        e_q = act(ps_op[:, 0:H], src, AF.Square, accum_out=tcol[:, 2:3],
                  waits=list(waits) + [P.now("scalar"), P.now("tensor"),
                                       P.now("vector")])
        e_mu = ts(tcol[:, 3:4], tcol[:, 1:2], 1.0 / H, ALU.mult, waits=[e_s])
        e_e2 = ts(tcol[:, 4:5], tcol[:, 2:3], 1.0 / H, ALU.mult, waits=[e_q])
        e_m2 = tt(tcol[:, 5:6], tcol[:, 3:4], tcol[:, 3:4], ALU.mult, waits=[e_mu])
        e_v = tt(tcol[:, 6:7], tcol[:, 4:5], tcol[:, 5:6], ALU.subtract,
                 waits=[e_e2, e_m2])
        e_ln = act(tcol[:, 7:8], tcol[:, 6:7], AF.Ln, bias=EPS, waits=[e_v])
        e_r = act(tcol[:, 8:9], tcol[:, 7:8], AF.Exp, scale=-0.5, waits=[e_ln])
        e_mn = ts(tcol[:, 9:10], tcol[:, 3:4], -1.0, ALU.mult, waits=[e_mu])
        e_bc = tt(tcol[:, 10:11], tcol[:, 9:10], tcol[:, 8:9], ALU.mult,
                  waits=[e_r, e_mn])
        return act(dst, src, AF.Identity, scale=tcol[:, 8:9], bias=tcol[:, 10:11],
                   waits=[e_bc])

    CONST_VALS = [EPS, 1.0, -1.0, -0.5, 1.0 / 127.0]
    for ci, cv in enumerate(CONST_VALS):
        P.op("gpsimd", lambda e, i=ci, v=cv: e.memset(consts[:, i:i + 1], v),
             inc=1)
        nc.const_aps.register(F32, cv, consts[:, ci:ci + 1])
    e_idf = P.op("gpsimd", lambda e: e.memset(ident_b[:, :], 0.0), inc=1)
    P.op("gpsimd",
         lambda e: e.affine_select(out=ident_b[:, :], in_=ident_b[:, :],
                                   compare_op=ALU.not_equal, fill=1.0,
                                   base=0, pattern=[[-1, PD]],
                                   channel_multiplier=1),
         waits=[e_idf], inc=1)
    P.op("gpsimd", lambda e: e.memset(onesb[:, :], 1.0), inc=1)
    e_s16a = P.op("gpsimd", lambda e: e.memset(sel16[:, :], 0.0), inc=1)
    P.op("gpsimd",
         lambda e: e.affine_select(out=sel16.rearrange("c (n p) -> c n p", n=NST),
                                   in_=sel16.rearrange("c (n p) -> c n p", n=NST),
                                   compare_op=ALU.not_equal, fill=1.0, base=0,
                                   pattern=[[-1, NST], [0, PD]],
                                   channel_multiplier=1),
         waits=[e_s16a], inc=1)
    P.op("gpsimd", lambda e: e.memset(states[:, :], 0.0), inc=1)
    P.op("gpsimd", lambda e: e.memset(yg[:, :], 0.0), inc=1)
    P.op("gpsimd", lambda e: e.memset(halo[:, :], 0.0), inc=1)
    e_pro = P.now("gpsimd")

    small = [(acols, alog_d), (cvw, cvw_d), (cvb, cvb_d), (dtb, dtb_d),
             (dvec, dvec_d), (inb, inb_d), (lnw, lnw_d), (lnb, lnb_d)]
    e_small = [ldma(t[:, :], d, stream="ld_s") for (t, d) in small]
    e_ob = ldma(obrow[:, :], outb_d, stream="ld_s")
    ldma(cfg_s[:, :], cfg_d, stream="ld_s")
    e_lds = P.now("ld_s")
    P.cfg_ap = (cfg_s[0:1, 0:1], cfg_s[0:1, 1:2])
    P.cfg_ready = e_lds
    e_a1 = act(acols[:, :], acols[:, :], AF.Exp, waits=[e_lds])
    ts(acols[:, :], acols[:, :], -1.0, ALU.mult, waits=[e_a1])

    e_obb = act(obrow_b[:, :], obrow[:, :], AF.Copy, waits=[e_lds])
    e_obm = [mm(ps_op[:, hh * 512:(hh + 1) * 512], onesb[:, :],
                obrow_b[:, hh * 512:(hh + 1) * 512], True, True,
                waits=[e_obb, e_pro]) for hh in range(2)]
    vcopy(outb_t[:, :], ps_op[:, :], waits=e_obm)

    def load_cast(dst_sb, dram, rows, cols, col0_dst, prev, scale=None):
        stage_cols = 2048
        for c0 in range(0, cols, stage_cols):
            cw = min(stage_cols, cols - c0)
            w = [prev] if prev else []
            e_l = ldma(wstage[0:rows, 0:cw], dram[:, c0:c0 + cw], waits=w)
            prev = act(dst_sb[0:rows, col0_dst + c0: col0_dst + c0 + cw],
                       wstage[0:rows, 0:cw], AF.Identity, scale=scale, waits=[e_l])
        return prev

    prev = None
    for hc in range(HC):
        prev = load_cast(inwT, inwT_d[hc * PD:(hc + 1) * PD, :], PD,
                         2 * INNER, hc * 2 * INNER, prev)
    for g in range(G):
        prev = load_cast(outwT, outwT_d[g * PD:(g + 1) * PD, :], PD, H,
                         g * H, prev)
        prev = load_cast(xpwT, xpwT_d[g * PD:(g + 1) * PD, :], PD, 96,
                         g * 96, prev)
    prev = load_cast(dtpwT, dtpwT_d, RDT, G * PD, 0, prev)
    P.barrier()

    ps_tr = ps_bb2.bitcast(BF16)

    def emit_chunk(c, is_own):
        for ti in range(TS):
            row0 = c * T + ti * PD
            htile = hidt[ti % 2][:, :]
            e_h = ldma(htile, hid_d[row0:row0 + PD, :],
                       waits=[P.now("scalar"), P.now("vector"),
                              P.now("tensor")], stream="ld_h%d" % (ti % 2))
            sumc, sqc = tcol[:, 1:2], tcol[:, 2:3]
            e_s = vred(sumc, htile, waits=[e_h, P.now("vector")])
            e_q = act(ps_op[:, 0:H], htile, AF.Square, accum_out=sqc,
                      waits=[e_h, P.now("scalar"), P.now("tensor")])
            mu = m1c[:, ti:ti + 1]
            e_mu = ts(mu, sumc, 1.0 / H, ALU.mult, waits=[e_s])
            e_e2 = ts(tcol[:, 4:5], sqc, 1.0 / H, ALU.mult, waits=[e_q])
            e_m2 = tt(tcol[:, 5:6], mu, mu, ALU.mult, waits=[e_mu])
            e_v = tt(tcol[:, 6:7], tcol[:, 4:5], tcol[:, 5:6], ALU.subtract,
                     waits=[e_e2, e_m2])
            e_ln = act(tcol[:, 7:8], tcol[:, 6:7], AF.Ln, bias=EPS, waits=[e_v])
            r1 = r1c[:, ti:ti + 1]
            e_r1 = act(r1, tcol[:, 7:8], AF.Exp, scale=-0.5, waits=[e_ln])
            e_r1sq = tt(tcol[:, 5:6], r1, r1, ALU.mult, waits=[e_r1])
            e_v2 = tt(tcol[:, 13:14], tcol[:, 6:7], tcol[:, 5:6], ALU.mult,
                      waits=[e_r1sq, e_v])
            e_l2 = act(tcol[:, 14:15], tcol[:, 13:14], AF.Ln, bias=EPS,
                       waits=[e_v2, e_r1])
            e_r2 = act(tcol[:, 9:10], tcol[:, 14:15], AF.Exp, scale=-0.5,
                       waits=[e_l2])
            e_sc = tt(tcol[:, 10:11], r1, tcol[:, 9:10], ALU.mult, waits=[e_r2])
            e_mn = ts(tcol[:, 11:12], mu, -1.0, ALU.mult, waits=[e_mu])
            e_bc = tt(tcol[:, 12:13], tcol[:, 11:12], tcol[:, 10:11], ALU.mult,
                      waits=[e_sc, e_mn])
            e_u = act(xnb[:, :], htile, AF.Identity, scale=tcol[:, 10:11],
                      bias=tcol[:, 12:13], waits=[e_bc, P.now("tensor")])
            for h2 in range(HC):
                psv = ps_tr[:, (h2 % 2) * PD:(h2 % 2 + 1) * PD]
                e_tr = transpose(psv, xnb[:, h2 * PD:(h2 + 1) * PD],
                                 waits=[e_u, P.now("scalar")])
                act(hnT[:, h2 * T + ti * PD: h2 * T + (ti + 1) * PD],
                    psv, AF.Identity, scale=lnw[:, h2:h2 + 1],
                    bias=lnb[:, h2:h2 + 1], waits=[e_tr])
        P.barrier()

        for g in range(G):
            xgc = xg[g % 2]
            pp = ps_a if g % 2 == 0 else ps_b
            last = None
            for h2 in range(HC):
                last = mm(pp[:, :], inwT[:, h2 * 2 * INNER + g * PD:
                                         h2 * 2 * INNER + (g + 1) * PD],
                          hnT[:, h2 * T:(h2 + 1) * T],
                          start=(h2 == 0), stop=(h2 == HC - 1),
                          waits=([P.now("scalar")] if h2 == 0 else []),
                          inc=(1 if h2 == HC - 1 else 0))
            e_x = act(xgc[:, 3:3 + T], pp[:, :], AF.Identity,
                      bias=inb[:, g:g + 1], waits=[last, P.now("vector")])
            e_hl = vcopy(xgc[:, 0:3], halo[:, g * 3:(g + 1) * 3],
                         waits=[e_x, P.now("vector")])
            zbc = zb[g % 2][:, 0:T]
            e_ck = ts(zbc, xgc[:, 0:T], cvw[:, g * KCV:g * KCV + 1],
                      ALU.mult, waits=[e_x, e_hl, P.now("scalar")])
            for k in range(1, KCV):
                e_ck = stt(zbc, xgc[:, k:k + T],
                           cvw[:, g * KCV + k:g * KCV + k + 1], zbc,
                           ALU.mult, ALU.add, waits=[e_ck])
            vcopy(halo[:, g * 3:(g + 1) * 3], xgc[:, T:T + 3], waits=[e_x])
            act(xcb[:, g * T:(g + 1) * T], zbc, AF.Silu,
                bias=cvb[:, g:g + 1], waits=[e_ck])
        if is_own:
            for g in range(G):
                pp = ps_a if g % 2 == 0 else ps_b
                last = None
                for h2 in range(HC):
                    last = mm(pp[:, :],
                              inwT[:, h2 * 2 * INNER + INNER + g * PD:
                                   h2 * 2 * INNER + INNER + (g + 1) * PD],
                              hnT[:, h2 * T:(h2 + 1) * T],
                              start=(h2 == 0), stop=(h2 == HC - 1),
                              waits=([P.now("scalar")] if h2 == 0 else []),
                              inc=(1 if h2 == HC - 1 else 0))
                act(yg[:, g * T:(g + 1) * T], pp[:, :], AF.Silu,
                    bias=inb[:, G + g:G + g + 1], waits=[last])
        P.barrier()

        w0 = [P.now("scalar")]
        for g in range(G):
            e_g1 = mm(ps_xp[0:16, :], xpwT[:, g * 96 + 64:g * 96 + 80],
                      xcb[:, g * T:(g + 1) * T], start=(g == 0),
                      stop=(g == G - 1), waits=(w0 if g == 0 else []),
                      inc=(1 if g == G - 1 else 0))
        e_bb = act(brow_b[:, :], ps_xp[0:16, :], AF.Copy, waits=[e_g1])
        e_cb = None
        if is_own:
            for g in range(G):
                e_g2 = mm(ps_xp[32:48, :], xpwT[:, g * 96 + 80:g * 96 + 96],
                          xcb[:, g * T:(g + 1) * T], start=(g == 0),
                          stop=(g == G - 1),
                          waits=([e_bb] if g == 0 else []),
                          inc=(1 if g == G - 1 else 0))
            e_cb = act(crow_b[:, :], ps_xp[32:48, :], AF.Copy, waits=[e_g2])
        for g in range(G):
            e_g3 = mm(ps_xp[64:128, :], xpwT[:, g * 96:g * 96 + 64],
                      xcb[:, g * T:(g + 1) * T], start=(g == 0),
                      stop=(g == G - 1),
                      waits=([e_cb if e_cb else e_bb] if g == 0 else []),
                      inc=(1 if g == G - 1 else 0))
        e_db = act(dtr_b[:, :], ps_xp[64:128, :], AF.Copy, waits=[e_g3])
        P.barrier()

        for sg in range(NSG):
            g0 = sg * SGN
            for i in range(SGN):
                g = g0 + i
                pp = ps_a if i % 2 == 0 else ps_b
                e_m = mm(pp[:, :], dtpwT[:, g * PD:(g + 1) * PD],
                         dtr_b[:, :], True, True,
                         waits=[e_db, P.now("scalar")])
                e_z = act(zb[0][:, 0:T], pp[:, :], AF.Identity,
                          bias=dtb[:, g:g + 1],
                          waits=[e_m, P.now("vector")])
                o0 = i * T
                e_ab = act(dtx_sg[:, o0:o0 + T], zb[0][:, 0:T], AF.Abs,
                           waits=[e_z, P.now("vector")])
                e_ex = act(dtx_sg[:, o0:o0 + T], dtx_sg[:, o0:o0 + T],
                           AF.Exp, scale=-1.0, waits=[e_ab])
                e_ln1 = act(dtx_sg[:, o0:o0 + T], dtx_sg[:, o0:o0 + T],
                            AF.Ln, bias=1.0, waits=[e_ex])
                e_rl = act(dt_sg[:, o0:o0 + T], zb[0][:, 0:T], AF.Relu,
                           waits=[e_ln1])
                tt(dt_sg[:, o0:o0 + T], dt_sg[:, o0:o0 + T],
                   dtx_sg[:, o0:o0 + T], ALU.add, waits=[e_rl])
            Wf = SGN * T
            e_dt = P.now("vector")
            e_dtx = tt(dtx_sg[:, 0:Wf], dt_sg[:, 0:Wf],
                       xcb[:, g0 * T:(g0 + SGN) * T], ALU.mult,
                       waits=[e_dt, P.now("scalar")])
            e_y0 = None
            if is_own:
                e_y0 = P.op("gpsimd",
                            lambda e, b=yac: e.memset(b[:, :], 0.0),
                            waits=[P.now("vector")], inc=1)

            dec_user = [None, None]
            bb_user = [None, None]
            bbe_user = [None, None]
            cc_user = None
            cb_user = [None, None]
            s_user = [None, None]
            for n in range(NST):
                kb = n % 2
                pbb = ps_bb if kb == 0 else ps_bb2
                e_bm = mm(pbb[:, :], sel16[:, n * PD:(n + 1) * PD],
                          brow_b[:, :], True, True,
                          waits=[e_bb] + ([bbe_user[kb]] if bbe_user[kb] else
                                          [P.now("vector"),
                                           P.now("scalar")]))
                e_be = act(bbr[kb][:, :], pbb[:, :], AF.Copy,
                           waits=[e_bm] + ([bb_user[kb]] if bb_user[kb]
                                           else []))
                bbe_user[kb] = e_be
                e_cv = None
                if is_own:
                    e_cm = mm(ps_xp[:, :], sel16[:, n * PD:(n + 1) * PD],
                              crow_b[:, :], True, True,
                              waits=[e_cb] + ([cc_user] if cc_user else
                                              [P.now("vector")]))
                    kc = n % 2
                    e_cv = act(cbcr[kc][:, :], ps_xp[:, :], AF.Copy,
                               waits=[e_cm] + ([cb_user[kc]] if cb_user[kc]
                                               else []))
                    cc_user = e_cv
                for i in range(SGN):
                    g = g0 + i
                    sl = i % 2
                    kd = (n * SGN + i) % 2
                    e_de = act(dec[kd][:, :], dt_sg[:, i * T:(i + 1) * T],
                               AF.Exp,
                               scale=acols[:, g * NST + n:g * NST + n + 1],
                               waits=([dec_user[kd]] if dec_user[kd]
                                      else [P.now("vector")]))
                    e_w = gtt(wti[kd][:, :], dtx_sg[:, i * T:(i + 1) * T],
                              bbr[kb][:, :], ALU.mult,
                              waits=[e_dtx, e_be] +
                              ([dec_user[kd]] if dec_user[kd] else []))
                    bb_user[kb] = e_w
                    e_sc = scan(srot[:, sl * T:(sl + 1) * T], dec[kd][:, :],
                                wti[kd][:, :],
                                states[:, g * NST + n:g * NST + n + 1],
                                waits=[e_de, e_w] +
                                ([s_user[sl]] if s_user[sl] else []))
                    dec_user[kd] = e_sc
                    s_user[sl] = e_sc
                    if is_own:
                        kc = n % 2
                        e_p = tt(prot[kd][:, :], srot[:, sl * T:(sl + 1) * T],
                                 cbcr[kc][:, :], ALU.mult,
                                 waits=[e_sc, e_cv])
                        cb_user[kc] = e_p
                        tt(yac[:, i * T:(i + 1) * T],
                           yac[:, i * T:(i + 1) * T],
                           prot[kd][:, :], ALU.add,
                           waits=[e_p] + ([e_y0] if n == 0 else []))
                        s_user[sl] = e_p
                    if sl == 1:
                        sv = srot.rearrange("p (i t) -> p i t", i=2)[
                            :, :, T - 1:T]
                        dv = states.rearrange("p (g n) -> p g n", g=G)[
                            :, g0 + i - 1:g0 + i + 1, n:n + 1]
                        e_xt = vcopy(dv, sv, waits=[])
                        s_user[0] = e_xt
                        s_user[1] = e_xt
            if is_own:
                for i in range(SGN):
                    g = g0 + i
                    e_yf = stt(yac[:, i * T:(i + 1) * T],
                               xcb[:, g * T:(g + 1) * T], dvec[:, g:g + 1],
                               yac[:, i * T:(i + 1) * T], ALU.mult, ALU.add,
                               waits=[P.now("vector")])
                    tt(yg[:, g * T:(g + 1) * T], yac[:, i * T:(i + 1) * T],
                       yg[:, g * T:(g + 1) * T], ALU.mult,
                       waits=[e_yf])
        P.barrier()

        if not is_own:
            return
        for t2 in range(TS):
            last = None
            for g in range(G):
                for hh in range(2):
                    last = mm(ps_op[:, hh * 512:(hh + 1) * 512],
                              yg[:, g * T + t2 * PD:g * T + (t2 + 1) * PD],
                              outwT[:, g * H + hh * 512:g * H + (hh + 1) * 512],
                              start=(g == 0), stop=(g == G - 1),
                              waits=([P.now("scalar"), P.now("vector")]
                                     if g == 0 else []),
                              inc=(1 if (g == G - 1 and hh == 1) else 0))
            row0 = c * T + t2 * PD
            htile = hidt[t2 % 2][:, :]
            e_h = ldma(htile, hid_d[row0:row0 + PD, :],
                       waits=[P.now("scalar"), P.now("vector")],
                       stream="ld_h%d" % (t2 % 2))
            mu, r1 = m1c[:, t2:t2 + 1], r1c[:, t2:t2 + 1]
            e_mb = tt(tcol[:, 0:1], mu, r1, ALU.mult, waits=[P.now("vector")])
            e_mb2 = ts(tcol[:, 0:1], tcol[:, 0:1], -1.0, ALU.mult,
                       waits=[e_mb])
            e_l1 = act(attn[:, :], htile, AF.Identity, scale=r1,
                       bias=tcol[:, 0:1], waits=[e_mb2, e_h, P.now("scalar")])
            e_a1 = tt(attn[:, :], attn[:, :], ps_op[:, :], ALU.add,
                      waits=[e_l1, last])
            e_a2 = tt(attn[:, :], attn[:, :], outb_t[:, :], ALU.add,
                      waits=[e_a1])
            e_n2 = layernorm(attn[:, :], attn[:, :], [e_a2])
            e_x1 = tt(x1t[:, :], htile, attn[:, :], ALU.add,
                      waits=[e_n2, P.now("st")])
            sdma(x1sp_d[row0:row0 + PD, :], x1t[:, :], waits=[e_x1])
            e_n3 = layernorm(x1t[:, :], xnb[:, :], [e_x1])
            for h2 in range(HC):
                psv = ps_tr[:, (h2 % 2) * PD:(h2 % 2 + 1) * PD]
                e_tr = transpose(psv, xnb[:, h2 * PD:(h2 + 1) * PD],
                                 waits=[e_n3, P.now("vector")])
                vcopy(xnTs[:, h2 * PD:(h2 + 1) * PD], psv,
                      waits=[e_tr] + ([P.now("st")] if h2 == 0 else []))
            sdma(xnTsp_d[:, row0:row0 + PD].rearrange(
                "(h p) q -> p h q", p=PD),
                xnTs.rearrange("p (h q) -> p h q", h=HC),
                waits=[P.now("vector")])

    for c in range(NCH):
        P.begin_branch(c, 0)
        emit_chunk(c, True)
        P.end_branch()
        P.begin_branch(c, 1)
        emit_chunk(c, False)
        P.end_branch()
        P.barrier()

    P.barrier()
    sweep_es.close()

    ffn_es = contextlib.ExitStack()
    FW = lambda name, shape, dt: ffn_es.enter_context(nc.sbuf_tensor(name, shape, dt))
    wg_sb = FW("wg_sb", [PD, HC * FF], BF16)
    wu_sb = FW("wu_sb", [PD, HC * FF], BF16)
    wd_sb = FW("wd_sb", [PD, FC * H], BF16)

    DQ = 1.0 / 127.0
    fst_es = contextlib.ExitStack()
    fstage = [fst_es.enter_context(nc.sbuf_tensor(f"fstage{i}", [PD, FF], I32))
              for i in range(4)]
    cast_ev = [None] * 4
    WLQ = ("ld_f", "ld_x", "ld_s", "ld_h0")

    def wcast(dst, src, cols, k):
        sl = fstage[k % 4]
        e_l = ldma(sl[:, 0:cols], src,
                   waits=([cast_ev[k % 4]] if cast_ev[k % 4] else []),
                   stream=WLQ[k % 4])
        cast_ev[k % 4] = act(dst, sl[:, 0:cols], AF.Identity, scale=DQ,
                             waits=[e_l])
        return cast_ev[k % 4]

    wk = 0
    for hc2 in range(HC):
        e_wg = wcast(wg_sb[:, hc2 * FF:(hc2 + 1) * FF],
                     wgT_d[hc2 * PD:(hc2 + 1) * PD, :], FF, wk)
        wk += 1
    for hc2 in range(HC):
        e_wu = wcast(wu_sb[:, hc2 * FF:(hc2 + 1) * FF],
                     wuT_d[hc2 * PD:(hc2 + 1) * PD, :], FF, wk)
        wk += 1
    for f in range(FC):
        e_wd = wcast(wd_sb[:, f * H:(f + 1) * H],
                     wdT_d[f * PD:(f + 1) * PD, :], H, wk)
        wk += 1
    fst_es.close()

    frhs2 = [FW(f"frhs{i}", [PD, HC * T2], BF16) for i in range(2)]
    hbuf = FW("hbuf", [PD, FC * T2], BF16)
    sgr2 = [FW(f"sgr{i}", [PD, T2], F32) for i in range(2)]
    f_x1t = FW("f_x1t", [PD, H], F32)
    f_sq = FW("f_sq", [PD, H], F32)
    f_ev = FW("f_ev", [PD, H], F32)
    f_zero = FW("f_zero", [PD, H], F32)
    e_fz = P.op("gpsimd", lambda e: e.memset(f_zero[:, :], 0.0), inc=1)

    last_silu = [None, None]
    last_tt = [None, None]
    frhs_user = [None, None]
    fx1_user = None
    evac_ev = [None, None]
    sd_ev = None
    down_prev = None
    eo_last = None
    n_half = L // T2
    for j in range(n_half):
        row0 = j * T2
        if "ffn" in GUARDS:
            P.begin_branch(j)
        fr = frhs2[j % 2]
        wf = [frhs_user[j % 2]] if frhs_user[j % 2] else []
        for hc2 in range(HC):
            ldma(fr[:, hc2 * T2:(hc2 + 1) * T2],
                 xnTsp_d[hc2 * PD:(hc2 + 1) * PD, row0:row0 + T2],
                 waits=(wf if hc2 == 0 else []), stream="sync")
        e_r = P.now("sync")
        tts = []
        for fo in range(FC):
            bp = fo % 2
            bankG = (ps_a, ps_xp)[bp]
            bankU = (ps_b, ps_bb)[bp]
            wg0 = [e_r, e_wg] if fo == 0 else []
            if last_silu[bp]:
                wg0 = wg0 + [last_silu[bp]]
            lg = None
            for hc2 in range(HC):
                lg = mm(bankG[:, 0:T2],
                        wg_sb[:, hc2 * FF + fo * PD:hc2 * FF + (fo + 1) * PD],
                        fr[:, hc2 * T2:(hc2 + 1) * T2],
                        start=(hc2 == 0), stop=(hc2 == HC - 1),
                        waits=(wg0 if hc2 == 0 else []),
                        inc=(1 if hc2 == HC - 1 else 0))
            wu0 = [e_wu] if fo == 0 else []
            if last_tt[bp]:
                wu0 = wu0 + [last_tt[bp]]
            lu = None
            for hc2 in range(HC):
                lu = mm(bankU[:, 0:T2],
                        wu_sb[:, hc2 * FF + fo * PD:hc2 * FF + (fo + 1) * PD],
                        fr[:, hc2 * T2:(hc2 + 1) * T2],
                        start=(hc2 == 0), stop=(hc2 == HC - 1),
                        waits=(wu0 if hc2 == 0 else []),
                        inc=(1 if hc2 == HC - 1 else 0))
            frhs_user[j % 2] = lu
            e_sg = act(sgr2[bp][:, :], bankG[:, 0:T2], AF.Silu,
                       waits=[lg] + ([last_tt[bp]] if last_tt[bp] else []))
            last_silu[bp] = e_sg
            wt = [e_sg, lu]
            if fo == 0 and down_prev:
                wt.append(down_prev)
            last_tt[bp] = tt(hbuf[:, fo * T2:(fo + 1) * T2], sgr2[bp][:, :],
                             bankU[:, 0:T2], ALU.mult, waits=wt)
            tts.append(last_tt[bp])
        for i in range(TS2):
            sl = i % 2
            e_x1 = ldma(f_x1t[:, :],
                        x1sp_d[row0 + i * PD:row0 + (i + 1) * PD, :],
                        waits=([fx1_user] if fx1_user else []),
                        stream="ld_h1")
            halves = ((ps_op[:, 0:512], ps_op[:, 512:1024]) if sl == 0
                      else (ps_bb2[:, 0:512], ps_f[:, 0:512]))
            m_hh = [None, None]
            for fo in range(FC):
                for hh in range(2):
                    w = []
                    if fo == 0 and hh == 0:
                        w = [e_wd] + ([evac_ev[sl]] if evac_ev[sl] else [])
                    if hh == 0 and i == 0:
                        w = w + [tts[fo]]
                    m_hh[hh] = mm(halves[hh],
                                  hbuf[:, fo * T2 + i * PD:fo * T2 + (i + 1) * PD],
                                  wd_sb[:, fo * H + hh * 512:fo * H + (hh + 1) * 512],
                                  start=(fo == 0), stop=(fo == FC - 1),
                                  waits=w,
                                  inc=(1 if fo == FC - 1 else 0))
            b = 1 + 12 * sl
            e_ev1 = act(f_ev[:, 0:512], halves[0], AF.Copy,
                        waits=[m_hh[0]] + ([eo_last] if eo_last else []))
            e_ev2 = act(f_ev[:, 512:1024], halves[1], AF.Copy, waits=[m_hh[1]])
            evac_ev[sl] = e_ev2
            e_s = vred(tcol[:, b:b + 1], f_ev[:, :], waits=[e_ev2])
            e_q = act(f_sq[:, :], f_ev[:, :], AF.Square,
                      accum_out=tcol[:, b + 1:b + 2],
                      waits=([sd_ev] if sd_ev else []))
            e_mu = ts(tcol[:, b + 2:b + 3], tcol[:, b:b + 1], 1.0 / H,
                      ALU.mult, waits=[e_s])
            e_e2 = ts(tcol[:, b + 3:b + 4], tcol[:, b + 1:b + 2], 1.0 / H,
                      ALU.mult, waits=[e_q])
            e_m2 = tt(tcol[:, b + 4:b + 5], tcol[:, b + 2:b + 3],
                      tcol[:, b + 2:b + 3], ALU.mult, waits=[e_mu])
            e_v = tt(tcol[:, b + 5:b + 6], tcol[:, b + 3:b + 4],
                     tcol[:, b + 4:b + 5], ALU.subtract, waits=[e_e2, e_m2])
            e_ln = act(tcol[:, b + 6:b + 7], tcol[:, b + 5:b + 6], AF.Ln,
                       bias=EPS, waits=[e_v])
            e_rs = act(tcol[:, b + 7:b + 8], tcol[:, b + 6:b + 7], AF.Exp,
                       scale=-0.5, waits=[e_ln])
            e_mn = ts(tcol[:, b + 8:b + 9], tcol[:, b + 2:b + 3], -1.0,
                      ALU.mult, waits=[e_mu])
            e_bc = tt(tcol[:, b + 9:b + 10], tcol[:, b + 8:b + 9],
                      tcol[:, b + 7:b + 8], ALU.mult, waits=[e_rs, e_mn])
            e_n4 = act(f_ev[:, :], f_ev[:, :], AF.Identity,
                       scale=tcol[:, b + 7:b + 8], bias=tcol[:, b + 9:b + 10],
                       waits=[e_bc])
            e_o = tt(f_sq[:, :], f_x1t[:, :], f_ev[:, :], ALU.add,
                     waits=[e_n4, e_x1])
            sd_ev = sdma(out_d[row0 + i * PD:row0 + (i + 1) * PD, :],
                         f_sq[:, :], waits=[e_o])
            fx1_user = e_o
            eo_last = e_o
            if i == TS2 - 1:
                down_prev = m_hh[1]
        if "ffn" in GUARDS:
            P.else_branch()
            for i in range(TS2):
                sdma(out_d[row0 + i * PD:row0 + (i + 1) * PD, :],
                     f_zero[:, :], waits=[e_fz])
            P.end_branch()

    P.op("gpsimd", None, waits=[P.now("st")], inc=0)
    P.barrier()
    ffn_es.close()

    with es, nc.Block() as block:
        P.emit(block)
    return nc


def _prep_weights(inp):
    g = lambda k: np.ascontiguousarray(inp[k])
    w = {}
    w["inwT_dr"] = np.ascontiguousarray(g("L0_attn_in_w").T)
    w["outwT_dr"] = np.ascontiguousarray(g("L0_attn_out_w").T)
    w["xpwT_dr"] = np.ascontiguousarray(g("L0_attn_xp_w").T)
    w["dtpwT_dr"] = np.ascontiguousarray(g("L0_attn_dtp_w").T)
    w["lnw_dr"] = np.ascontiguousarray(g("L0_attn_ln_w").reshape(HC, PD).T)
    w["lnb_dr"] = np.ascontiguousarray(g("L0_attn_ln_b").reshape(HC, PD).T)
    w["inb_dr"] = np.ascontiguousarray(g("L0_attn_in_b").reshape(2 * G, PD).T)
    w["cvw_dr"] = np.ascontiguousarray(
        g("L0_attn_conv_w").reshape(G, PD, KCV).transpose(1, 0, 2).reshape(PD, G * KCV))
    w["cvb_dr"] = np.ascontiguousarray(g("L0_attn_conv_b").reshape(G, PD).T)
    w["dtb_dr"] = np.ascontiguousarray(g("L0_attn_dtp_b").reshape(G, PD).T)
    w["alog_dr"] = np.ascontiguousarray(
        g("L0_attn_A_log").reshape(G, PD, NST).transpose(1, 0, 2).reshape(PD, G * NST))
    w["dvec_dr"] = np.ascontiguousarray(g("L0_attn_D").reshape(G, PD).T)
    w["outb_dr"] = np.ascontiguousarray(g("L0_attn_out_b").reshape(1, H))
    w["wgT_dr"] = np.ascontiguousarray(g("L0_ffn_w_gate").T)
    w["wuT_dr"] = np.ascontiguousarray(g("L0_ffn_w_up").T)
    w["wdT_dr"] = np.ascontiguousarray(g("L0_ffn_w_down").T)
    return w


def kernel(**inputs):
    L = inputs["L0_attn_hidden"].shape[1]
    NCH = L // 512
    w = _prep_weights(inputs)
    nc = build(L)
    bounds = [0, NCH // 4, NCH // 2, 3 * NCH // 4, NCH]
    in_maps = []
    for core in range(8):
        b, q = core // 4, core % 4
        own = 0
        for c in range(bounds[q], bounds[q + 1]):
            own |= 1 << c
        scanonly = 0
        for c in range(0, bounds[q]):
            scanonly |= 1 << c
        m = dict(w)
        m["hidden"] = np.ascontiguousarray(
            np.asarray(inputs["L0_attn_hidden"][b], dtype=np.float32))
        m["cfg_dr"] = np.array([[own, scanonly, 0, 0]],
                               dtype=np.uint32).view(np.int32)
        in_maps.append(m)
    res = run_bass_kernel_spmd(nc, in_maps, core_ids=list(range(8)))
    out = np.empty((2, L, H), np.float32)
    for core in range(8):
        b, q = core // 4, core % 4
        r0, r1 = bounds[q] * 512, bounds[q + 1] * 512
        out[b, r0:r1] = res.results[core]["out"][r0:r1]
    return out


# version 09
# speedup vs baseline: 1.5276x; 1.0014x over previous
import contextlib
import numpy as np

import concourse.bass as bass
import concourse.mybir as mybir
from concourse.bass_utils import run_bass_kernel_spmd

F32 = mybir.dt.float32
BF16 = mybir.dt.bfloat16
I32 = mybir.dt.int32
AF = mybir.ActivationFunctionType
ALU = mybir.AluOpType
AX = mybir.AxisListType

H = 1024
INNER = 2048
NST = 16
KCV = 4
RDT = 64
FF = 3072
PD = 128
G = INNER // PD
HC = H // PD
FC = FF // PD
EPS = 1e-5

STREAMS = ("sync", "tensor", "vector", "scalar", "gpsimd", "st",
           "ld_h0", "ld_h1", "ld_s", "ld_f", "ld_x")
ENG_OF = {"sync": "sync", "tensor": "tensor", "vector": "vector",
          "scalar": "scalar", "gpsimd": "gpsimd", "st": "sync",
          "ld_h0": "sync", "ld_h1": "sync", "ld_s": "sync",
          "ld_f": "sync", "ld_x": "sync"}


ENGS5 = ("sync", "tensor", "vector", "scalar", "gpsimd")

GUARDS = {"gate", "D", "E", "ffn"}


class Plan:

    def __init__(self, nc, sems, cfg_ap=None, cfg_ready=None):
        self.nc = nc
        self.sems = sems
        self.cnt = {s: 0 for s in STREAMS}
        self.ops = {e: [] for e in ENGS5}
        self.cfg_ap = cfg_ap
        self.cfg_ready = cfg_ready
        self._br = None

    def op(self, stream, fn, waits=(), inc=None):
        if inc is None:
            inc = 16 if (stream.startswith("ld_") or stream in ("sync", "st")) else 1
        eng = ENG_OF[stream]
        self.ops[eng].append(("op", fn, tuple(waits), inc, stream))
        self.cnt[stream] += inc
        return (stream, self.cnt[stream])

    def now(self, stream):
        return (stream, self.cnt[stream])

    def barrier(self):
        marks = [(s, self.cnt[s]) for s in STREAMS if self.cnt[s] > 0]
        for e in self.ops:
            w = [m for m in marks if ENG_OF[m[0]] != e]
            if w:
                self.ops[e].append(("op", None, tuple(w), 0, e))

    def begin_branch(self, bit, mask_idx=0):
        assert self._br is None
        self._br = [bit, dict(self.cnt), None]
        for e in ENGS5:
            self.ops[e].append(("if", bit, mask_idx))

    def else_branch(self):
        bit, snap, _ = self._br
        self._br[2] = dict(self.cnt)
        self.cnt = dict(snap)
        for e in ENGS5:
            self.ops[e].append(("else",))

    def end_branch(self):
        bit, snap, armA = self._br
        self._br = None
        if armA is None:
            armA = dict(self.cnt)
            self.cnt = dict(snap)
            for e in ENGS5:
                self.ops[e].append(("else",))
        armB = dict(self.cnt)
        final = {s: max(armA[s], armB[s]) for s in STREAMS}
        padA = {s: final[s] - armA[s] for s in STREAMS if final[s] > armA[s]}
        padB = {s: final[s] - armB[s] for s in STREAMS if final[s] > armB[s]}
        self.cnt = final
        for e in ENGS5:
            for k in range(len(self.ops[e]) - 1, -1, -1):
                if self.ops[e][k] == ("else",):
                    self.ops[e][k] = ("else", padA)
                    break
            self.ops[e].append(("endif", padB))

    def _prune_empty_regions(self):
        for e, ops in self.ops.items():
            out = []
            i = 0
            while i < len(ops):
                entry = ops[i]
                if entry[0] != "if":
                    out.append(entry)
                    i += 1
                    continue
                depth = 0
                j = i
                k_else = None
                while True:
                    kind = ops[j][0]
                    if kind == "if":
                        depth += 1
                    elif kind == "endif":
                        depth -= 1
                        if depth == 0:
                            break
                    elif kind == "else" and depth == 1:
                        k_else = j
                    j += 1
                padA = ops[k_else][1]
                padB = ops[j][1]
                body_ops = [ops[m] for m in range(i + 1, j)
                            if ops[m][0] == "op"]
                my_padA = any(ENG_OF[s] == e for s in padA)
                my_padB = any(ENG_OF[s] == e for s in padB)
                inner_marks = any(ops[m][0] in ("if", "else", "endif")
                                  for m in range(i + 1, j)
                                  if m != k_else)
                if not body_ops and not my_padA and not my_padB \
                        and not inner_marks:
                    i = j + 1
                else:
                    out.extend(ops[i:j + 1])
                    i = j + 1
            self.ops[e] = out

    def emit(self, block):
        self._prune_empty_regions()

        def make(eng_name):
            def body(eng):
                has_if = any(entry[0] == "if" for entry in self.ops[eng_name])

                def inner(rmask, rb):
                    satisfied = {}
                    sat_stack = []
                    guard_stack = []
                    loaded = [False]

                    def ensure_mask():
                        if loaded[0]:
                            return
                        loaded[0] = True
                        ws, wv = self.cfg_ready
                        if satisfied.get(ws, 0) < wv:
                            eng.wait_ge(self.sems[ws], wv)
                            satisfied[ws] = wv
                        eng.reg_load(rmask[0], self.cfg_ap[0])
                        eng.reg_load(rmask[1], self.cfg_ap[1])

                    def do_waits(waits):
                        for (ws, wv) in waits:
                            if satisfied.get(ws, 0) >= wv:
                                continue
                            eng.wait_ge(self.sems[ws], wv)
                            satisfied[ws] = wv

                    def do_pads(pad):
                        for s, k in pad.items():
                            if ENG_OF[s] == eng_name:
                                step = 16 if (s.startswith("ld_") or
                                              s in ("st",)) else 1
                                left = k
                                while left > 0:
                                    d = min(step, left)
                                    eng.sem_inc(self.sems[s], d)
                                    left -= d

                    for entry in self.ops[eng_name]:
                        kind = entry[0]
                        if kind == "op":
                            _, fn, waits, inc, stream = entry
                            do_waits(waits)
                            if fn is not None:
                                instr = fn(eng)
                                if inc:
                                    instr.then_inc(self.sems[stream], inc)
                        elif kind == "if":
                            ensure_mask()
                            bit, mi = entry[1], entry[2]
                            eng.reg_alu(rb, rmask[mi], bit,
                                        ALU.logical_shift_right)
                            eng.reg_alu(rb, rb, 1, ALU.bitwise_and)
                            gd = eng.If_ne(rb, 0)
                            gd.__enter__()
                            guard_stack.append(gd)
                            sat_stack.append(dict(satisfied))
                        elif kind == "else":
                            padA = entry[1]
                            do_pads(padA)
                            snap = sat_stack[-1]
                            sat_stack[-1] = dict(satisfied)
                            satisfied.clear()
                            satisfied.update(snap)
                            guard_stack[-1].__exit__(None, None, None)
                            gd = eng.Else()
                            gd.__enter__()
                            guard_stack[-1] = gd
                        elif kind == "endif":
                            padB = entry[1]
                            do_pads(padB)
                            guard_stack.pop().__exit__(None, None, None)
                            armA_sat = sat_stack.pop()
                            merged = {}
                            for k in set(armA_sat) | set(satisfied):
                                merged[k] = min(armA_sat.get(k, 0),
                                                satisfied.get(k, 0))
                            satisfied.clear()
                            satisfied.update(merged)

                if has_if:
                    with eng.register("r_mask0_" + eng_name) as rm0, \
                         eng.register("r_mask1_" + eng_name) as rm1, \
                         eng.register("r_bit_" + eng_name) as rb:
                        inner((rm0, rm1), rb)
                else:
                    inner(None, None)

            return body

        for e in ENGS5:
            if self.ops[e]:
                getattr(block, e)(make(e))


def build(L, T=512):
    NCH = L // T
    TS = T // PD
    SGN = 2
    NSG = G // SGN
    T2 = 512
    TS2 = T2 // PD

    nc = bass.Bass("TRN2", target_bir_lowering=False, debug=False)

    hid_d = nc.dram_tensor("hidden", [L, H], F32, kind="ExternalInput").ap()
    inwT_d = nc.dram_tensor("inwT_dr", [H, 2 * INNER], F32, kind="ExternalInput").ap()
    outwT_d = nc.dram_tensor("outwT_dr", [INNER, H], F32, kind="ExternalInput").ap()
    xpwT_d = nc.dram_tensor("xpwT_dr", [INNER, 96], F32, kind="ExternalInput").ap()
    dtpwT_d = nc.dram_tensor("dtpwT_dr", [RDT, INNER], F32, kind="ExternalInput").ap()
    lnw_d = nc.dram_tensor("lnw_dr", [PD, HC], F32, kind="ExternalInput").ap()
    lnb_d = nc.dram_tensor("lnb_dr", [PD, HC], F32, kind="ExternalInput").ap()
    inb_d = nc.dram_tensor("inb_dr", [PD, 2 * G], F32, kind="ExternalInput").ap()
    cvw_d = nc.dram_tensor("cvw_dr", [PD, G * KCV], F32, kind="ExternalInput").ap()
    cvb_d = nc.dram_tensor("cvb_dr", [PD, G], F32, kind="ExternalInput").ap()
    dtb_d = nc.dram_tensor("dtb_dr", [PD, G], F32, kind="ExternalInput").ap()
    alog_d = nc.dram_tensor("alog_dr", [PD, G * NST], F32, kind="ExternalInput").ap()
    dvec_d = nc.dram_tensor("dvec_dr", [PD, G], F32, kind="ExternalInput").ap()
    outb_d = nc.dram_tensor("outb_dr", [1, H], F32, kind="ExternalInput").ap()
    wgT_d = nc.dram_tensor("wgT_dr", [H, FF], I32, kind="ExternalInput").ap()
    wuT_d = nc.dram_tensor("wuT_dr", [H, FF], I32, kind="ExternalInput").ap()
    wdT_d = nc.dram_tensor("wdT_dr", [FF, H], I32, kind="ExternalInput").ap()

    cfg_d = nc.dram_tensor("cfg_dr", [1, 4], I32, kind="ExternalInput").ap()
    out_d = nc.dram_tensor("out", [L, H], F32, kind="ExternalOutput").ap()
    x1sp_d = nc.dram_tensor("x1sp", [L, H], F32).ap()
    xnTsp_d = nc.dram_tensor("xnTsp", [H, L], BF16).ap()

    es = contextlib.ExitStack()
    SB = lambda name, shape, dt: es.enter_context(nc.sbuf_tensor(name, shape, dt))
    PS = lambda name, shape, dt: es.enter_context(nc.psum_tensor(name, shape, dt))

    ps_a = PS("ps_a", [PD, 512], F32)
    ps_b = PS("ps_b", [PD, 512], F32)
    ps_xp = PS("ps_xp", [PD, 512], F32)
    ps_bb = PS("ps_bb", [PD, 512], F32)
    ps_bb2 = PS("ps_bb2", [PD, 512], F32)
    ps_op = PS("ps_op", [PD, 1024], F32)
    ps_f = PS("ps_f", [PD, 512], F32)
    ps_trv = None

    sems = {}
    for s in STREAMS:
        sems[s] = es.enter_context(nc.semaphore("sem_" + s))
    P = Plan(nc, sems)
    cfg_s = SB("cfg_s", [1, 4], I32)

    tcol = SB("tcol", [PD, 32], F32)
    consts = SB("consts", [PD, 8], F32)

    sweep_es = contextlib.ExitStack()
    SW = lambda name, shape, dt: sweep_es.enter_context(nc.sbuf_tensor(name, shape, dt))
    ident_b = SW("ident_b", [PD, PD], BF16)
    acols = SW("acols", [PD, G * NST], F32)
    cvw = SW("cvw", [PD, G * KCV], F32)
    cvb = SW("cvb", [PD, G], F32)
    dtb = SW("dtb", [PD, G], F32)
    dvec = SW("dvec", [PD, G], F32)
    inb = SW("inb", [PD, 2 * G], F32)
    lnw = SW("lnw", [PD, HC], F32)
    lnb = SW("lnb", [PD, HC], F32)
    outb_t = SW("outb_t", [PD, H], F32)
    states = SW("states", [PD, G * NST], F32)
    onesb = SW("onesb", [1, PD], BF16)
    sel16 = SW("sel16", [16, NST * PD], BF16)
    brow_b = SW("brow_b", [16, T], BF16)
    crow_b = SW("crow_b", [16, T], BF16)
    dtr_b = SW("dtr_b", [RDT, T], BF16)
    inwT = SW("inwT", [PD, HC * 2 * INNER], BF16)
    outwT = SW("outwT", [PD, G * H], BF16)
    xpwT = SW("xpwT", [PD, G * 96], BF16)
    dtpwT = SW("dtpwT", [RDT, G * PD], BF16)
    hidt = [SW(f"hidt{i}", [PD, H], F32) for i in range(2)]
    hnT = SW("hnT", [PD, HC * T], BF16)
    xcb = SW("xcb", [PD, G * T], BF16)
    yg = SW("yg", [PD, G * T], BF16)
    halo = SW("halo", [PD, G * 3], F32)
    dt_sg = SW("dt_sg", [PD, SGN * T], F32)
    dtx_sg = SW("dtx_sg", [PD, SGN * T], F32)
    r1c = SW("r1c", [PD, TS], F32)
    m1c = SW("m1c", [PD, TS], F32)
    xnb = SW("xnb", [PD, H], BF16)
    pro_es = contextlib.ExitStack()
    wstage = pro_es.enter_context(nc.sbuf_tensor("wstage", [PD, 2048], F32))
    obrow = pro_es.enter_context(nc.sbuf_tensor("obrow", [1, H], F32))
    obrow_b = pro_es.enter_context(nc.sbuf_tensor("obrow_b", [1, H], BF16))
    pro_es.close()
    d_es = contextlib.ExitStack()
    DW = lambda name, shape, dt: d_es.enter_context(nc.sbuf_tensor(name, shape, dt))
    xg = [DW(f"xg{i}", [PD, T + 3], F32) for i in range(2)]
    zb = [DW(f"zb{i}", [PD, T], F32) for i in range(2)]
    dec = [DW(f"dec{i}", [PD, T], F32) for i in range(2)]
    wti = [DW(f"wti{i}", [PD, T], F32) for i in range(2)]
    srot = DW("srot", [PD, 2 * T], BF16)
    prot = [DW(f"prot{i}", [PD, T], BF16) for i in range(2)]
    cbcr = [DW(f"cbcr{i}", [PD, T], BF16) for i in range(2)]
    bbr = [DW(f"bbr{i}", [PD, T], F32) for i in range(2)]
    yac = DW("yac", [PD, SGN * T], F32)
    d_es.close()
    attn = SW("attn", [PD, H], F32)
    x1t = SW("x1t", [PD, H], F32)
    xnTs = SW("xnTs", [PD, HC * PD], BF16)

    def ldma(dst, src, waits=(), stream="sync"):
        return P.op(stream, lambda e, d=dst, s=src: e.dma_start(out=d, in_=s),
                    waits=waits)

    def sdma(dst, src, waits=()):
        return P.op("st", lambda e, d=dst, s=src: e.dma_start(out=d, in_=s),
                    waits=waits)

    def act(out, in_, func, bias=None, scale=None, accum_out=None, waits=()):
        kw = {}
        if bias is not None:
            kw["bias"] = bias
        if scale is not None:
            kw["scale"] = scale
        if accum_out is not None:
            kw["accum_out"] = accum_out
        return P.op("scalar",
                    lambda e, o=out, i=in_, f=func, kw=kw: e.activation(o, i, f, **kw),
                    waits=waits)

    def tt(out, a, b, op=ALU.mult, waits=()):
        return P.op("vector",
                    lambda e, o=out, x=a, y=b, p=op: e.tensor_tensor(o, x, y, op=p),
                    waits=waits)

    def gtt(out, a, b, op=ALU.mult, waits=()):
        return P.op("gpsimd",
                    lambda e, o=out, x=a, y=b, p=op: e.tensor_tensor(o, x, y, op=p),
                    waits=waits)

    def ts(out, a, s1, op0, waits=()):
        return P.op("vector",
                    lambda e, o=out, x=a, s=s1, p=op0:
                    e.tensor_scalar(o, x, s, None, p),
                    waits=waits)

    def stt(out, in0, scalar, in1, op0, op1, waits=()):
        return P.op("vector",
                    lambda e, o=out, a=in0, s=scalar, b=in1, p0=op0, p1=op1:
                    e.scalar_tensor_tensor(o, a, s, b, op0=p0, op1=p1),
                    waits=waits)

    def vred(out, in_, op=ALU.add, axis=AX.X, waits=()):
        return P.op("vector",
                    lambda e, o=out, i=in_, p=op, ax=axis:
                    e.tensor_reduce(o, i, axis=ax, op=p),
                    waits=waits)

    def vcopy(out, in_, waits=()):
        return P.op("vector",
                    lambda e, o=out, i=in_: e.tensor_copy(o, i), waits=waits)

    def scan(out, d0, d1, init, waits=()):
        return P.op("vector",
                    lambda e, o=out, a=d0, b=d1, i=init:
                    e.tensor_tensor_scan(o, a, b, i, op0=ALU.mult, op1=ALU.add),
                    waits=waits)

    def mm(psum, lhsT, rhs, start, stop, waits=(), inc=1):
        return P.op("tensor",
                    lambda e, o=psum, l=lhsT, r=rhs, st=start, sp=stop:
                    e.matmul(o, l, r, start=st, stop=sp),
                    waits=waits, inc=inc)

    def transpose(psum, in_, waits=(), inc=1):
        return P.op("tensor",
                    lambda e, o=psum, i=in_: e.transpose(o, i, ident_b[:, :]),
                    waits=waits, inc=inc)

    def layernorm(src, dst, waits):
        e_s = vred(tcol[:, 1:2], src, waits=list(waits) + [P.now("vector")])
        e_q = act(ps_op[:, 0:H], src, AF.Square, accum_out=tcol[:, 2:3],
                  waits=list(waits) + [P.now("scalar"), P.now("tensor"),
                                       P.now("vector")])
        e_mu = ts(tcol[:, 3:4], tcol[:, 1:2], 1.0 / H, ALU.mult, waits=[e_s])
        e_e2 = ts(tcol[:, 4:5], tcol[:, 2:3], 1.0 / H, ALU.mult, waits=[e_q])
        e_m2 = tt(tcol[:, 5:6], tcol[:, 3:4], tcol[:, 3:4], ALU.mult, waits=[e_mu])
        e_v = tt(tcol[:, 6:7], tcol[:, 4:5], tcol[:, 5:6], ALU.subtract,
                 waits=[e_e2, e_m2])
        e_ln = act(tcol[:, 7:8], tcol[:, 6:7], AF.Ln, bias=EPS, waits=[e_v])
        e_r = act(tcol[:, 8:9], tcol[:, 7:8], AF.Exp, scale=-0.5, waits=[e_ln])
        e_mn = ts(tcol[:, 9:10], tcol[:, 3:4], -1.0, ALU.mult, waits=[e_mu])
        e_bc = tt(tcol[:, 10:11], tcol[:, 9:10], tcol[:, 8:9], ALU.mult,
                  waits=[e_r, e_mn])
        return act(dst, src, AF.Identity, scale=tcol[:, 8:9], bias=tcol[:, 10:11],
                   waits=[e_bc])

    CONST_VALS = [EPS, 1.0, -1.0, -0.5, 1.0 / 127.0]
    for ci, cv in enumerate(CONST_VALS):
        P.op("gpsimd", lambda e, i=ci, v=cv: e.memset(consts[:, i:i + 1], v),
             inc=1)
        nc.const_aps.register(F32, cv, consts[:, ci:ci + 1])
    e_idf = P.op("gpsimd", lambda e: e.memset(ident_b[:, :], 0.0), inc=1)
    P.op("gpsimd",
         lambda e: e.affine_select(out=ident_b[:, :], in_=ident_b[:, :],
                                   compare_op=ALU.not_equal, fill=1.0,
                                   base=0, pattern=[[-1, PD]],
                                   channel_multiplier=1),
         waits=[e_idf], inc=1)
    P.op("gpsimd", lambda e: e.memset(onesb[:, :], 1.0), inc=1)
    e_s16a = P.op("gpsimd", lambda e: e.memset(sel16[:, :], 0.0), inc=1)
    P.op("gpsimd",
         lambda e: e.affine_select(out=sel16.rearrange("c (n p) -> c n p", n=NST),
                                   in_=sel16.rearrange("c (n p) -> c n p", n=NST),
                                   compare_op=ALU.not_equal, fill=1.0, base=0,
                                   pattern=[[-1, NST], [0, PD]],
                                   channel_multiplier=1),
         waits=[e_s16a], inc=1)
    P.op("gpsimd", lambda e: e.memset(states[:, :], 0.0), inc=1)
    P.op("gpsimd", lambda e: e.memset(yg[:, :], 0.0), inc=1)
    P.op("gpsimd", lambda e: e.memset(halo[:, :], 0.0), inc=1)
    e_pro = P.now("gpsimd")

    small = [(acols, alog_d), (cvw, cvw_d), (cvb, cvb_d), (dtb, dtb_d),
             (dvec, dvec_d), (inb, inb_d), (lnw, lnw_d), (lnb, lnb_d)]
    e_small = [ldma(t[:, :], d, stream="ld_s") for (t, d) in small]
    e_ob = ldma(obrow[:, :], outb_d, stream="ld_s")
    ldma(cfg_s[:, :], cfg_d, stream="ld_s")
    e_lds = P.now("ld_s")
    P.cfg_ap = (cfg_s[0:1, 0:1], cfg_s[0:1, 1:2])
    P.cfg_ready = e_lds
    e_a1 = act(acols[:, :], acols[:, :], AF.Exp, waits=[e_lds])
    ts(acols[:, :], acols[:, :], -1.0, ALU.mult, waits=[e_a1])

    e_obb = act(obrow_b[:, :], obrow[:, :], AF.Copy, waits=[e_lds])
    e_obm = [mm(ps_op[:, hh * 512:(hh + 1) * 512], onesb[:, :],
                obrow_b[:, hh * 512:(hh + 1) * 512], True, True,
                waits=[e_obb, e_pro]) for hh in range(2)]
    vcopy(outb_t[:, :], ps_op[:, :], waits=e_obm)

    def load_cast(dst_sb, dram, rows, cols, col0_dst, prev, scale=None):
        stage_cols = 2048
        for c0 in range(0, cols, stage_cols):
            cw = min(stage_cols, cols - c0)
            w = [prev] if prev else []
            e_l = ldma(wstage[0:rows, 0:cw], dram[:, c0:c0 + cw], waits=w)
            prev = act(dst_sb[0:rows, col0_dst + c0: col0_dst + c0 + cw],
                       wstage[0:rows, 0:cw], AF.Identity, scale=scale, waits=[e_l])
        return prev

    prev = None
    for hc in range(HC):
        prev = load_cast(inwT, inwT_d[hc * PD:(hc + 1) * PD, :], PD,
                         2 * INNER, hc * 2 * INNER, prev)
    for g in range(G):
        prev = load_cast(outwT, outwT_d[g * PD:(g + 1) * PD, :], PD, H,
                         g * H, prev)
        prev = load_cast(xpwT, xpwT_d[g * PD:(g + 1) * PD, :], PD, 96,
                         g * 96, prev)
    prev = load_cast(dtpwT, dtpwT_d, RDT, G * PD, 0, prev)
    P.barrier()

    ps_tr = ps_bb2.bitcast(BF16)

    def emit_chunk(c, is_own):
        for ti in range(TS):
            row0 = c * T + ti * PD
            htile = hidt[ti % 2][:, :]
            e_h = ldma(htile, hid_d[row0:row0 + PD, :],
                       waits=[P.now("scalar"), P.now("vector"),
                              P.now("tensor")], stream="ld_h%d" % (ti % 2))
            sumc, sqc = tcol[:, 1:2], tcol[:, 2:3]
            e_s = vred(sumc, htile, waits=[e_h, P.now("vector")])
            e_q = act(ps_op[:, 0:H], htile, AF.Square, accum_out=sqc,
                      waits=[e_h, P.now("scalar"), P.now("tensor")])
            mu = m1c[:, ti:ti + 1]
            e_mu = ts(mu, sumc, 1.0 / H, ALU.mult, waits=[e_s])
            e_e2 = ts(tcol[:, 4:5], sqc, 1.0 / H, ALU.mult, waits=[e_q])
            e_m2 = tt(tcol[:, 5:6], mu, mu, ALU.mult, waits=[e_mu])
            e_v = tt(tcol[:, 6:7], tcol[:, 4:5], tcol[:, 5:6], ALU.subtract,
                     waits=[e_e2, e_m2])
            e_ln = act(tcol[:, 7:8], tcol[:, 6:7], AF.Ln, bias=EPS, waits=[e_v])
            r1 = r1c[:, ti:ti + 1]
            e_r1 = act(r1, tcol[:, 7:8], AF.Exp, scale=-0.5, waits=[e_ln])
            e_r1sq = tt(tcol[:, 5:6], r1, r1, ALU.mult, waits=[e_r1])
            e_v2 = tt(tcol[:, 13:14], tcol[:, 6:7], tcol[:, 5:6], ALU.mult,
                      waits=[e_r1sq, e_v])
            e_l2 = act(tcol[:, 14:15], tcol[:, 13:14], AF.Ln, bias=EPS,
                       waits=[e_v2, e_r1])
            e_r2 = act(tcol[:, 9:10], tcol[:, 14:15], AF.Exp, scale=-0.5,
                       waits=[e_l2])
            e_sc = tt(tcol[:, 10:11], r1, tcol[:, 9:10], ALU.mult, waits=[e_r2])
            e_mn = ts(tcol[:, 11:12], mu, -1.0, ALU.mult, waits=[e_mu])
            e_bc = tt(tcol[:, 12:13], tcol[:, 11:12], tcol[:, 10:11], ALU.mult,
                      waits=[e_sc, e_mn])
            e_u = act(xnb[:, :], htile, AF.Identity, scale=tcol[:, 10:11],
                      bias=tcol[:, 12:13], waits=[e_bc, P.now("tensor")])
            for h2 in range(HC):
                psv = ps_tr[:, (h2 % 2) * PD:(h2 % 2 + 1) * PD]
                e_tr = transpose(psv, xnb[:, h2 * PD:(h2 + 1) * PD],
                                 waits=[e_u, P.now("scalar")])
                act(hnT[:, h2 * T + ti * PD: h2 * T + (ti + 1) * PD],
                    psv, AF.Identity, scale=lnw[:, h2:h2 + 1],
                    bias=lnb[:, h2:h2 + 1], waits=[e_tr])
        P.barrier()

        for g in range(G):
            xgc = xg[g % 2]
            pp = ps_a if g % 2 == 0 else ps_b
            last = None
            for h2 in range(HC):
                last = mm(pp[:, :], inwT[:, h2 * 2 * INNER + g * PD:
                                         h2 * 2 * INNER + (g + 1) * PD],
                          hnT[:, h2 * T:(h2 + 1) * T],
                          start=(h2 == 0), stop=(h2 == HC - 1),
                          waits=([P.now("scalar")] if h2 == 0 else []),
                          inc=(1 if h2 == HC - 1 else 0))
            e_x = act(xgc[:, 3:3 + T], pp[:, :], AF.Identity,
                      bias=inb[:, g:g + 1], waits=[last, P.now("vector")])
            e_hl = vcopy(xgc[:, 0:3], halo[:, g * 3:(g + 1) * 3],
                         waits=[e_x, P.now("vector")])
            zbc = zb[g % 2][:, 0:T]
            e_ck = ts(zbc, xgc[:, 0:T], cvw[:, g * KCV:g * KCV + 1],
                      ALU.mult, waits=[e_x, e_hl, P.now("scalar")])
            for k in range(1, KCV):
                e_ck = stt(zbc, xgc[:, k:k + T],
                           cvw[:, g * KCV + k:g * KCV + k + 1], zbc,
                           ALU.mult, ALU.add, waits=[e_ck])
            vcopy(halo[:, g * 3:(g + 1) * 3], xgc[:, T:T + 3], waits=[e_x])
            act(xcb[:, g * T:(g + 1) * T], zbc, AF.Silu,
                bias=cvb[:, g:g + 1], waits=[e_ck])
        if is_own:
            for g in range(G):
                pp = ps_a if g % 2 == 0 else ps_b
                last = None
                for h2 in range(HC):
                    last = mm(pp[:, :],
                              inwT[:, h2 * 2 * INNER + INNER + g * PD:
                                   h2 * 2 * INNER + INNER + (g + 1) * PD],
                              hnT[:, h2 * T:(h2 + 1) * T],
                              start=(h2 == 0), stop=(h2 == HC - 1),
                              waits=([P.now("scalar")] if h2 == 0 else []),
                              inc=(1 if h2 == HC - 1 else 0))
                act(yg[:, g * T:(g + 1) * T], pp[:, :], AF.Silu,
                    bias=inb[:, G + g:G + g + 1], waits=[last])
        P.barrier()

        w0 = [P.now("scalar")]
        for g in range(G):
            e_g1 = mm(ps_xp[0:16, :], xpwT[:, g * 96 + 64:g * 96 + 80],
                      xcb[:, g * T:(g + 1) * T], start=(g == 0),
                      stop=(g == G - 1), waits=(w0 if g == 0 else []),
                      inc=(1 if g == G - 1 else 0))
        e_bb = act(brow_b[:, :], ps_xp[0:16, :], AF.Copy, waits=[e_g1])
        e_cb = None
        if is_own:
            for g in range(G):
                e_g2 = mm(ps_xp[32:48, :], xpwT[:, g * 96 + 80:g * 96 + 96],
                          xcb[:, g * T:(g + 1) * T], start=(g == 0),
                          stop=(g == G - 1),
                          waits=([e_bb] if g == 0 else []),
                          inc=(1 if g == G - 1 else 0))
            e_cb = act(crow_b[:, :], ps_xp[32:48, :], AF.Copy, waits=[e_g2])
        for g in range(G):
            e_g3 = mm(ps_xp[64:128, :], xpwT[:, g * 96:g * 96 + 64],
                      xcb[:, g * T:(g + 1) * T], start=(g == 0),
                      stop=(g == G - 1),
                      waits=([e_cb if e_cb else e_bb] if g == 0 else []),
                      inc=(1 if g == G - 1 else 0))
        e_db = act(dtr_b[:, :], ps_xp[64:128, :], AF.Copy, waits=[e_g3])
        P.barrier()

        for sg in range(NSG):
            g0 = sg * SGN
            for i in range(SGN):
                g = g0 + i
                pp = ps_a if i % 2 == 0 else ps_b
                e_m = mm(pp[:, :], dtpwT[:, g * PD:(g + 1) * PD],
                         dtr_b[:, :], True, True,
                         waits=[e_db, P.now("scalar")])
                e_z = act(zb[0][:, 0:T], pp[:, :], AF.Identity,
                          bias=dtb[:, g:g + 1],
                          waits=[e_m, P.now("vector")])
                o0 = i * T
                e_ab = act(dtx_sg[:, o0:o0 + T], zb[0][:, 0:T], AF.Abs,
                           waits=[e_z, P.now("vector")])
                e_ex = act(dtx_sg[:, o0:o0 + T], dtx_sg[:, o0:o0 + T],
                           AF.Exp, scale=-1.0, waits=[e_ab])
                e_ln1 = act(dtx_sg[:, o0:o0 + T], dtx_sg[:, o0:o0 + T],
                            AF.Ln, bias=1.0, waits=[e_ex])
                e_rl = act(dt_sg[:, o0:o0 + T], zb[0][:, 0:T], AF.Relu,
                           waits=[e_ln1])
                tt(dt_sg[:, o0:o0 + T], dt_sg[:, o0:o0 + T],
                   dtx_sg[:, o0:o0 + T], ALU.add, waits=[e_rl])
            Wf = SGN * T
            e_dt = P.now("vector")
            e_dtx = tt(dtx_sg[:, 0:Wf], dt_sg[:, 0:Wf],
                       xcb[:, g0 * T:(g0 + SGN) * T], ALU.mult,
                       waits=[e_dt, P.now("scalar")])
            e_y0 = None
            if is_own:
                e_y0 = P.op("gpsimd",
                            lambda e, b=yac: e.memset(b[:, :], 0.0),
                            waits=[P.now("vector")], inc=1)

            dec_user = [None, None]
            bb_user = [None, None]
            bbe_user = [None, None]
            cc_user = None
            cb_user = [None, None]
            s_user = [None, None]
            for n in range(NST):
                kb = n % 2
                pbb = ps_bb if kb == 0 else ps_bb2
                e_bm = mm(pbb[:, :], sel16[:, n * PD:(n + 1) * PD],
                          brow_b[:, :], True, True,
                          waits=[e_bb] + ([bbe_user[kb]] if bbe_user[kb] else
                                          [P.now("vector"),
                                           P.now("scalar")]))
                e_be = act(bbr[kb][:, :], pbb[:, :], AF.Copy,
                           waits=[e_bm] + ([bb_user[kb]] if bb_user[kb]
                                           else []))
                bbe_user[kb] = e_be
                e_cv = None
                if is_own:
                    e_cm = mm(ps_xp[:, :], sel16[:, n * PD:(n + 1) * PD],
                              crow_b[:, :], True, True,
                              waits=[e_cb] + ([cc_user] if cc_user else
                                              [P.now("vector")]))
                    kc = n % 2
                    e_cv = act(cbcr[kc][:, :], ps_xp[:, :], AF.Copy,
                               waits=[e_cm] + ([cb_user[kc]] if cb_user[kc]
                                               else []))
                    cc_user = e_cv
                for i in range(SGN):
                    g = g0 + i
                    sl = i % 2
                    kd = (n * SGN + i) % 2
                    e_de = act(dec[kd][:, :], dt_sg[:, i * T:(i + 1) * T],
                               AF.Exp,
                               scale=acols[:, g * NST + n:g * NST + n + 1],
                               waits=([dec_user[kd]] if dec_user[kd]
                                      else [P.now("vector")]))
                    e_w = gtt(wti[kd][:, :], dtx_sg[:, i * T:(i + 1) * T],
                              bbr[kb][:, :], ALU.mult,
                              waits=[e_dtx, e_be] +
                              ([dec_user[kd]] if dec_user[kd] else []))
                    bb_user[kb] = e_w
                    e_sc = scan(srot[:, sl * T:(sl + 1) * T], dec[kd][:, :],
                                wti[kd][:, :],
                                states[:, g * NST + n:g * NST + n + 1],
                                waits=[e_de, e_w] +
                                ([s_user[sl]] if s_user[sl] else []))
                    dec_user[kd] = e_sc
                    s_user[sl] = e_sc
                    if is_own:
                        kc = n % 2
                        e_p = tt(prot[kd][:, :], srot[:, sl * T:(sl + 1) * T],
                                 cbcr[kc][:, :], ALU.mult,
                                 waits=[e_sc, e_cv])
                        cb_user[kc] = e_p
                        tt(yac[:, i * T:(i + 1) * T],
                           yac[:, i * T:(i + 1) * T],
                           prot[kd][:, :], ALU.add,
                           waits=[e_p] + ([e_y0] if n == 0 else []))
                        s_user[sl] = e_p
                    if sl == 1:
                        sv = srot.rearrange("p (i t) -> p i t", i=2)[
                            :, :, T - 1:T]
                        dv = states.rearrange("p (g n) -> p g n", g=G)[
                            :, g0 + i - 1:g0 + i + 1, n:n + 1]
                        e_xt = vcopy(dv, sv, waits=[])
                        s_user[0] = e_xt
                        s_user[1] = e_xt
            if is_own:
                for i in range(SGN):
                    g = g0 + i
                    e_yf = stt(yac[:, i * T:(i + 1) * T],
                               xcb[:, g * T:(g + 1) * T], dvec[:, g:g + 1],
                               yac[:, i * T:(i + 1) * T], ALU.mult, ALU.add,
                               waits=[P.now("vector")])
                    tt(yg[:, g * T:(g + 1) * T], yac[:, i * T:(i + 1) * T],
                       yg[:, g * T:(g + 1) * T], ALU.mult,
                       waits=[e_yf])
        P.barrier()

        if not is_own:
            return
        for t2 in range(TS):
            last = None
            for g in range(G):
                for hh in range(2):
                    last = mm(ps_op[:, hh * 512:(hh + 1) * 512],
                              yg[:, g * T + t2 * PD:g * T + (t2 + 1) * PD],
                              outwT[:, g * H + hh * 512:g * H + (hh + 1) * 512],
                              start=(g == 0), stop=(g == G - 1),
                              waits=([P.now("scalar"), P.now("vector")]
                                     if g == 0 else []),
                              inc=(1 if (g == G - 1 and hh == 1) else 0))
            row0 = c * T + t2 * PD
            htile = hidt[t2 % 2][:, :]
            e_h = ldma(htile, hid_d[row0:row0 + PD, :],
                       waits=[P.now("scalar"), P.now("vector")],
                       stream="ld_h%d" % (t2 % 2))
            mu, r1 = m1c[:, t2:t2 + 1], r1c[:, t2:t2 + 1]
            e_mb = tt(tcol[:, 0:1], mu, r1, ALU.mult, waits=[P.now("vector")])
            e_mb2 = ts(tcol[:, 0:1], tcol[:, 0:1], -1.0, ALU.mult,
                       waits=[e_mb])
            e_l1 = act(attn[:, :], htile, AF.Identity, scale=r1,
                       bias=tcol[:, 0:1], waits=[e_mb2, e_h, P.now("scalar")])
            e_a1 = tt(attn[:, :], attn[:, :], ps_op[:, :], ALU.add,
                      waits=[e_l1, last])
            e_a2 = tt(attn[:, :], attn[:, :], outb_t[:, :], ALU.add,
                      waits=[e_a1])
            e_n2 = layernorm(attn[:, :], attn[:, :], [e_a2])
            e_x1 = tt(x1t[:, :], htile, attn[:, :], ALU.add,
                      waits=[e_n2, P.now("st")])
            sdma(x1sp_d[row0:row0 + PD, :], x1t[:, :], waits=[e_x1])
            e_n3 = layernorm(x1t[:, :], xnb[:, :], [e_x1])
            for h2 in range(HC):
                psv = ps_tr[:, (h2 % 2) * PD:(h2 % 2 + 1) * PD]
                e_tr = transpose(psv, xnb[:, h2 * PD:(h2 + 1) * PD],
                                 waits=[e_n3, P.now("vector")])
                vcopy(xnTs[:, h2 * PD:(h2 + 1) * PD], psv,
                      waits=[e_tr] + ([P.now("st")] if h2 == 0 else []))
            sdma(xnTsp_d[:, row0:row0 + PD].rearrange(
                "(h p) q -> p h q", p=PD),
                xnTs.rearrange("p (h q) -> p h q", h=HC),
                waits=[P.now("vector")])

    for c in range(NCH):
        P.begin_branch(c, 0)
        emit_chunk(c, True)
        P.end_branch()
        P.begin_branch(c, 1)
        emit_chunk(c, False)
        P.end_branch()
        P.barrier()

    P.barrier()
    sweep_es.close()

    ffn_es = contextlib.ExitStack()
    FW = lambda name, shape, dt: ffn_es.enter_context(nc.sbuf_tensor(name, shape, dt))
    wg_sb = FW("wg_sb", [PD, HC * FF], BF16)
    wu_sb = FW("wu_sb", [PD, HC * FF], BF16)
    wd_sb = FW("wd_sb", [PD, FC * H], BF16)

    DQ = 1.0 / 127.0
    fst_es = contextlib.ExitStack()
    fstage = [fst_es.enter_context(nc.sbuf_tensor(f"fstage{i}", [PD, FF], I32))
              for i in range(4)]
    cast_ev = [None] * 4
    WLQ = ("ld_f", "ld_x", "ld_s", "ld_h0")

    def wcast(dst, src, cols, k):
        sl = fstage[k % 4]
        e_l = ldma(sl[:, 0:cols], src,
                   waits=([cast_ev[k % 4]] if cast_ev[k % 4] else []),
                   stream=WLQ[k % 4])
        cast_ev[k % 4] = act(dst, sl[:, 0:cols], AF.Identity, scale=DQ,
                             waits=[e_l])
        return cast_ev[k % 4]

    wk = 0
    for hc2 in range(HC):
        e_wg = wcast(wg_sb[:, hc2 * FF:(hc2 + 1) * FF],
                     wgT_d[hc2 * PD:(hc2 + 1) * PD, :], FF, wk)
        wk += 1
    for hc2 in range(HC):
        e_wu = wcast(wu_sb[:, hc2 * FF:(hc2 + 1) * FF],
                     wuT_d[hc2 * PD:(hc2 + 1) * PD, :], FF, wk)
        wk += 1
    for f in range(FC):
        e_wd = wcast(wd_sb[:, f * H:(f + 1) * H],
                     wdT_d[f * PD:(f + 1) * PD, :], H, wk)
        wk += 1
    fst_es.close()

    frhs2 = [FW(f"frhs{i}", [PD, HC * T2], BF16) for i in range(2)]
    hbuf = FW("hbuf", [PD, FC * T2], BF16)
    sgr2 = [FW(f"sgr{i}", [PD, T2], F32) for i in range(2)]
    f_x1t = FW("f_x1t", [PD, H], F32)
    f_sq = FW("f_sq", [PD, H], F32)
    f_ev = FW("f_ev", [PD, H], F32)

    last_silu = [None, None]
    last_tt = [None, None]
    frhs_user = [None, None]
    fx1_user = None
    evac_ev = [None, None]
    sd_ev = None
    down_prev = None
    eo_last = None
    n_half = L // T2
    for j in range(n_half):
        row0 = j * T2
        if "ffn" in GUARDS:
            P.begin_branch(j)
        fr = frhs2[j % 2]
        wf = [frhs_user[j % 2]] if frhs_user[j % 2] else []
        for hc2 in range(HC):
            ldma(fr[:, hc2 * T2:(hc2 + 1) * T2],
                 xnTsp_d[hc2 * PD:(hc2 + 1) * PD, row0:row0 + T2],
                 waits=(wf if hc2 == 0 else []), stream="sync")
        e_r = P.now("sync")
        tts = []
        for fo in range(FC):
            bp = fo % 2
            bankG = (ps_a, ps_xp)[bp]
            bankU = (ps_b, ps_bb)[bp]
            wg0 = [e_r, e_wg] if fo == 0 else []
            if last_silu[bp]:
                wg0 = wg0 + [last_silu[bp]]
            lg = None
            for hc2 in range(HC):
                lg = mm(bankG[:, 0:T2],
                        wg_sb[:, hc2 * FF + fo * PD:hc2 * FF + (fo + 1) * PD],
                        fr[:, hc2 * T2:(hc2 + 1) * T2],
                        start=(hc2 == 0), stop=(hc2 == HC - 1),
                        waits=(wg0 if hc2 == 0 else []),
                        inc=(1 if hc2 == HC - 1 else 0))
            wu0 = [e_wu] if fo == 0 else []
            if last_tt[bp]:
                wu0 = wu0 + [last_tt[bp]]
            lu = None
            for hc2 in range(HC):
                lu = mm(bankU[:, 0:T2],
                        wu_sb[:, hc2 * FF + fo * PD:hc2 * FF + (fo + 1) * PD],
                        fr[:, hc2 * T2:(hc2 + 1) * T2],
                        start=(hc2 == 0), stop=(hc2 == HC - 1),
                        waits=(wu0 if hc2 == 0 else []),
                        inc=(1 if hc2 == HC - 1 else 0))
            frhs_user[j % 2] = lu
            e_sg = act(sgr2[bp][:, :], bankG[:, 0:T2], AF.Silu,
                       waits=[lg] + ([last_tt[bp]] if last_tt[bp] else []))
            last_silu[bp] = e_sg
            wt = [e_sg, lu]
            if fo == 0 and down_prev:
                wt.append(down_prev)
            last_tt[bp] = tt(hbuf[:, fo * T2:(fo + 1) * T2], sgr2[bp][:, :],
                             bankU[:, 0:T2], ALU.mult, waits=wt)
            tts.append(last_tt[bp])
        for i in range(TS2):
            sl = i % 2
            e_x1 = ldma(f_x1t[:, :],
                        x1sp_d[row0 + i * PD:row0 + (i + 1) * PD, :],
                        waits=([fx1_user] if fx1_user else []),
                        stream="ld_h1")
            halves = ((ps_op[:, 0:512], ps_op[:, 512:1024]) if sl == 0
                      else (ps_bb2[:, 0:512], ps_f[:, 0:512]))
            m_hh = [None, None]
            for fo in range(FC):
                for hh in range(2):
                    w = []
                    if fo == 0 and hh == 0:
                        w = [e_wd] + ([evac_ev[sl]] if evac_ev[sl] else [])
                    if hh == 0 and i == 0:
                        w = w + [tts[fo]]
                    m_hh[hh] = mm(halves[hh],
                                  hbuf[:, fo * T2 + i * PD:fo * T2 + (i + 1) * PD],
                                  wd_sb[:, fo * H + hh * 512:fo * H + (hh + 1) * 512],
                                  start=(fo == 0), stop=(fo == FC - 1),
                                  waits=w,
                                  inc=(1 if fo == FC - 1 else 0))
            b = 1 + 12 * sl
            e_ev1 = act(f_ev[:, 0:512], halves[0], AF.Copy,
                        waits=[m_hh[0]] + ([eo_last] if eo_last else []))
            e_ev2 = act(f_ev[:, 512:1024], halves[1], AF.Copy, waits=[m_hh[1]])
            evac_ev[sl] = e_ev2
            e_s = vred(tcol[:, b:b + 1], f_ev[:, :], waits=[e_ev2])
            e_q = act(f_sq[:, :], f_ev[:, :], AF.Square,
                      accum_out=tcol[:, b + 1:b + 2],
                      waits=([sd_ev] if sd_ev else []))
            e_mu = ts(tcol[:, b + 2:b + 3], tcol[:, b:b + 1], 1.0 / H,
                      ALU.mult, waits=[e_s])
            e_e2 = ts(tcol[:, b + 3:b + 4], tcol[:, b + 1:b + 2], 1.0 / H,
                      ALU.mult, waits=[e_q])
            e_m2 = tt(tcol[:, b + 4:b + 5], tcol[:, b + 2:b + 3],
                      tcol[:, b + 2:b + 3], ALU.mult, waits=[e_mu])
            e_v = tt(tcol[:, b + 5:b + 6], tcol[:, b + 3:b + 4],
                     tcol[:, b + 4:b + 5], ALU.subtract, waits=[e_e2, e_m2])
            e_ln = act(tcol[:, b + 6:b + 7], tcol[:, b + 5:b + 6], AF.Ln,
                       bias=EPS, waits=[e_v])
            e_rs = act(tcol[:, b + 7:b + 8], tcol[:, b + 6:b + 7], AF.Exp,
                       scale=-0.5, waits=[e_ln])
            e_mn = ts(tcol[:, b + 8:b + 9], tcol[:, b + 2:b + 3], -1.0,
                      ALU.mult, waits=[e_mu])
            e_bc = tt(tcol[:, b + 9:b + 10], tcol[:, b + 8:b + 9],
                      tcol[:, b + 7:b + 8], ALU.mult, waits=[e_rs, e_mn])
            e_n4 = act(f_ev[:, :], f_ev[:, :], AF.Identity,
                       scale=tcol[:, b + 7:b + 8], bias=tcol[:, b + 9:b + 10],
                       waits=[e_bc])
            e_o = tt(f_sq[:, :], f_x1t[:, :], f_ev[:, :], ALU.add,
                     waits=[e_n4, e_x1])
            sd_ev = sdma(out_d[row0 + i * PD:row0 + (i + 1) * PD, :],
                         f_sq[:, :], waits=[e_o])
            fx1_user = e_o
            eo_last = e_o
            if i == TS2 - 1:
                down_prev = m_hh[1]
        if "ffn" in GUARDS:
            P.end_branch()

    P.op("gpsimd", None, waits=[P.now("st")], inc=0)
    P.barrier()
    ffn_es.close()

    with es, nc.Block() as block:
        P.emit(block)
    return nc


def _prep_weights(inp):
    g = lambda k: np.ascontiguousarray(inp[k])
    w = {}
    w["inwT_dr"] = np.ascontiguousarray(g("L0_attn_in_w").T)
    w["outwT_dr"] = np.ascontiguousarray(g("L0_attn_out_w").T)
    w["xpwT_dr"] = np.ascontiguousarray(g("L0_attn_xp_w").T)
    w["dtpwT_dr"] = np.ascontiguousarray(g("L0_attn_dtp_w").T)
    w["lnw_dr"] = np.ascontiguousarray(g("L0_attn_ln_w").reshape(HC, PD).T)
    w["lnb_dr"] = np.ascontiguousarray(g("L0_attn_ln_b").reshape(HC, PD).T)
    w["inb_dr"] = np.ascontiguousarray(g("L0_attn_in_b").reshape(2 * G, PD).T)
    w["cvw_dr"] = np.ascontiguousarray(
        g("L0_attn_conv_w").reshape(G, PD, KCV).transpose(1, 0, 2).reshape(PD, G * KCV))
    w["cvb_dr"] = np.ascontiguousarray(g("L0_attn_conv_b").reshape(G, PD).T)
    w["dtb_dr"] = np.ascontiguousarray(g("L0_attn_dtp_b").reshape(G, PD).T)
    w["alog_dr"] = np.ascontiguousarray(
        g("L0_attn_A_log").reshape(G, PD, NST).transpose(1, 0, 2).reshape(PD, G * NST))
    w["dvec_dr"] = np.ascontiguousarray(g("L0_attn_D").reshape(G, PD).T)
    w["outb_dr"] = np.ascontiguousarray(g("L0_attn_out_b").reshape(1, H))
    w["wgT_dr"] = np.ascontiguousarray(g("L0_ffn_w_gate").T)
    w["wuT_dr"] = np.ascontiguousarray(g("L0_ffn_w_up").T)
    w["wdT_dr"] = np.ascontiguousarray(g("L0_ffn_w_down").T)
    return w


def kernel(**inputs):
    L = inputs["L0_attn_hidden"].shape[1]
    NCH = L // 512
    w = _prep_weights(inputs)
    nc = build(L)
    bounds = [0, NCH // 4, NCH // 2, 3 * NCH // 4, NCH]
    in_maps = []
    for core in range(8):
        b, q = core // 4, core % 4
        own = 0
        for c in range(bounds[q], bounds[q + 1]):
            own |= 1 << c
        scanonly = 0
        for c in range(0, bounds[q]):
            scanonly |= 1 << c
        m = dict(w)
        m["hidden"] = np.ascontiguousarray(
            np.asarray(inputs["L0_attn_hidden"][b], dtype=np.float32))
        m["cfg_dr"] = np.array([[own, scanonly, 0, 0]],
                               dtype=np.uint32).view(np.int32)
        in_maps.append(m)
    res = run_bass_kernel_spmd(nc, in_maps, core_ids=list(range(8)))
    out = np.empty((2, L, H), np.float32)
    for core in range(8):
        b, q = core // 4, core % 4
        r0, r1 = bounds[q] * 512, bounds[q + 1] * 512
        out[b, r0:r1] = res.results[core]["out"][r0:r1]
    return out


# version 10
# speedup vs baseline: 1.8864x; 1.2349x over previous
import bisect
import contextlib
import numpy as np

import concourse.bass as bass
import concourse.mybir as mybir
from concourse.bass_utils import run_bass_kernel_spmd

F32 = mybir.dt.float32
BF16 = mybir.dt.bfloat16
I32 = mybir.dt.int32
AF = mybir.ActivationFunctionType
ALU = mybir.AluOpType
AX = mybir.AxisListType

H = 1024
INNER = 2048
NST = 16
KCV = 4
RDT = 64
FF = 3072
PD = 128
G = INNER // PD
HC = H // PD
FC = FF // PD
EPS = 1e-5

STREAMS = ("sync", "tensor", "vector", "scalar", "gpsimd", "st",
           "ld_h0", "ld_h1", "ld_s", "ld_f", "ld_x")
ENG_OF = {"sync": "sync", "tensor": "tensor", "vector": "vector",
          "scalar": "scalar", "gpsimd": "gpsimd", "st": "sync",
          "ld_h0": "sync", "ld_h1": "sync", "ld_s": "sync",
          "ld_f": "sync", "ld_x": "sync"}


ENGS5 = ("sync", "tensor", "vector", "scalar", "gpsimd")

GUARDS = {"gate", "D", "E", "ffn"}


class Plan:

    def __init__(self, nc, sems, cfg_ap=None, cfg_ready=None):
        self.nc = nc
        self.sems = sems
        self.cnt = {s: 0 for s in STREAMS}
        self.ops = {e: [] for e in ENGS5}
        self.cfg_ap = cfg_ap
        self.cfg_ready = cfg_ready
        self._br = None
        self.refs = {s: set() for s in STREAMS}

    def _record(self, waits):
        for (ws, wv) in waits:
            if self._br is not None and wv > self._br[1].get(ws, 0):
                continue
            self.refs[ws].add(wv)

    def op(self, stream, fn, waits=(), inc=None):
        if inc is None:
            inc = 16 if (stream.startswith("ld_") or stream in ("sync", "st")) else 1
        eng = ENG_OF[stream]
        self._record(waits)
        self.ops[eng].append(("op", fn, tuple(waits), inc, stream))
        self.cnt[stream] += inc
        return (stream, self.cnt[stream])

    def now(self, stream):
        return (stream, self.cnt[stream])

    def barrier(self):
        marks = [(s, self.cnt[s]) for s in STREAMS if self.cnt[s] > 0]
        self._record(marks)
        for e in self.ops:
            w = [m for m in marks if ENG_OF[m[0]] != e]
            if w:
                self.ops[e].append(("op", None, tuple(w), 0, e))

    def begin_branch(self, bit, mask_idx=0):
        assert self._br is None
        self._br = [bit, dict(self.cnt), None]
        for e in ENGS5:
            self.ops[e].append(("if", bit, mask_idx))

    def else_branch(self):
        bit, snap, _ = self._br
        self._br[2] = dict(self.cnt)
        self.cnt = dict(snap)
        for e in ENGS5:
            self.ops[e].append(("else",))

    def end_branch(self):
        bit, snap, armA = self._br
        self._br = None
        if armA is None:
            armA = dict(self.cnt)
            self.cnt = dict(snap)
            for e in ENGS5:
                self.ops[e].append(("else",))
        armB = dict(self.cnt)
        final = {s: max(armA[s], armB[s]) for s in STREAMS}
        padA = {s: (armA[s], final[s] - armA[s])
                for s in STREAMS if final[s] > armA[s]}
        padB = {s: (armB[s], final[s] - armB[s])
                for s in STREAMS if final[s] > armB[s]}
        self.cnt = final
        for e in ENGS5:
            for k in range(len(self.ops[e]) - 1, -1, -1):
                if self.ops[e][k] == ("else",):
                    self.ops[e][k] = ("else", padA)
                    break
            self.ops[e].append(("endif", padB))

    def _prune_empty_regions(self):
        for e, ops in self.ops.items():
            out = []
            i = 0
            while i < len(ops):
                entry = ops[i]
                if entry[0] != "if":
                    out.append(entry)
                    i += 1
                    continue
                depth = 0
                j = i
                k_else = None
                while True:
                    kind = ops[j][0]
                    if kind == "if":
                        depth += 1
                    elif kind == "endif":
                        depth -= 1
                        if depth == 0:
                            break
                    elif kind == "else" and depth == 1:
                        k_else = j
                    j += 1
                padA = ops[k_else][1]
                padB = ops[j][1]
                body_ops = [ops[m] for m in range(i + 1, j)
                            if ops[m][0] == "op"]
                my_padA = any(ENG_OF[s] == e for s in padA)
                my_padB = any(ENG_OF[s] == e for s in padB)
                inner_marks = any(ops[m][0] in ("if", "else", "endif")
                                  for m in range(i + 1, j)
                                  if m != k_else)
                if not body_ops and not my_padA and not my_padB \
                        and not inner_marks:
                    i = j + 1
                else:
                    out.extend(ops[i:j + 1])
                    i = j + 1
            self.ops[e] = out

    def emit(self, block):
        self._prune_empty_regions()
        if self.cfg_ready:
            self.refs[self.cfg_ready[0]].add(self.cfg_ready[1])
        refs_sorted = {s: sorted(self.refs[s]) for s in STREAMS}

        def make(eng_name):
            def body(eng):
                has_if = any(entry[0] == "if" for entry in self.ops[eng_name])

                def inner(rmask, rb):
                    satisfied = {}
                    sat_stack = []
                    guard_stack = []
                    loaded = [False]

                    def ensure_mask():
                        if loaded[0]:
                            return
                        loaded[0] = True
                        ws, wv = self.cfg_ready
                        if satisfied.get(ws, 0) < wv:
                            eng.wait_ge(self.sems[ws], wv)
                            satisfied[ws] = wv
                        eng.reg_load(rmask[0], self.cfg_ap[0])
                        eng.reg_load(rmask[1], self.cfg_ap[1])

                    def do_waits(waits):
                        for (ws, wv) in waits:
                            if satisfied.get(ws, 0) >= wv:
                                continue
                            eng.wait_ge(self.sems[ws], wv)
                            satisfied[ws] = wv

                    def do_pads(pad):
                        for s, (base, k) in pad.items():
                            if ENG_OF[s] != eng_name or k <= 0:
                                continue
                            tgt = base + k
                            cur = base
                            rs = refs_sorted[s]
                            lo = bisect.bisect_right(rs, base)
                            while lo < len(rs) and rs[lo] < tgt:
                                v = rs[lo]
                                eng.sem_inc(self.sems[s], v - cur)
                                cur = v
                                lo += 1
                            eng.sem_inc(self.sems[s], tgt - cur)

                    for entry in self.ops[eng_name]:
                        kind = entry[0]
                        if kind == "op":
                            _, fn, waits, inc, stream = entry
                            do_waits(waits)
                            if fn is not None:
                                instr = fn(eng)
                                if inc:
                                    instr.then_inc(self.sems[stream], inc)
                        elif kind == "if":
                            ensure_mask()
                            bit, mi = entry[1], entry[2]
                            eng.reg_alu(rb, rmask[mi], bit,
                                        ALU.logical_shift_right)
                            eng.reg_alu(rb, rb, 1, ALU.bitwise_and)
                            gd = eng.If_ne(rb, 0)
                            gd.__enter__()
                            guard_stack.append(gd)
                            sat_stack.append(dict(satisfied))
                        elif kind == "else":
                            padA = entry[1]
                            do_pads(padA)
                            snap = sat_stack[-1]
                            sat_stack[-1] = dict(satisfied)
                            satisfied.clear()
                            satisfied.update(snap)
                            guard_stack[-1].__exit__(None, None, None)
                            gd = eng.Else()
                            gd.__enter__()
                            guard_stack[-1] = gd
                        elif kind == "endif":
                            padB = entry[1]
                            do_pads(padB)
                            guard_stack.pop().__exit__(None, None, None)
                            armA_sat = sat_stack.pop()
                            merged = {}
                            for k in set(armA_sat) | set(satisfied):
                                merged[k] = min(armA_sat.get(k, 0),
                                                satisfied.get(k, 0))
                            satisfied.clear()
                            satisfied.update(merged)

                if has_if:
                    with eng.register("r_mask0_" + eng_name) as rm0, \
                         eng.register("r_mask1_" + eng_name) as rm1, \
                         eng.register("r_bit_" + eng_name) as rb:
                        inner((rm0, rm1), rb)
                else:
                    inner(None, None)

            return body

        for e in ENGS5:
            if self.ops[e]:
                getattr(block, e)(make(e))


def build(L, T=512):
    NCH = L // T
    TS = T // PD
    SGN = 2
    NSG = G // SGN
    T2 = 512
    TS2 = T2 // PD

    nc = bass.Bass("TRN2", target_bir_lowering=False, debug=False)

    hid_d = nc.dram_tensor("hidden", [L, H], F32, kind="ExternalInput").ap()
    inwT_d = nc.dram_tensor("inwT_dr", [H, 2 * INNER], F32, kind="ExternalInput").ap()
    outwT_d = nc.dram_tensor("outwT_dr", [INNER, H], F32, kind="ExternalInput").ap()
    xpwT_d = nc.dram_tensor("xpwT_dr", [INNER, 96], F32, kind="ExternalInput").ap()
    dtpwT_d = nc.dram_tensor("dtpwT_dr", [RDT, INNER], F32, kind="ExternalInput").ap()
    lnw_d = nc.dram_tensor("lnw_dr", [PD, HC], F32, kind="ExternalInput").ap()
    lnb_d = nc.dram_tensor("lnb_dr", [PD, HC], F32, kind="ExternalInput").ap()
    inb_d = nc.dram_tensor("inb_dr", [PD, 2 * G], F32, kind="ExternalInput").ap()
    cvw_d = nc.dram_tensor("cvw_dr", [PD, G * KCV], F32, kind="ExternalInput").ap()
    cvb_d = nc.dram_tensor("cvb_dr", [PD, G], F32, kind="ExternalInput").ap()
    dtb_d = nc.dram_tensor("dtb_dr", [PD, G], F32, kind="ExternalInput").ap()
    alog_d = nc.dram_tensor("alog_dr", [PD, G * NST], F32, kind="ExternalInput").ap()
    dvec_d = nc.dram_tensor("dvec_dr", [PD, G], F32, kind="ExternalInput").ap()
    outb_d = nc.dram_tensor("outb_dr", [1, H], F32, kind="ExternalInput").ap()
    wgT_d = nc.dram_tensor("wgT_dr", [H, FF], I32, kind="ExternalInput").ap()
    wuT_d = nc.dram_tensor("wuT_dr", [H, FF], I32, kind="ExternalInput").ap()
    wdT_d = nc.dram_tensor("wdT_dr", [FF, H], I32, kind="ExternalInput").ap()

    cfg_d = nc.dram_tensor("cfg_dr", [1, 4], I32, kind="ExternalInput").ap()
    out_d = nc.dram_tensor("out", [L, H], F32, kind="ExternalOutput").ap()
    x1sp_d = nc.dram_tensor("x1sp", [L, H], F32).ap()
    xnTsp_d = nc.dram_tensor("xnTsp", [H, L], BF16).ap()

    es = contextlib.ExitStack()
    SB = lambda name, shape, dt: es.enter_context(nc.sbuf_tensor(name, shape, dt))
    PS = lambda name, shape, dt: es.enter_context(nc.psum_tensor(name, shape, dt))

    ps_a = PS("ps_a", [PD, 512], F32)
    ps_b = PS("ps_b", [PD, 512], F32)
    ps_xp = PS("ps_xp", [PD, 512], F32)
    ps_bb = PS("ps_bb", [PD, 512], F32)
    ps_bb2 = PS("ps_bb2", [PD, 512], F32)
    ps_op = PS("ps_op", [PD, 1024], F32)
    ps_f = PS("ps_f", [PD, 512], F32)
    ps_trv = None

    sems = {}
    for s in STREAMS:
        sems[s] = es.enter_context(nc.semaphore("sem_" + s))
    P = Plan(nc, sems)
    cfg_s = SB("cfg_s", [1, 4], I32)

    tcol = SB("tcol", [PD, 32], F32)
    consts = SB("consts", [PD, 8], F32)

    sweep_es = contextlib.ExitStack()
    SW = lambda name, shape, dt: sweep_es.enter_context(nc.sbuf_tensor(name, shape, dt))
    ident_b = SW("ident_b", [PD, PD], BF16)
    acols = SW("acols", [PD, G * NST], F32)
    cvw = SW("cvw", [PD, G * KCV], F32)
    cvb = SW("cvb", [PD, G], F32)
    dtb = SW("dtb", [PD, G], F32)
    dvec = SW("dvec", [PD, G], F32)
    inb = SW("inb", [PD, 2 * G], F32)
    lnw = SW("lnw", [PD, HC], F32)
    lnb = SW("lnb", [PD, HC], F32)
    outb_t = SW("outb_t", [PD, H], F32)
    states = SW("states", [PD, G * NST], F32)
    onesb = SW("onesb", [1, PD], BF16)
    sel16 = SW("sel16", [16, NST * PD], BF16)
    brow_b = SW("brow_b", [16, T], BF16)
    crow_b = SW("crow_b", [16, T], BF16)
    dtr_b = SW("dtr_b", [RDT, T], BF16)
    inwT = SW("inwT", [PD, HC * 2 * INNER], BF16)
    outwT = SW("outwT", [PD, G * H], BF16)
    xpwT = SW("xpwT", [PD, G * 96], BF16)
    dtpwT = SW("dtpwT", [RDT, G * PD], BF16)
    hidt = [SW(f"hidt{i}", [PD, H], F32) for i in range(2)]
    hnT = SW("hnT", [PD, HC * T], BF16)
    xcb = SW("xcb", [PD, G * T], BF16)
    yg = SW("yg", [PD, G * T], BF16)
    halo = SW("halo", [PD, G * 3], F32)
    dt_sg = SW("dt_sg", [PD, SGN * T], F32)
    dtx_sg = SW("dtx_sg", [PD, SGN * T], F32)
    r1c = SW("r1c", [PD, TS], F32)
    m1c = SW("m1c", [PD, TS], F32)
    xnb = SW("xnb", [PD, H], BF16)
    pro_es = contextlib.ExitStack()
    wstage = pro_es.enter_context(nc.sbuf_tensor("wstage", [PD, 2048], F32))
    obrow = pro_es.enter_context(nc.sbuf_tensor("obrow", [1, H], F32))
    obrow_b = pro_es.enter_context(nc.sbuf_tensor("obrow_b", [1, H], BF16))
    pro_es.close()
    d_es = contextlib.ExitStack()
    DW = lambda name, shape, dt: d_es.enter_context(nc.sbuf_tensor(name, shape, dt))
    xg = [DW(f"xg{i}", [PD, T + 3], F32) for i in range(2)]
    zb = [DW(f"zb{i}", [PD, T], F32) for i in range(2)]
    dec = [DW(f"dec{i}", [PD, T], F32) for i in range(2)]
    wti = [DW(f"wti{i}", [PD, T], F32) for i in range(2)]
    srot = DW("srot", [PD, 2 * T], BF16)
    prot = [DW(f"prot{i}", [PD, T], BF16) for i in range(2)]
    cbcr = [DW(f"cbcr{i}", [PD, T], BF16) for i in range(2)]
    bbr = [DW(f"bbr{i}", [PD, T], F32) for i in range(2)]
    yac = DW("yac", [PD, SGN * T], F32)
    d_es.close()
    attn = SW("attn", [PD, H], F32)
    x1t = SW("x1t", [PD, H], F32)
    xnTs = SW("xnTs", [PD, HC * PD], BF16)

    def ldma(dst, src, waits=(), stream="sync"):
        return P.op(stream, lambda e, d=dst, s=src: e.dma_start(out=d, in_=s),
                    waits=waits)

    def sdma(dst, src, waits=()):
        return P.op("st", lambda e, d=dst, s=src: e.dma_start(out=d, in_=s),
                    waits=waits)

    def act(out, in_, func, bias=None, scale=None, accum_out=None, waits=()):
        kw = {}
        if bias is not None:
            kw["bias"] = bias
        if scale is not None:
            kw["scale"] = scale
        if accum_out is not None:
            kw["accum_out"] = accum_out
        return P.op("scalar",
                    lambda e, o=out, i=in_, f=func, kw=kw: e.activation(o, i, f, **kw),
                    waits=waits)

    def tt(out, a, b, op=ALU.mult, waits=()):
        return P.op("vector",
                    lambda e, o=out, x=a, y=b, p=op: e.tensor_tensor(o, x, y, op=p),
                    waits=waits)

    def gtt(out, a, b, op=ALU.mult, waits=()):
        return P.op("gpsimd",
                    lambda e, o=out, x=a, y=b, p=op: e.tensor_tensor(o, x, y, op=p),
                    waits=waits)

    def ts(out, a, s1, op0, waits=()):
        return P.op("vector",
                    lambda e, o=out, x=a, s=s1, p=op0:
                    e.tensor_scalar(o, x, s, None, p),
                    waits=waits)

    def stt(out, in0, scalar, in1, op0, op1, waits=()):
        return P.op("vector",
                    lambda e, o=out, a=in0, s=scalar, b=in1, p0=op0, p1=op1:
                    e.scalar_tensor_tensor(o, a, s, b, op0=p0, op1=p1),
                    waits=waits)

    def vred(out, in_, op=ALU.add, axis=AX.X, waits=()):
        return P.op("vector",
                    lambda e, o=out, i=in_, p=op, ax=axis:
                    e.tensor_reduce(o, i, axis=ax, op=p),
                    waits=waits)

    def vcopy(out, in_, waits=()):
        return P.op("vector",
                    lambda e, o=out, i=in_: e.tensor_copy(o, i), waits=waits)

    def scan(out, d0, d1, init, waits=()):
        return P.op("vector",
                    lambda e, o=out, a=d0, b=d1, i=init:
                    e.tensor_tensor_scan(o, a, b, i, op0=ALU.mult, op1=ALU.add),
                    waits=waits)

    def mm(psum, lhsT, rhs, start, stop, waits=(), inc=1):
        return P.op("tensor",
                    lambda e, o=psum, l=lhsT, r=rhs, st=start, sp=stop:
                    e.matmul(o, l, r, start=st, stop=sp),
                    waits=waits, inc=inc)

    def transpose(psum, in_, waits=(), inc=1):
        return P.op("tensor",
                    lambda e, o=psum, i=in_: e.transpose(o, i, ident_b[:, :]),
                    waits=waits, inc=inc)

    def layernorm(src, dst, waits):
        e_s = vred(tcol[:, 1:2], src, waits=list(waits) + [P.now("vector")])
        e_q = act(ps_op[:, 0:H], src, AF.Square, accum_out=tcol[:, 2:3],
                  waits=list(waits) + [P.now("scalar"), P.now("tensor"),
                                       P.now("vector")])
        e_mu = ts(tcol[:, 3:4], tcol[:, 1:2], 1.0 / H, ALU.mult, waits=[e_s])
        e_e2 = ts(tcol[:, 4:5], tcol[:, 2:3], 1.0 / H, ALU.mult, waits=[e_q])
        e_m2 = tt(tcol[:, 5:6], tcol[:, 3:4], tcol[:, 3:4], ALU.mult, waits=[e_mu])
        e_v = tt(tcol[:, 6:7], tcol[:, 4:5], tcol[:, 5:6], ALU.subtract,
                 waits=[e_e2, e_m2])
        e_ln = act(tcol[:, 7:8], tcol[:, 6:7], AF.Ln, bias=EPS, waits=[e_v])
        e_r = act(tcol[:, 8:9], tcol[:, 7:8], AF.Exp, scale=-0.5, waits=[e_ln])
        e_mn = ts(tcol[:, 9:10], tcol[:, 3:4], -1.0, ALU.mult, waits=[e_mu])
        e_bc = tt(tcol[:, 10:11], tcol[:, 9:10], tcol[:, 8:9], ALU.mult,
                  waits=[e_r, e_mn])
        return act(dst, src, AF.Identity, scale=tcol[:, 8:9], bias=tcol[:, 10:11],
                   waits=[e_bc])

    CONST_VALS = [EPS, 1.0, -1.0, -0.5, 1.0 / 127.0]
    for ci, cv in enumerate(CONST_VALS):
        P.op("gpsimd", lambda e, i=ci, v=cv: e.memset(consts[:, i:i + 1], v),
             inc=1)
        nc.const_aps.register(F32, cv, consts[:, ci:ci + 1])
    e_idf = P.op("gpsimd", lambda e: e.memset(ident_b[:, :], 0.0), inc=1)
    P.op("gpsimd",
         lambda e: e.affine_select(out=ident_b[:, :], in_=ident_b[:, :],
                                   compare_op=ALU.not_equal, fill=1.0,
                                   base=0, pattern=[[-1, PD]],
                                   channel_multiplier=1),
         waits=[e_idf], inc=1)
    P.op("gpsimd", lambda e: e.memset(onesb[:, :], 1.0), inc=1)
    e_s16a = P.op("gpsimd", lambda e: e.memset(sel16[:, :], 0.0), inc=1)
    P.op("gpsimd",
         lambda e: e.affine_select(out=sel16.rearrange("c (n p) -> c n p", n=NST),
                                   in_=sel16.rearrange("c (n p) -> c n p", n=NST),
                                   compare_op=ALU.not_equal, fill=1.0, base=0,
                                   pattern=[[-1, NST], [0, PD]],
                                   channel_multiplier=1),
         waits=[e_s16a], inc=1)
    P.op("gpsimd", lambda e: e.memset(states[:, :], 0.0), inc=1)
    P.op("gpsimd", lambda e: e.memset(yg[:, :], 0.0), inc=1)
    P.op("gpsimd", lambda e: e.memset(halo[:, :], 0.0), inc=1)
    e_pro = P.now("gpsimd")

    small = [(acols, alog_d), (cvw, cvw_d), (cvb, cvb_d), (dtb, dtb_d),
             (dvec, dvec_d), (inb, inb_d), (lnw, lnw_d), (lnb, lnb_d)]
    e_small = [ldma(t[:, :], d, stream="ld_s") for (t, d) in small]
    e_ob = ldma(obrow[:, :], outb_d, stream="ld_s")
    ldma(cfg_s[:, :], cfg_d, stream="ld_s")
    e_lds = P.now("ld_s")
    P.cfg_ap = (cfg_s[0:1, 0:1], cfg_s[0:1, 1:2])
    P.cfg_ready = e_lds
    e_a1 = act(acols[:, :], acols[:, :], AF.Exp, waits=[e_lds])
    ts(acols[:, :], acols[:, :], -1.0, ALU.mult, waits=[e_a1])

    e_obb = act(obrow_b[:, :], obrow[:, :], AF.Copy, waits=[e_lds])
    e_obm = [mm(ps_op[:, hh * 512:(hh + 1) * 512], onesb[:, :],
                obrow_b[:, hh * 512:(hh + 1) * 512], True, True,
                waits=[e_obb, e_pro]) for hh in range(2)]
    vcopy(outb_t[:, :], ps_op[:, :], waits=e_obm)

    def load_cast(dst_sb, dram, rows, cols, col0_dst, prev, scale=None):
        stage_cols = 2048
        for c0 in range(0, cols, stage_cols):
            cw = min(stage_cols, cols - c0)
            w = [prev] if prev else []
            e_l = ldma(wstage[0:rows, 0:cw], dram[:, c0:c0 + cw], waits=w)
            prev = act(dst_sb[0:rows, col0_dst + c0: col0_dst + c0 + cw],
                       wstage[0:rows, 0:cw], AF.Identity, scale=scale, waits=[e_l])
        return prev

    prev = None
    for hc in range(HC):
        prev = load_cast(inwT, inwT_d[hc * PD:(hc + 1) * PD, :], PD,
                         2 * INNER, hc * 2 * INNER, prev)
    for g in range(G):
        prev = load_cast(outwT, outwT_d[g * PD:(g + 1) * PD, :], PD, H,
                         g * H, prev)
        prev = load_cast(xpwT, xpwT_d[g * PD:(g + 1) * PD, :], PD, 96,
                         g * 96, prev)
    prev = load_cast(dtpwT, dtpwT_d, RDT, G * PD, 0, prev)
    P.barrier()

    ps_tr = ps_bb2.bitcast(BF16)

    def emit_chunk(c, is_own):
        for ti in range(TS):
            row0 = c * T + ti * PD
            htile = hidt[ti % 2][:, :]
            e_h = ldma(htile, hid_d[row0:row0 + PD, :],
                       waits=[P.now("scalar"), P.now("vector"),
                              P.now("tensor")], stream="ld_h%d" % (ti % 2))
            sumc, sqc = tcol[:, 1:2], tcol[:, 2:3]
            e_s = vred(sumc, htile, waits=[e_h, P.now("vector")])
            e_q = act(ps_op[:, 0:H], htile, AF.Square, accum_out=sqc,
                      waits=[e_h, P.now("scalar"), P.now("tensor")])
            mu = m1c[:, ti:ti + 1]
            e_mu = ts(mu, sumc, 1.0 / H, ALU.mult, waits=[e_s])
            e_e2 = ts(tcol[:, 4:5], sqc, 1.0 / H, ALU.mult, waits=[e_q])
            e_m2 = tt(tcol[:, 5:6], mu, mu, ALU.mult, waits=[e_mu])
            e_v = tt(tcol[:, 6:7], tcol[:, 4:5], tcol[:, 5:6], ALU.subtract,
                     waits=[e_e2, e_m2])
            e_ln = act(tcol[:, 7:8], tcol[:, 6:7], AF.Ln, bias=EPS, waits=[e_v])
            r1 = r1c[:, ti:ti + 1]
            e_r1 = act(r1, tcol[:, 7:8], AF.Exp, scale=-0.5, waits=[e_ln])
            e_r1sq = tt(tcol[:, 5:6], r1, r1, ALU.mult, waits=[e_r1])
            e_v2 = tt(tcol[:, 13:14], tcol[:, 6:7], tcol[:, 5:6], ALU.mult,
                      waits=[e_r1sq, e_v])
            e_l2 = act(tcol[:, 14:15], tcol[:, 13:14], AF.Ln, bias=EPS,
                       waits=[e_v2, e_r1])
            e_r2 = act(tcol[:, 9:10], tcol[:, 14:15], AF.Exp, scale=-0.5,
                       waits=[e_l2])
            e_sc = tt(tcol[:, 10:11], r1, tcol[:, 9:10], ALU.mult, waits=[e_r2])
            e_mn = ts(tcol[:, 11:12], mu, -1.0, ALU.mult, waits=[e_mu])
            e_bc = tt(tcol[:, 12:13], tcol[:, 11:12], tcol[:, 10:11], ALU.mult,
                      waits=[e_sc, e_mn])
            e_u = act(xnb[:, :], htile, AF.Identity, scale=tcol[:, 10:11],
                      bias=tcol[:, 12:13], waits=[e_bc, P.now("tensor")])
            for h2 in range(HC):
                psv = ps_tr[:, (h2 % 2) * PD:(h2 % 2 + 1) * PD]
                e_tr = transpose(psv, xnb[:, h2 * PD:(h2 + 1) * PD],
                                 waits=[e_u, P.now("scalar")])
                act(hnT[:, h2 * T + ti * PD: h2 * T + (ti + 1) * PD],
                    psv, AF.Identity, scale=lnw[:, h2:h2 + 1],
                    bias=lnb[:, h2:h2 + 1], waits=[e_tr])
        P.barrier()

        for g in range(G):
            xgc = xg[g % 2]
            pp = ps_a if g % 2 == 0 else ps_b
            last = None
            for h2 in range(HC):
                last = mm(pp[:, :], inwT[:, h2 * 2 * INNER + g * PD:
                                         h2 * 2 * INNER + (g + 1) * PD],
                          hnT[:, h2 * T:(h2 + 1) * T],
                          start=(h2 == 0), stop=(h2 == HC - 1),
                          waits=([P.now("scalar")] if h2 == 0 else []),
                          inc=(1 if h2 == HC - 1 else 0))
            e_x = act(xgc[:, 3:3 + T], pp[:, :], AF.Identity,
                      bias=inb[:, g:g + 1], waits=[last, P.now("vector")])
            e_hl = vcopy(xgc[:, 0:3], halo[:, g * 3:(g + 1) * 3],
                         waits=[e_x, P.now("vector")])
            zbc = zb[g % 2][:, 0:T]
            e_ck = ts(zbc, xgc[:, 0:T], cvw[:, g * KCV:g * KCV + 1],
                      ALU.mult, waits=[e_x, e_hl, P.now("scalar")])
            for k in range(1, KCV):
                e_ck = stt(zbc, xgc[:, k:k + T],
                           cvw[:, g * KCV + k:g * KCV + k + 1], zbc,
                           ALU.mult, ALU.add, waits=[e_ck])
            vcopy(halo[:, g * 3:(g + 1) * 3], xgc[:, T:T + 3], waits=[e_x])
            act(xcb[:, g * T:(g + 1) * T], zbc, AF.Silu,
                bias=cvb[:, g:g + 1], waits=[e_ck])
        if is_own:
            for g in range(G):
                pp = ps_a if g % 2 == 0 else ps_b
                last = None
                for h2 in range(HC):
                    last = mm(pp[:, :],
                              inwT[:, h2 * 2 * INNER + INNER + g * PD:
                                   h2 * 2 * INNER + INNER + (g + 1) * PD],
                              hnT[:, h2 * T:(h2 + 1) * T],
                              start=(h2 == 0), stop=(h2 == HC - 1),
                              waits=([P.now("scalar")] if h2 == 0 else []),
                              inc=(1 if h2 == HC - 1 else 0))
                act(yg[:, g * T:(g + 1) * T], pp[:, :], AF.Silu,
                    bias=inb[:, G + g:G + g + 1], waits=[last])
        P.barrier()

        w0 = [P.now("scalar")]
        for g in range(G):
            e_g1 = mm(ps_xp[0:16, :], xpwT[:, g * 96 + 64:g * 96 + 80],
                      xcb[:, g * T:(g + 1) * T], start=(g == 0),
                      stop=(g == G - 1), waits=(w0 if g == 0 else []),
                      inc=(1 if g == G - 1 else 0))
        e_bb = act(brow_b[:, :], ps_xp[0:16, :], AF.Copy, waits=[e_g1])
        e_cb = None
        if is_own:
            for g in range(G):
                e_g2 = mm(ps_xp[32:48, :], xpwT[:, g * 96 + 80:g * 96 + 96],
                          xcb[:, g * T:(g + 1) * T], start=(g == 0),
                          stop=(g == G - 1),
                          waits=([e_bb] if g == 0 else []),
                          inc=(1 if g == G - 1 else 0))
            e_cb = act(crow_b[:, :], ps_xp[32:48, :], AF.Copy, waits=[e_g2])
        for g in range(G):
            e_g3 = mm(ps_xp[64:128, :], xpwT[:, g * 96:g * 96 + 64],
                      xcb[:, g * T:(g + 1) * T], start=(g == 0),
                      stop=(g == G - 1),
                      waits=([e_cb if e_cb else e_bb] if g == 0 else []),
                      inc=(1 if g == G - 1 else 0))
        e_db = act(dtr_b[:, :], ps_xp[64:128, :], AF.Copy, waits=[e_g3])
        P.barrier()

        for sg in range(NSG):
            g0 = sg * SGN
            for i in range(SGN):
                g = g0 + i
                pp = ps_a if i % 2 == 0 else ps_b
                e_m = mm(pp[:, :], dtpwT[:, g * PD:(g + 1) * PD],
                         dtr_b[:, :], True, True,
                         waits=[e_db, P.now("scalar")])
                e_z = act(zb[0][:, 0:T], pp[:, :], AF.Identity,
                          bias=dtb[:, g:g + 1],
                          waits=[e_m, P.now("vector")])
                o0 = i * T
                e_ab = act(dtx_sg[:, o0:o0 + T], zb[0][:, 0:T], AF.Abs,
                           waits=[e_z, P.now("vector")])
                e_ex = act(dtx_sg[:, o0:o0 + T], dtx_sg[:, o0:o0 + T],
                           AF.Exp, scale=-1.0, waits=[e_ab])
                e_ln1 = act(dtx_sg[:, o0:o0 + T], dtx_sg[:, o0:o0 + T],
                            AF.Ln, bias=1.0, waits=[e_ex])
                e_rl = act(dt_sg[:, o0:o0 + T], zb[0][:, 0:T], AF.Relu,
                           waits=[e_ln1])
                tt(dt_sg[:, o0:o0 + T], dt_sg[:, o0:o0 + T],
                   dtx_sg[:, o0:o0 + T], ALU.add, waits=[e_rl])
            Wf = SGN * T
            e_dt = P.now("vector")
            e_dtx = tt(dtx_sg[:, 0:Wf], dt_sg[:, 0:Wf],
                       xcb[:, g0 * T:(g0 + SGN) * T], ALU.mult,
                       waits=[e_dt, P.now("scalar")])
            e_y0 = None
            if is_own:
                e_y0 = P.op("gpsimd",
                            lambda e, b=yac: e.memset(b[:, :], 0.0),
                            waits=[P.now("vector")], inc=1)

            dec_user = [None, None]
            bb_user = [None, None]
            bbe_user = [None, None]
            cc_user = None
            cb_user = [None, None]
            s_user = [None, None]
            for n in range(NST):
                kb = n % 2
                pbb = ps_bb if kb == 0 else ps_bb2
                e_bm = mm(pbb[:, :], sel16[:, n * PD:(n + 1) * PD],
                          brow_b[:, :], True, True,
                          waits=[e_bb] + ([bbe_user[kb]] if bbe_user[kb] else
                                          [P.now("vector"),
                                           P.now("scalar")]))
                e_be = act(bbr[kb][:, :], pbb[:, :], AF.Copy,
                           waits=[e_bm] + ([bb_user[kb]] if bb_user[kb]
                                           else []))
                bbe_user[kb] = e_be
                e_cv = None
                if is_own:
                    e_cm = mm(ps_xp[:, :], sel16[:, n * PD:(n + 1) * PD],
                              crow_b[:, :], True, True,
                              waits=[e_cb] + ([cc_user] if cc_user else
                                              [P.now("vector")]))
                    kc = n % 2
                    e_cv = act(cbcr[kc][:, :], ps_xp[:, :], AF.Copy,
                               waits=[e_cm] + ([cb_user[kc]] if cb_user[kc]
                                               else []))
                    cc_user = e_cv
                for i in range(SGN):
                    g = g0 + i
                    sl = i % 2
                    kd = (n * SGN + i) % 2
                    e_de = act(dec[kd][:, :], dt_sg[:, i * T:(i + 1) * T],
                               AF.Exp,
                               scale=acols[:, g * NST + n:g * NST + n + 1],
                               waits=([dec_user[kd]] if dec_user[kd]
                                      else [P.now("vector")]))
                    e_w = gtt(wti[kd][:, :], dtx_sg[:, i * T:(i + 1) * T],
                              bbr[kb][:, :], ALU.mult,
                              waits=[e_dtx, e_be] +
                              ([dec_user[kd]] if dec_user[kd] else []))
                    bb_user[kb] = e_w
                    e_sc = scan(srot[:, sl * T:(sl + 1) * T], dec[kd][:, :],
                                wti[kd][:, :],
                                states[:, g * NST + n:g * NST + n + 1],
                                waits=[e_de, e_w] +
                                ([s_user[sl]] if s_user[sl] else []))
                    dec_user[kd] = e_sc
                    s_user[sl] = e_sc
                    if is_own:
                        kc = n % 2
                        e_p = tt(prot[kd][:, :], srot[:, sl * T:(sl + 1) * T],
                                 cbcr[kc][:, :], ALU.mult,
                                 waits=[e_sc, e_cv])
                        cb_user[kc] = e_p
                        tt(yac[:, i * T:(i + 1) * T],
                           yac[:, i * T:(i + 1) * T],
                           prot[kd][:, :], ALU.add,
                           waits=[e_p] + ([e_y0] if n == 0 else []))
                        s_user[sl] = e_p
                    if sl == 1:
                        sv = srot.rearrange("p (i t) -> p i t", i=2)[
                            :, :, T - 1:T]
                        dv = states.rearrange("p (g n) -> p g n", g=G)[
                            :, g0 + i - 1:g0 + i + 1, n:n + 1]
                        e_xt = vcopy(dv, sv, waits=[])
                        s_user[0] = e_xt
                        s_user[1] = e_xt
            if is_own:
                for i in range(SGN):
                    g = g0 + i
                    e_yf = stt(yac[:, i * T:(i + 1) * T],
                               xcb[:, g * T:(g + 1) * T], dvec[:, g:g + 1],
                               yac[:, i * T:(i + 1) * T], ALU.mult, ALU.add,
                               waits=[P.now("vector")])
                    tt(yg[:, g * T:(g + 1) * T], yac[:, i * T:(i + 1) * T],
                       yg[:, g * T:(g + 1) * T], ALU.mult,
                       waits=[e_yf])
        P.barrier()

        if not is_own:
            return
        for t2 in range(TS):
            last = None
            for g in range(G):
                for hh in range(2):
                    last = mm(ps_op[:, hh * 512:(hh + 1) * 512],
                              yg[:, g * T + t2 * PD:g * T + (t2 + 1) * PD],
                              outwT[:, g * H + hh * 512:g * H + (hh + 1) * 512],
                              start=(g == 0), stop=(g == G - 1),
                              waits=([P.now("scalar"), P.now("vector")]
                                     if g == 0 else []),
                              inc=(1 if (g == G - 1 and hh == 1) else 0))
            row0 = c * T + t2 * PD
            htile = hidt[t2 % 2][:, :]
            e_h = ldma(htile, hid_d[row0:row0 + PD, :],
                       waits=[P.now("scalar"), P.now("vector")],
                       stream="ld_h%d" % (t2 % 2))
            mu, r1 = m1c[:, t2:t2 + 1], r1c[:, t2:t2 + 1]
            e_mb = tt(tcol[:, 0:1], mu, r1, ALU.mult, waits=[P.now("vector")])
            e_mb2 = ts(tcol[:, 0:1], tcol[:, 0:1], -1.0, ALU.mult,
                       waits=[e_mb])
            e_l1 = act(attn[:, :], htile, AF.Identity, scale=r1,
                       bias=tcol[:, 0:1], waits=[e_mb2, e_h, P.now("scalar")])
            e_a1 = tt(attn[:, :], attn[:, :], ps_op[:, :], ALU.add,
                      waits=[e_l1, last])
            e_a2 = tt(attn[:, :], attn[:, :], outb_t[:, :], ALU.add,
                      waits=[e_a1])
            e_n2 = layernorm(attn[:, :], attn[:, :], [e_a2])
            e_x1 = tt(x1t[:, :], htile, attn[:, :], ALU.add,
                      waits=[e_n2, P.now("st")])
            sdma(x1sp_d[row0:row0 + PD, :], x1t[:, :], waits=[e_x1])
            e_n3 = layernorm(x1t[:, :], xnb[:, :], [e_x1])
            for h2 in range(HC):
                psv = ps_tr[:, (h2 % 2) * PD:(h2 % 2 + 1) * PD]
                e_tr = transpose(psv, xnb[:, h2 * PD:(h2 + 1) * PD],
                                 waits=[e_n3, P.now("vector")])
                vcopy(xnTs[:, h2 * PD:(h2 + 1) * PD], psv,
                      waits=[e_tr] + ([P.now("st")] if h2 == 0 else []))
            sdma(xnTsp_d[:, row0:row0 + PD].rearrange(
                "(h p) q -> p h q", p=PD),
                xnTs.rearrange("p (h q) -> p h q", h=HC),
                waits=[P.now("vector")])

    for c in range(NCH):
        P.begin_branch(c, 0)
        emit_chunk(c, True)
        P.end_branch()
        P.begin_branch(c, 1)
        emit_chunk(c, False)
        P.end_branch()
        P.barrier()

    P.barrier()
    sweep_es.close()

    ffn_es = contextlib.ExitStack()
    FW = lambda name, shape, dt: ffn_es.enter_context(nc.sbuf_tensor(name, shape, dt))
    wg_sb = FW("wg_sb", [PD, HC * FF], BF16)
    wu_sb = FW("wu_sb", [PD, HC * FF], BF16)
    wd_sb = FW("wd_sb", [PD, FC * H], BF16)

    DQ = 1.0 / 127.0
    fst_es = contextlib.ExitStack()
    fstage = [fst_es.enter_context(nc.sbuf_tensor(f"fstage{i}", [PD, FF], I32))
              for i in range(4)]
    cast_ev = [None] * 4
    WLQ = ("ld_f", "ld_x", "ld_s", "ld_h0")

    def wcast(dst, src, cols, k):
        sl = fstage[k % 4]
        e_l = ldma(sl[:, 0:cols], src,
                   waits=([cast_ev[k % 4]] if cast_ev[k % 4] else []),
                   stream=WLQ[k % 4])
        cast_ev[k % 4] = act(dst, sl[:, 0:cols], AF.Identity, scale=DQ,
                             waits=[e_l])
        return cast_ev[k % 4]

    wk = 0
    for hc2 in range(HC):
        e_wg = wcast(wg_sb[:, hc2 * FF:(hc2 + 1) * FF],
                     wgT_d[hc2 * PD:(hc2 + 1) * PD, :], FF, wk)
        wk += 1
    for hc2 in range(HC):
        e_wu = wcast(wu_sb[:, hc2 * FF:(hc2 + 1) * FF],
                     wuT_d[hc2 * PD:(hc2 + 1) * PD, :], FF, wk)
        wk += 1
    for f in range(FC):
        e_wd = wcast(wd_sb[:, f * H:(f + 1) * H],
                     wdT_d[f * PD:(f + 1) * PD, :], H, wk)
        wk += 1
    fst_es.close()

    frhs2 = [FW(f"frhs{i}", [PD, HC * T2], BF16) for i in range(2)]
    hbuf = FW("hbuf", [PD, FC * T2], BF16)
    sgr2 = [FW(f"sgr{i}", [PD, T2], F32) for i in range(2)]
    f_x1t = FW("f_x1t", [PD, H], F32)
    f_sq = FW("f_sq", [PD, H], F32)
    f_ev = FW("f_ev", [PD, H], F32)

    last_silu = [None, None]
    last_tt = [None, None]
    frhs_user = [None, None]
    fx1_user = None
    evac_ev = [None, None]
    sd_ev = None
    down_prev = None
    eo_last = None
    n_half = L // T2
    for j in range(n_half):
        row0 = j * T2
        if "ffn" in GUARDS:
            P.begin_branch(j)
        fr = frhs2[j % 2]
        wf = [frhs_user[j % 2]] if frhs_user[j % 2] else []
        for hc2 in range(HC):
            ldma(fr[:, hc2 * T2:(hc2 + 1) * T2],
                 xnTsp_d[hc2 * PD:(hc2 + 1) * PD, row0:row0 + T2],
                 waits=(wf if hc2 == 0 else []), stream="sync")
        e_r = P.now("sync")
        tts = []
        for fo in range(FC):
            bp = fo % 2
            bankG = (ps_a, ps_xp)[bp]
            bankU = (ps_b, ps_bb)[bp]
            wg0 = [e_r, e_wg] if fo == 0 else []
            if last_silu[bp]:
                wg0 = wg0 + [last_silu[bp]]
            lg = None
            for hc2 in range(HC):
                lg = mm(bankG[:, 0:T2],
                        wg_sb[:, hc2 * FF + fo * PD:hc2 * FF + (fo + 1) * PD],
                        fr[:, hc2 * T2:(hc2 + 1) * T2],
                        start=(hc2 == 0), stop=(hc2 == HC - 1),
                        waits=(wg0 if hc2 == 0 else []),
                        inc=(1 if hc2 == HC - 1 else 0))
            wu0 = [e_wu] if fo == 0 else []
            if last_tt[bp]:
                wu0 = wu0 + [last_tt[bp]]
            lu = None
            for hc2 in range(HC):
                lu = mm(bankU[:, 0:T2],
                        wu_sb[:, hc2 * FF + fo * PD:hc2 * FF + (fo + 1) * PD],
                        fr[:, hc2 * T2:(hc2 + 1) * T2],
                        start=(hc2 == 0), stop=(hc2 == HC - 1),
                        waits=(wu0 if hc2 == 0 else []),
                        inc=(1 if hc2 == HC - 1 else 0))
            frhs_user[j % 2] = lu
            e_sg = act(sgr2[bp][:, :], bankG[:, 0:T2], AF.Silu,
                       waits=[lg] + ([last_tt[bp]] if last_tt[bp] else []))
            last_silu[bp] = e_sg
            wt = [e_sg, lu]
            if fo == 0 and down_prev:
                wt.append(down_prev)
            last_tt[bp] = tt(hbuf[:, fo * T2:(fo + 1) * T2], sgr2[bp][:, :],
                             bankU[:, 0:T2], ALU.mult, waits=wt)
            tts.append(last_tt[bp])
        for i in range(TS2):
            sl = i % 2
            e_x1 = ldma(f_x1t[:, :],
                        x1sp_d[row0 + i * PD:row0 + (i + 1) * PD, :],
                        waits=([fx1_user] if fx1_user else []),
                        stream="ld_h1")
            halves = ((ps_op[:, 0:512], ps_op[:, 512:1024]) if sl == 0
                      else (ps_bb2[:, 0:512], ps_f[:, 0:512]))
            m_hh = [None, None]
            for fo in range(FC):
                for hh in range(2):
                    w = []
                    if fo == 0 and hh == 0:
                        w = [e_wd] + ([evac_ev[sl]] if evac_ev[sl] else [])
                    if hh == 0 and i == 0:
                        w = w + [tts[fo]]
                    m_hh[hh] = mm(halves[hh],
                                  hbuf[:, fo * T2 + i * PD:fo * T2 + (i + 1) * PD],
                                  wd_sb[:, fo * H + hh * 512:fo * H + (hh + 1) * 512],
                                  start=(fo == 0), stop=(fo == FC - 1),
                                  waits=w,
                                  inc=(1 if fo == FC - 1 else 0))
            b = 1 + 12 * sl
            e_ev1 = act(f_ev[:, 0:512], halves[0], AF.Copy,
                        waits=[m_hh[0]] + ([eo_last] if eo_last else []))
            e_ev2 = act(f_ev[:, 512:1024], halves[1], AF.Copy, waits=[m_hh[1]])
            evac_ev[sl] = e_ev2
            e_s = vred(tcol[:, b:b + 1], f_ev[:, :], waits=[e_ev2])
            e_q = act(f_sq[:, :], f_ev[:, :], AF.Square,
                      accum_out=tcol[:, b + 1:b + 2],
                      waits=([sd_ev] if sd_ev else []))
            e_mu = ts(tcol[:, b + 2:b + 3], tcol[:, b:b + 1], 1.0 / H,
                      ALU.mult, waits=[e_s])
            e_e2 = ts(tcol[:, b + 3:b + 4], tcol[:, b + 1:b + 2], 1.0 / H,
                      ALU.mult, waits=[e_q])
            e_m2 = tt(tcol[:, b + 4:b + 5], tcol[:, b + 2:b + 3],
                      tcol[:, b + 2:b + 3], ALU.mult, waits=[e_mu])
            e_v = tt(tcol[:, b + 5:b + 6], tcol[:, b + 3:b + 4],
                     tcol[:, b + 4:b + 5], ALU.subtract, waits=[e_e2, e_m2])
            e_ln = act(tcol[:, b + 6:b + 7], tcol[:, b + 5:b + 6], AF.Ln,
                       bias=EPS, waits=[e_v])
            e_rs = act(tcol[:, b + 7:b + 8], tcol[:, b + 6:b + 7], AF.Exp,
                       scale=-0.5, waits=[e_ln])
            e_mn = ts(tcol[:, b + 8:b + 9], tcol[:, b + 2:b + 3], -1.0,
                      ALU.mult, waits=[e_mu])
            e_bc = tt(tcol[:, b + 9:b + 10], tcol[:, b + 8:b + 9],
                      tcol[:, b + 7:b + 8], ALU.mult, waits=[e_rs, e_mn])
            e_n4 = act(f_ev[:, :], f_ev[:, :], AF.Identity,
                       scale=tcol[:, b + 7:b + 8], bias=tcol[:, b + 9:b + 10],
                       waits=[e_bc])
            e_o = tt(f_sq[:, :], f_x1t[:, :], f_ev[:, :], ALU.add,
                     waits=[e_n4, e_x1])
            sd_ev = sdma(out_d[row0 + i * PD:row0 + (i + 1) * PD, :],
                         f_sq[:, :], waits=[e_o])
            fx1_user = e_o
            eo_last = e_o
            if i == TS2 - 1:
                down_prev = m_hh[1]
        if "ffn" in GUARDS:
            P.end_branch()

    P.op("gpsimd", None, waits=[P.now("st")], inc=0)
    P.barrier()
    ffn_es.close()

    with es, nc.Block() as block:
        P.emit(block)
    return nc


def _prep_weights(inp):
    g = lambda k: np.ascontiguousarray(inp[k])
    w = {}
    w["inwT_dr"] = np.ascontiguousarray(g("L0_attn_in_w").T)
    w["outwT_dr"] = np.ascontiguousarray(g("L0_attn_out_w").T)
    w["xpwT_dr"] = np.ascontiguousarray(g("L0_attn_xp_w").T)
    w["dtpwT_dr"] = np.ascontiguousarray(g("L0_attn_dtp_w").T)
    w["lnw_dr"] = np.ascontiguousarray(g("L0_attn_ln_w").reshape(HC, PD).T)
    w["lnb_dr"] = np.ascontiguousarray(g("L0_attn_ln_b").reshape(HC, PD).T)
    w["inb_dr"] = np.ascontiguousarray(g("L0_attn_in_b").reshape(2 * G, PD).T)
    w["cvw_dr"] = np.ascontiguousarray(
        g("L0_attn_conv_w").reshape(G, PD, KCV).transpose(1, 0, 2).reshape(PD, G * KCV))
    w["cvb_dr"] = np.ascontiguousarray(g("L0_attn_conv_b").reshape(G, PD).T)
    w["dtb_dr"] = np.ascontiguousarray(g("L0_attn_dtp_b").reshape(G, PD).T)
    w["alog_dr"] = np.ascontiguousarray(
        g("L0_attn_A_log").reshape(G, PD, NST).transpose(1, 0, 2).reshape(PD, G * NST))
    w["dvec_dr"] = np.ascontiguousarray(g("L0_attn_D").reshape(G, PD).T)
    w["outb_dr"] = np.ascontiguousarray(g("L0_attn_out_b").reshape(1, H))
    w["wgT_dr"] = np.ascontiguousarray(g("L0_ffn_w_gate").T)
    w["wuT_dr"] = np.ascontiguousarray(g("L0_ffn_w_up").T)
    w["wdT_dr"] = np.ascontiguousarray(g("L0_ffn_w_down").T)
    return w


def kernel(**inputs):
    L = inputs["L0_attn_hidden"].shape[1]
    NCH = L // 512
    w = _prep_weights(inputs)
    nc = build(L)
    bounds = [0, NCH // 4, NCH // 2, 3 * NCH // 4, NCH]
    in_maps = []
    for core in range(8):
        b, q = core // 4, core % 4
        own = 0
        for c in range(bounds[q], bounds[q + 1]):
            own |= 1 << c
        scanonly = 0
        for c in range(0, bounds[q]):
            scanonly |= 1 << c
        m = dict(w)
        m["hidden"] = np.ascontiguousarray(
            np.asarray(inputs["L0_attn_hidden"][b], dtype=np.float32))
        m["cfg_dr"] = np.array([[own, scanonly, 0, 0]],
                               dtype=np.uint32).view(np.int32)
        in_maps.append(m)
    res = run_bass_kernel_spmd(nc, in_maps, core_ids=list(range(8)))
    out = np.empty((2, L, H), np.float32)
    for core in range(8):
        b, q = core // 4, core % 4
        r0, r1 = bounds[q] * 512, bounds[q + 1] * 512
        out[b, r0:r1] = res.results[core]["out"][r0:r1]
    return out


# version 11
# speedup vs baseline: 2.0908x; 1.1083x over previous
import bisect
import contextlib
import numpy as np

import concourse.bass as bass
import concourse.mybir as mybir
from concourse.bass_utils import run_bass_kernel_spmd

F32 = mybir.dt.float32
BF16 = mybir.dt.bfloat16
I32 = mybir.dt.int32
AF = mybir.ActivationFunctionType
ALU = mybir.AluOpType
AX = mybir.AxisListType

H = 1024
INNER = 2048
NST = 16
KCV = 4
RDT = 64
FF = 3072
PD = 128
G = INNER // PD
HC = H // PD
FC = FF // PD
EPS = 1e-5

STREAMS = ("sync", "tensor", "vector", "scalar", "gpsimd", "st",
           "ld_h0", "ld_h1", "ld_s", "ld_f", "ld_x")
ENG_OF = {"sync": "sync", "tensor": "tensor", "vector": "vector",
          "scalar": "scalar", "gpsimd": "gpsimd", "st": "sync",
          "ld_h0": "sync", "ld_h1": "sync", "ld_s": "sync",
          "ld_f": "sync", "ld_x": "sync"}


ENGS5 = ("sync", "tensor", "vector", "scalar", "gpsimd")

GUARDS = {"gate", "D", "E", "ffn"}


class Plan:

    def __init__(self, nc, sems, cfg_ap=None, cfg_ready=None):
        self.nc = nc
        self.sems = sems
        self.cnt = {s: 0 for s in STREAMS}
        self.ops = {e: [] for e in ENGS5}
        self.cfg_ap = cfg_ap
        self.cfg_ready = cfg_ready
        self._br = None
        self.refs = {s: set() for s in STREAMS}

    def _record(self, waits):
        for (ws, wv) in waits:
            if self._br is not None and wv > self._br[1].get(ws, 0):
                continue
            self.refs[ws].add(wv)

    def op(self, stream, fn, waits=(), inc=None):
        if inc is None:
            inc = 16 if (stream.startswith("ld_") or stream in ("sync", "st")) else 1
        eng = ENG_OF[stream]
        self._record(waits)
        self.ops[eng].append(("op", fn, tuple(waits), inc, stream))
        self.cnt[stream] += inc
        return (stream, self.cnt[stream])

    def now(self, stream):
        return (stream, self.cnt[stream])

    def barrier(self):
        marks = [(s, self.cnt[s]) for s in STREAMS if self.cnt[s] > 0]
        self._record(marks)
        for e in self.ops:
            w = [m for m in marks if ENG_OF[m[0]] != e]
            if w:
                self.ops[e].append(("op", None, tuple(w), 0, e))

    def begin_branch(self, bit, mask_idx=0):
        assert self._br is None
        self._br = [bit, dict(self.cnt), None]
        for e in ENGS5:
            self.ops[e].append(("if", bit, mask_idx))

    def else_branch(self):
        bit, snap, _ = self._br
        self._br[2] = dict(self.cnt)
        self.cnt = dict(snap)
        for e in ENGS5:
            self.ops[e].append(("else",))

    def end_branch(self):
        bit, snap, armA = self._br
        self._br = None
        if armA is None:
            armA = dict(self.cnt)
            self.cnt = dict(snap)
            for e in ENGS5:
                self.ops[e].append(("else",))
        armB = dict(self.cnt)
        final = {s: max(armA[s], armB[s]) for s in STREAMS}
        padA = {s: (armA[s], final[s] - armA[s])
                for s in STREAMS if final[s] > armA[s]}
        padB = {s: (armB[s], final[s] - armB[s])
                for s in STREAMS if final[s] > armB[s]}
        self.cnt = final
        for e in ENGS5:
            for k in range(len(self.ops[e]) - 1, -1, -1):
                if self.ops[e][k] == ("else",):
                    self.ops[e][k] = ("else", padA)
                    break
            self.ops[e].append(("endif", padB))

    def _prune_empty_regions(self):
        for e, ops in self.ops.items():
            out = []
            i = 0
            while i < len(ops):
                entry = ops[i]
                if entry[0] != "if":
                    out.append(entry)
                    i += 1
                    continue
                depth = 0
                j = i
                k_else = None
                while True:
                    kind = ops[j][0]
                    if kind == "if":
                        depth += 1
                    elif kind == "endif":
                        depth -= 1
                        if depth == 0:
                            break
                    elif kind == "else" and depth == 1:
                        k_else = j
                    j += 1
                padA = ops[k_else][1]
                padB = ops[j][1]
                body_ops = [ops[m] for m in range(i + 1, j)
                            if ops[m][0] == "op"]
                my_padA = any(ENG_OF[s] == e for s in padA)
                my_padB = any(ENG_OF[s] == e for s in padB)
                inner_marks = any(ops[m][0] in ("if", "else", "endif")
                                  for m in range(i + 1, j)
                                  if m != k_else)
                if not body_ops and not my_padA and not my_padB \
                        and not inner_marks:
                    i = j + 1
                else:
                    out.extend(ops[i:j + 1])
                    i = j + 1
            self.ops[e] = out

    def emit(self, block):
        self._prune_empty_regions()
        if self.cfg_ready:
            self.refs[self.cfg_ready[0]].add(self.cfg_ready[1])
        refs_sorted = {s: sorted(self.refs[s]) for s in STREAMS}

        def make(eng_name):
            def body(eng):
                has_if = any(entry[0] == "if" for entry in self.ops[eng_name])

                def inner(rmask, rb):
                    satisfied = {}
                    sat_stack = []
                    guard_stack = []
                    loaded = [False]

                    def ensure_mask():
                        if loaded[0]:
                            return
                        loaded[0] = True
                        ws, wv = self.cfg_ready
                        if satisfied.get(ws, 0) < wv:
                            eng.wait_ge(self.sems[ws], wv)
                            satisfied[ws] = wv
                        eng.reg_load(rmask[0], self.cfg_ap[0])
                        eng.reg_load(rmask[1], self.cfg_ap[1])

                    def do_waits(waits):
                        for (ws, wv) in waits:
                            if satisfied.get(ws, 0) >= wv:
                                continue
                            eng.wait_ge(self.sems[ws], wv)
                            satisfied[ws] = wv

                    def do_pads(pad):
                        for s, (base, k) in pad.items():
                            if ENG_OF[s] != eng_name or k <= 0:
                                continue
                            tgt = base + k
                            cur = base
                            rs = refs_sorted[s]
                            lo = bisect.bisect_right(rs, base)
                            while lo < len(rs) and rs[lo] < tgt:
                                v = rs[lo]
                                eng.sem_inc(self.sems[s], v - cur)
                                cur = v
                                lo += 1
                            eng.sem_inc(self.sems[s], tgt - cur)

                    for entry in self.ops[eng_name]:
                        kind = entry[0]
                        if kind == "op":
                            _, fn, waits, inc, stream = entry
                            do_waits(waits)
                            if fn is not None:
                                instr = fn(eng)
                                if inc:
                                    instr.then_inc(self.sems[stream], inc)
                        elif kind == "if":
                            ensure_mask()
                            bit, mi = entry[1], entry[2]
                            eng.reg_alu(rb, rmask[mi], bit,
                                        ALU.logical_shift_right)
                            eng.reg_alu(rb, rb, 1, ALU.bitwise_and)
                            gd = eng.If_ne(rb, 0)
                            gd.__enter__()
                            guard_stack.append(gd)
                            sat_stack.append(dict(satisfied))
                        elif kind == "else":
                            padA = entry[1]
                            do_pads(padA)
                            snap = sat_stack[-1]
                            sat_stack[-1] = dict(satisfied)
                            satisfied.clear()
                            satisfied.update(snap)
                            guard_stack[-1].__exit__(None, None, None)
                            gd = eng.Else()
                            gd.__enter__()
                            guard_stack[-1] = gd
                        elif kind == "endif":
                            padB = entry[1]
                            do_pads(padB)
                            guard_stack.pop().__exit__(None, None, None)
                            armA_sat = sat_stack.pop()
                            merged = {}
                            for k in set(armA_sat) | set(satisfied):
                                merged[k] = min(armA_sat.get(k, 0),
                                                satisfied.get(k, 0))
                            satisfied.clear()
                            satisfied.update(merged)

                if has_if:
                    with eng.register("r_mask0_" + eng_name) as rm0, \
                         eng.register("r_mask1_" + eng_name) as rm1, \
                         eng.register("r_bit_" + eng_name) as rb:
                        inner((rm0, rm1), rb)
                else:
                    inner(None, None)

            return body

        for e in ENGS5:
            if self.ops[e]:
                getattr(block, e)(make(e))


def build(L, T=512):
    NCH = L // T
    TS = T // PD
    SGN = 2
    NSG = G // SGN
    T2 = 512
    TS2 = T2 // PD

    nc = bass.Bass("TRN2", target_bir_lowering=False, debug=False)

    hid_d = nc.dram_tensor("hidden", [L, H], F32, kind="ExternalInput").ap()
    inwT_d = nc.dram_tensor("inwT_dr", [H, 2 * INNER], F32, kind="ExternalInput").ap()
    outwT_d = nc.dram_tensor("outwT_dr", [INNER, H], F32, kind="ExternalInput").ap()
    xpwT_d = nc.dram_tensor("xpwT_dr", [INNER, 96], F32, kind="ExternalInput").ap()
    dtpwT_d = nc.dram_tensor("dtpwT_dr", [RDT, INNER], F32, kind="ExternalInput").ap()
    lnw_d = nc.dram_tensor("lnw_dr", [PD, HC], F32, kind="ExternalInput").ap()
    lnb_d = nc.dram_tensor("lnb_dr", [PD, HC], F32, kind="ExternalInput").ap()
    inb_d = nc.dram_tensor("inb_dr", [PD, 2 * G], F32, kind="ExternalInput").ap()
    cvw_d = nc.dram_tensor("cvw_dr", [PD, G * KCV], F32, kind="ExternalInput").ap()
    cvb_d = nc.dram_tensor("cvb_dr", [PD, G], F32, kind="ExternalInput").ap()
    dtb_d = nc.dram_tensor("dtb_dr", [PD, G], F32, kind="ExternalInput").ap()
    alog_d = nc.dram_tensor("alog_dr", [PD, G * NST], F32, kind="ExternalInput").ap()
    dvec_d = nc.dram_tensor("dvec_dr", [PD, G], F32, kind="ExternalInput").ap()
    outb_d = nc.dram_tensor("outb_dr", [1, H], F32, kind="ExternalInput").ap()
    wgT_d = nc.dram_tensor("wgT_dr", [H, FF], I32, kind="ExternalInput").ap()
    wuT_d = nc.dram_tensor("wuT_dr", [H, FF], I32, kind="ExternalInput").ap()
    wdT_d = nc.dram_tensor("wdT_dr", [FF, H], I32, kind="ExternalInput").ap()

    cfg_d = nc.dram_tensor("cfg_dr", [1, 4], I32, kind="ExternalInput").ap()
    out_d = nc.dram_tensor("out", [L, H], F32, kind="ExternalOutput").ap()
    x1sp_d = nc.dram_tensor("x1sp", [L, H], F32).ap()
    xnTsp_d = nc.dram_tensor("xnTsp", [H, L], BF16).ap()

    es = contextlib.ExitStack()
    SB = lambda name, shape, dt: es.enter_context(nc.sbuf_tensor(name, shape, dt))
    PS = lambda name, shape, dt: es.enter_context(nc.psum_tensor(name, shape, dt))

    ps_a = PS("ps_a", [PD, 512], F32)
    ps_b = PS("ps_b", [PD, 512], F32)
    ps_xp = PS("ps_xp", [PD, 512], F32)
    ps_bb = PS("ps_bb", [PD, 512], F32)
    ps_bb2 = PS("ps_bb2", [PD, 512], F32)
    ps_op = PS("ps_op", [PD, 1024], F32)
    ps_f = PS("ps_f", [PD, 512], F32)
    ps_trv = None

    sems = {}
    for s in STREAMS:
        sems[s] = es.enter_context(nc.semaphore("sem_" + s))
    P = Plan(nc, sems)
    cfg_s = SB("cfg_s", [1, 4], I32)

    tcol = SB("tcol", [PD, 32], F32)
    consts = SB("consts", [PD, 8], F32)

    sweep_es = contextlib.ExitStack()
    SW = lambda name, shape, dt: sweep_es.enter_context(nc.sbuf_tensor(name, shape, dt))
    ident_b = SW("ident_b", [PD, PD], BF16)
    acols = SW("acols", [PD, G * NST], F32)
    cvw = SW("cvw", [PD, G * KCV], F32)
    cvb = SW("cvb", [PD, G], F32)
    dtb = SW("dtb", [PD, G], F32)
    dvec = SW("dvec", [PD, G], F32)
    inb = SW("inb", [PD, 2 * G], F32)
    lnw = SW("lnw", [PD, HC], F32)
    lnb = SW("lnb", [PD, HC], F32)
    outb_t = SW("outb_t", [PD, H], F32)
    states = SW("states", [PD, G * NST], F32)
    onesb = SW("onesb", [1, PD], BF16)
    sel16 = SW("sel16", [16, NST * PD], BF16)
    brow_b = SW("brow_b", [16, T], BF16)
    crow_b = SW("crow_b", [16, T], BF16)
    dtr_b = SW("dtr_b", [RDT, T], BF16)
    inwT = SW("inwT", [PD, HC * 2 * INNER], BF16)
    outwT = SW("outwT", [PD, G * H], BF16)
    xpwT = SW("xpwT", [PD, G * 96], BF16)
    dtpwT = SW("dtpwT", [RDT, G * PD], BF16)
    hidt = [SW(f"hidt{i}", [PD, H], F32) for i in range(2)]
    hnT = SW("hnT", [PD, HC * T], BF16)
    xcb = SW("xcb", [PD, G * T], BF16)
    yg = SW("yg", [PD, G * T], BF16)
    halo = SW("halo", [PD, G * 3], F32)
    dt_sg = SW("dt_sg", [PD, SGN * T], F32)
    dtx_sg = SW("dtx_sg", [PD, SGN * T], F32)
    r1c = SW("r1c", [PD, TS], F32)
    m1c = SW("m1c", [PD, TS], F32)
    xnb = SW("xnb", [PD, H], BF16)
    pro_es = contextlib.ExitStack()
    wstage = pro_es.enter_context(nc.sbuf_tensor("wstage", [PD, 2048], F32))
    obrow = pro_es.enter_context(nc.sbuf_tensor("obrow", [1, H], F32))
    obrow_b = pro_es.enter_context(nc.sbuf_tensor("obrow_b", [1, H], BF16))
    pro_es.close()
    d_es = contextlib.ExitStack()
    DW = lambda name, shape, dt: d_es.enter_context(nc.sbuf_tensor(name, shape, dt))
    xg = [DW(f"xg{i}", [PD, T + 3], F32) for i in range(2)]
    zb = [DW(f"zb{i}", [PD, T], F32) for i in range(2)]
    dec = [DW(f"dec{i}", [PD, T], F32) for i in range(2)]
    wti = [DW(f"wti{i}", [PD, T], F32) for i in range(2)]
    srot = DW("srot", [PD, 2 * T], BF16)
    prot = [DW(f"prot{i}", [PD, T], BF16) for i in range(4)]
    cbcr = [DW(f"cbcr{i}", [PD, T], BF16) for i in range(2)]
    bbr = [DW(f"bbr{i}", [PD, T], F32) for i in range(2)]
    yac = DW("yac", [PD, SGN * T], F32)
    d_es.close()
    attn = SW("attn", [PD, H], F32)
    x1t = SW("x1t", [PD, H], F32)
    xnTs = SW("xnTs", [PD, HC * PD], BF16)

    def ldma(dst, src, waits=(), stream="sync"):
        return P.op(stream, lambda e, d=dst, s=src: e.dma_start(out=d, in_=s),
                    waits=waits)

    def sdma(dst, src, waits=()):
        return P.op("st", lambda e, d=dst, s=src: e.dma_start(out=d, in_=s),
                    waits=waits)

    def act(out, in_, func, bias=None, scale=None, accum_out=None, waits=()):
        kw = {}
        if bias is not None:
            kw["bias"] = bias
        if scale is not None:
            kw["scale"] = scale
        if accum_out is not None:
            kw["accum_out"] = accum_out
        return P.op("scalar",
                    lambda e, o=out, i=in_, f=func, kw=kw: e.activation(o, i, f, **kw),
                    waits=waits)

    def tt(out, a, b, op=ALU.mult, waits=()):
        return P.op("vector",
                    lambda e, o=out, x=a, y=b, p=op: e.tensor_tensor(o, x, y, op=p),
                    waits=waits)

    def gtt(out, a, b, op=ALU.mult, waits=()):
        return P.op("gpsimd",
                    lambda e, o=out, x=a, y=b, p=op: e.tensor_tensor(o, x, y, op=p),
                    waits=waits)

    def ts(out, a, s1, op0, waits=()):
        return P.op("vector",
                    lambda e, o=out, x=a, s=s1, p=op0:
                    e.tensor_scalar(o, x, s, None, p),
                    waits=waits)

    def stt(out, in0, scalar, in1, op0, op1, waits=()):
        return P.op("vector",
                    lambda e, o=out, a=in0, s=scalar, b=in1, p0=op0, p1=op1:
                    e.scalar_tensor_tensor(o, a, s, b, op0=p0, op1=p1),
                    waits=waits)

    def vred(out, in_, op=ALU.add, axis=AX.X, waits=()):
        return P.op("vector",
                    lambda e, o=out, i=in_, p=op, ax=axis:
                    e.tensor_reduce(o, i, axis=ax, op=p),
                    waits=waits)

    def vcopy(out, in_, waits=()):
        return P.op("vector",
                    lambda e, o=out, i=in_: e.tensor_copy(o, i), waits=waits)

    def scan(out, d0, d1, init, waits=()):
        return P.op("vector",
                    lambda e, o=out, a=d0, b=d1, i=init:
                    e.tensor_tensor_scan(o, a, b, i, op0=ALU.mult, op1=ALU.add),
                    waits=waits)

    def mm(psum, lhsT, rhs, start, stop, waits=(), inc=1):
        return P.op("tensor",
                    lambda e, o=psum, l=lhsT, r=rhs, st=start, sp=stop:
                    e.matmul(o, l, r, start=st, stop=sp),
                    waits=waits, inc=inc)

    def transpose(psum, in_, waits=(), inc=1):
        return P.op("tensor",
                    lambda e, o=psum, i=in_: e.transpose(o, i, ident_b[:, :]),
                    waits=waits, inc=inc)

    def layernorm(src, dst, waits):
        e_s = vred(tcol[:, 1:2], src, waits=list(waits) + [P.now("vector")])
        e_q = act(ps_op[:, 0:H], src, AF.Square, accum_out=tcol[:, 2:3],
                  waits=list(waits) + [P.now("scalar"), P.now("tensor"),
                                       P.now("vector")])
        e_mu = ts(tcol[:, 3:4], tcol[:, 1:2], 1.0 / H, ALU.mult, waits=[e_s])
        e_e2 = ts(tcol[:, 4:5], tcol[:, 2:3], 1.0 / H, ALU.mult, waits=[e_q])
        e_m2 = tt(tcol[:, 5:6], tcol[:, 3:4], tcol[:, 3:4], ALU.mult, waits=[e_mu])
        e_v = tt(tcol[:, 6:7], tcol[:, 4:5], tcol[:, 5:6], ALU.subtract,
                 waits=[e_e2, e_m2])
        e_ln = act(tcol[:, 7:8], tcol[:, 6:7], AF.Ln, bias=EPS, waits=[e_v])
        e_r = act(tcol[:, 8:9], tcol[:, 7:8], AF.Exp, scale=-0.5, waits=[e_ln])
        e_mn = ts(tcol[:, 9:10], tcol[:, 3:4], -1.0, ALU.mult, waits=[e_mu])
        e_bc = tt(tcol[:, 10:11], tcol[:, 9:10], tcol[:, 8:9], ALU.mult,
                  waits=[e_r, e_mn])
        return act(dst, src, AF.Identity, scale=tcol[:, 8:9], bias=tcol[:, 10:11],
                   waits=[e_bc])

    CONST_VALS = [EPS, 1.0, -1.0, -0.5, 1.0 / 127.0]
    for ci, cv in enumerate(CONST_VALS):
        P.op("gpsimd", lambda e, i=ci, v=cv: e.memset(consts[:, i:i + 1], v),
             inc=1)
        nc.const_aps.register(F32, cv, consts[:, ci:ci + 1])
    e_idf = P.op("gpsimd", lambda e: e.memset(ident_b[:, :], 0.0), inc=1)
    P.op("gpsimd",
         lambda e: e.affine_select(out=ident_b[:, :], in_=ident_b[:, :],
                                   compare_op=ALU.not_equal, fill=1.0,
                                   base=0, pattern=[[-1, PD]],
                                   channel_multiplier=1),
         waits=[e_idf], inc=1)
    P.op("gpsimd", lambda e: e.memset(onesb[:, :], 1.0), inc=1)
    e_s16a = P.op("gpsimd", lambda e: e.memset(sel16[:, :], 0.0), inc=1)
    P.op("gpsimd",
         lambda e: e.affine_select(out=sel16.rearrange("c (n p) -> c n p", n=NST),
                                   in_=sel16.rearrange("c (n p) -> c n p", n=NST),
                                   compare_op=ALU.not_equal, fill=1.0, base=0,
                                   pattern=[[-1, NST], [0, PD]],
                                   channel_multiplier=1),
         waits=[e_s16a], inc=1)
    P.op("gpsimd", lambda e: e.memset(states[:, :], 0.0), inc=1)
    P.op("gpsimd", lambda e: e.memset(yg[:, :], 0.0), inc=1)
    P.op("gpsimd", lambda e: e.memset(halo[:, :], 0.0), inc=1)
    e_pro = P.now("gpsimd")

    small = [(acols, alog_d), (cvw, cvw_d), (cvb, cvb_d), (dtb, dtb_d),
             (dvec, dvec_d), (inb, inb_d), (lnw, lnw_d), (lnb, lnb_d)]
    e_small = [ldma(t[:, :], d, stream="ld_s") for (t, d) in small]
    e_ob = ldma(obrow[:, :], outb_d, stream="ld_s")
    ldma(cfg_s[:, :], cfg_d, stream="ld_s")
    e_lds = P.now("ld_s")
    P.cfg_ap = (cfg_s[0:1, 0:1], cfg_s[0:1, 1:2])
    P.cfg_ready = e_lds
    e_a1 = act(acols[:, :], acols[:, :], AF.Exp, waits=[e_lds])
    ts(acols[:, :], acols[:, :], -1.0, ALU.mult, waits=[e_a1])

    e_obb = act(obrow_b[:, :], obrow[:, :], AF.Copy, waits=[e_lds])
    e_obm = [mm(ps_op[:, hh * 512:(hh + 1) * 512], onesb[:, :],
                obrow_b[:, hh * 512:(hh + 1) * 512], True, True,
                waits=[e_obb, e_pro]) for hh in range(2)]
    vcopy(outb_t[:, :], ps_op[:, :], waits=e_obm)

    def load_cast(dst_sb, dram, rows, cols, col0_dst, prev, scale=None):
        stage_cols = 2048
        for c0 in range(0, cols, stage_cols):
            cw = min(stage_cols, cols - c0)
            w = [prev] if prev else []
            e_l = ldma(wstage[0:rows, 0:cw], dram[:, c0:c0 + cw], waits=w)
            prev = act(dst_sb[0:rows, col0_dst + c0: col0_dst + c0 + cw],
                       wstage[0:rows, 0:cw], AF.Identity, scale=scale, waits=[e_l])
        return prev

    prev = None
    for hc in range(HC):
        prev = load_cast(inwT, inwT_d[hc * PD:(hc + 1) * PD, :], PD,
                         2 * INNER, hc * 2 * INNER, prev)
    for g in range(G):
        prev = load_cast(outwT, outwT_d[g * PD:(g + 1) * PD, :], PD, H,
                         g * H, prev)
        prev = load_cast(xpwT, xpwT_d[g * PD:(g + 1) * PD, :], PD, 96,
                         g * 96, prev)
    prev = load_cast(dtpwT, dtpwT_d, RDT, G * PD, 0, prev)
    P.barrier()

    ps_tr = ps_bb2.bitcast(BF16)

    def emit_chunk(c, is_own):
        for ti in range(TS):
            row0 = c * T + ti * PD
            htile = hidt[ti % 2][:, :]
            e_h = ldma(htile, hid_d[row0:row0 + PD, :],
                       waits=[P.now("scalar"), P.now("vector"),
                              P.now("tensor")], stream="ld_h%d" % (ti % 2))
            sumc, sqc = tcol[:, 1:2], tcol[:, 2:3]
            e_s = vred(sumc, htile, waits=[e_h, P.now("vector")])
            e_q = act(ps_op[:, 0:H], htile, AF.Square, accum_out=sqc,
                      waits=[e_h, P.now("scalar"), P.now("tensor")])
            mu = m1c[:, ti:ti + 1]
            e_mu = ts(mu, sumc, 1.0 / H, ALU.mult, waits=[e_s])
            e_e2 = ts(tcol[:, 4:5], sqc, 1.0 / H, ALU.mult, waits=[e_q])
            e_m2 = tt(tcol[:, 5:6], mu, mu, ALU.mult, waits=[e_mu])
            e_v = tt(tcol[:, 6:7], tcol[:, 4:5], tcol[:, 5:6], ALU.subtract,
                     waits=[e_e2, e_m2])
            e_ln = act(tcol[:, 7:8], tcol[:, 6:7], AF.Ln, bias=EPS, waits=[e_v])
            r1 = r1c[:, ti:ti + 1]
            e_r1 = act(r1, tcol[:, 7:8], AF.Exp, scale=-0.5, waits=[e_ln])
            e_r1sq = tt(tcol[:, 5:6], r1, r1, ALU.mult, waits=[e_r1])
            e_v2 = tt(tcol[:, 13:14], tcol[:, 6:7], tcol[:, 5:6], ALU.mult,
                      waits=[e_r1sq, e_v])
            e_l2 = act(tcol[:, 14:15], tcol[:, 13:14], AF.Ln, bias=EPS,
                       waits=[e_v2, e_r1])
            e_r2 = act(tcol[:, 9:10], tcol[:, 14:15], AF.Exp, scale=-0.5,
                       waits=[e_l2])
            e_sc = tt(tcol[:, 10:11], r1, tcol[:, 9:10], ALU.mult, waits=[e_r2])
            e_mn = ts(tcol[:, 11:12], mu, -1.0, ALU.mult, waits=[e_mu])
            e_bc = tt(tcol[:, 12:13], tcol[:, 11:12], tcol[:, 10:11], ALU.mult,
                      waits=[e_sc, e_mn])
            e_u = act(xnb[:, :], htile, AF.Identity, scale=tcol[:, 10:11],
                      bias=tcol[:, 12:13], waits=[e_bc, P.now("tensor")])
            for h2 in range(HC):
                psv = ps_tr[:, (h2 % 2) * PD:(h2 % 2 + 1) * PD]
                e_tr = transpose(psv, xnb[:, h2 * PD:(h2 + 1) * PD],
                                 waits=[e_u, P.now("scalar")])
                act(hnT[:, h2 * T + ti * PD: h2 * T + (ti + 1) * PD],
                    psv, AF.Identity, scale=lnw[:, h2:h2 + 1],
                    bias=lnb[:, h2:h2 + 1], waits=[e_tr])
        P.barrier()

        for g in range(G):
            xgc = xg[g % 2]
            pp = ps_a if g % 2 == 0 else ps_b
            last = None
            for h2 in range(HC):
                last = mm(pp[:, :], inwT[:, h2 * 2 * INNER + g * PD:
                                         h2 * 2 * INNER + (g + 1) * PD],
                          hnT[:, h2 * T:(h2 + 1) * T],
                          start=(h2 == 0), stop=(h2 == HC - 1),
                          waits=([P.now("scalar")] if h2 == 0 else []),
                          inc=(1 if h2 == HC - 1 else 0))
            e_x = act(xgc[:, 3:3 + T], pp[:, :], AF.Identity,
                      bias=inb[:, g:g + 1], waits=[last, P.now("vector")])
            e_hl = vcopy(xgc[:, 0:3], halo[:, g * 3:(g + 1) * 3],
                         waits=[e_x, P.now("vector")])
            zbc = zb[g % 2][:, 0:T]
            e_ck = ts(zbc, xgc[:, 0:T], cvw[:, g * KCV:g * KCV + 1],
                      ALU.mult, waits=[e_x, e_hl, P.now("scalar")])
            for k in range(1, KCV):
                e_ck = stt(zbc, xgc[:, k:k + T],
                           cvw[:, g * KCV + k:g * KCV + k + 1], zbc,
                           ALU.mult, ALU.add, waits=[e_ck])
            vcopy(halo[:, g * 3:(g + 1) * 3], xgc[:, T:T + 3], waits=[e_x])
            act(xcb[:, g * T:(g + 1) * T], zbc, AF.Silu,
                bias=cvb[:, g:g + 1], waits=[e_ck])
        if is_own:
            for g in range(G):
                pp = ps_a if g % 2 == 0 else ps_b
                last = None
                for h2 in range(HC):
                    last = mm(pp[:, :],
                              inwT[:, h2 * 2 * INNER + INNER + g * PD:
                                   h2 * 2 * INNER + INNER + (g + 1) * PD],
                              hnT[:, h2 * T:(h2 + 1) * T],
                              start=(h2 == 0), stop=(h2 == HC - 1),
                              waits=([P.now("scalar")] if h2 == 0 else []),
                              inc=(1 if h2 == HC - 1 else 0))
                act(yg[:, g * T:(g + 1) * T], pp[:, :], AF.Silu,
                    bias=inb[:, G + g:G + g + 1], waits=[last])
        P.barrier()

        w0 = [P.now("scalar")]
        for g in range(G):
            e_g1 = mm(ps_xp[0:16, :], xpwT[:, g * 96 + 64:g * 96 + 80],
                      xcb[:, g * T:(g + 1) * T], start=(g == 0),
                      stop=(g == G - 1), waits=(w0 if g == 0 else []),
                      inc=(1 if g == G - 1 else 0))
        e_bb = act(brow_b[:, :], ps_xp[0:16, :], AF.Copy, waits=[e_g1])
        e_cb = None
        if is_own:
            for g in range(G):
                e_g2 = mm(ps_xp[32:48, :], xpwT[:, g * 96 + 80:g * 96 + 96],
                          xcb[:, g * T:(g + 1) * T], start=(g == 0),
                          stop=(g == G - 1),
                          waits=([e_bb] if g == 0 else []),
                          inc=(1 if g == G - 1 else 0))
            e_cb = act(crow_b[:, :], ps_xp[32:48, :], AF.Copy, waits=[e_g2])
        for g in range(G):
            e_g3 = mm(ps_xp[64:128, :], xpwT[:, g * 96:g * 96 + 64],
                      xcb[:, g * T:(g + 1) * T], start=(g == 0),
                      stop=(g == G - 1),
                      waits=([e_cb if e_cb else e_bb] if g == 0 else []),
                      inc=(1 if g == G - 1 else 0))
        e_db = act(dtr_b[:, :], ps_xp[64:128, :], AF.Copy, waits=[e_g3])
        P.barrier()

        prot_user = [None] * 4
        yac_user = [None, None]
        for sg in range(NSG):
            g0 = sg * SGN
            for i in range(SGN):
                g = g0 + i
                pp = ps_a if i % 2 == 0 else ps_b
                e_m = mm(pp[:, :], dtpwT[:, g * PD:(g + 1) * PD],
                         dtr_b[:, :], True, True,
                         waits=[e_db, P.now("scalar")])
                e_z = act(zb[0][:, 0:T], pp[:, :], AF.Identity,
                          bias=dtb[:, g:g + 1],
                          waits=[e_m, P.now("vector")])
                o0 = i * T
                e_ab = act(dtx_sg[:, o0:o0 + T], zb[0][:, 0:T], AF.Abs,
                           waits=[e_z, P.now("vector")])
                e_ex = act(dtx_sg[:, o0:o0 + T], dtx_sg[:, o0:o0 + T],
                           AF.Exp, scale=-1.0, waits=[e_ab])
                e_ln1 = act(dtx_sg[:, o0:o0 + T], dtx_sg[:, o0:o0 + T],
                            AF.Ln, bias=1.0, waits=[e_ex])
                e_rl = act(dt_sg[:, o0:o0 + T], zb[0][:, 0:T], AF.Relu,
                           waits=[e_ln1])
                tt(dt_sg[:, o0:o0 + T], dt_sg[:, o0:o0 + T],
                   dtx_sg[:, o0:o0 + T], ALU.add, waits=[e_rl])
            Wf = SGN * T
            e_dt = P.now("vector")
            e_dtx = tt(dtx_sg[:, 0:Wf], dt_sg[:, 0:Wf],
                       xcb[:, g0 * T:(g0 + SGN) * T], ALU.mult,
                       waits=[e_dt, P.now("scalar")])
            dec_user = [None, None]
            bb_user = [None, None]
            bbe_user = [None, None]
            cc_user = None
            cb_user = [None, None]
            s_user = [None, None]
            pend_p = [None, None]
            for n in range(NST):
                kb = n % 2
                pbb = ps_bb if kb == 0 else ps_bb2
                e_bm = mm(pbb[:, :], sel16[:, n * PD:(n + 1) * PD],
                          brow_b[:, :], True, True,
                          waits=[e_bb] + ([bbe_user[kb]] if bbe_user[kb] else
                                          [P.now("vector"),
                                           P.now("scalar")]))
                e_be = act(bbr[kb][:, :], pbb[:, :], AF.Copy,
                           waits=[e_bm] + ([bb_user[kb]] if bb_user[kb]
                                           else []))
                bbe_user[kb] = e_be
                e_cv = None
                if is_own:
                    e_cm = mm(ps_xp[:, :], sel16[:, n * PD:(n + 1) * PD],
                              crow_b[:, :], True, True,
                              waits=[e_cb] + ([cc_user] if cc_user else
                                              [P.now("vector")]))
                    kc = n % 2
                    e_cv = act(cbcr[kc][:, :], ps_xp[:, :], AF.Copy,
                               waits=[e_cm] + ([cb_user[kc]] if cb_user[kc]
                                               else []))
                    cc_user = e_cv
                for i in range(SGN):
                    g = g0 + i
                    sl = i % 2
                    kd = (n * SGN + i) % 2
                    e_de = act(dec[kd][:, :], dt_sg[:, i * T:(i + 1) * T],
                               AF.Exp,
                               scale=acols[:, g * NST + n:g * NST + n + 1],
                               waits=([dec_user[kd]] if dec_user[kd]
                                      else [P.now("vector")]))
                    e_w = gtt(wti[kd][:, :], dtx_sg[:, i * T:(i + 1) * T],
                              bbr[kb][:, :], ALU.mult,
                              waits=[e_dtx, e_be] +
                              ([dec_user[kd]] if dec_user[kd] else []))
                    bb_user[kb] = e_w
                    e_sc = scan(srot[:, sl * T:(sl + 1) * T], dec[kd][:, :],
                                wti[kd][:, :],
                                states[:, g * NST + n:g * NST + n + 1],
                                waits=[e_de, e_w] +
                                ([s_user[sl]] if s_user[sl] else []))
                    dec_user[kd] = e_sc
                    s_user[sl] = e_sc
                    if is_own:
                        kc = n % 2
                        kp = (n % 2) * 2 + i
                        e_p = tt(prot[kp][:, :], srot[:, sl * T:(sl + 1) * T],
                                 cbcr[kc][:, :], ALU.mult,
                                 waits=[e_sc, e_cv] +
                                 ([prot_user[kp]] if prot_user[kp] else []))
                        cb_user[kc] = e_p
                        s_user[sl] = e_p
                        if pend_p[i] is not None:
                            pk, pe = pend_p[i]
                            w = [pe]
                            if n == 1:
                                w.append(yac_user[i] if yac_user[i]
                                         else P.now("scalar"))
                            prot_user[pk] = mm(
                                ps_op[:, i * 512:(i + 1) * 512],
                                ident_b[:, :], prot[pk][:, :],
                                start=(n == 1), stop=False, waits=w)
                        pend_p[i] = (kp, e_p)
                    if sl == 1:
                        sv = srot.rearrange("p (i t) -> p i t", i=2)[
                            :, :, T - 1:T]
                        dv = states.rearrange("p (g n) -> p g n", g=G)[
                            :, g0 + i - 1:g0 + i + 1, n:n + 1]
                        e_xt = vcopy(dv, sv, waits=[])
                        s_user[0] = e_xt
                        s_user[1] = e_xt
            if is_own:
                for i in range(SGN):
                    g = g0 + i
                    pk, pe = pend_p[i]
                    e_stop = mm(ps_op[:, i * 512:(i + 1) * 512],
                                ident_b[:, :], prot[pk][:, :],
                                start=False, stop=True, waits=[pe])
                    prot_user[pk] = e_stop
                    e_yf = stt(yac[:, i * T:(i + 1) * T],
                               xcb[:, g * T:(g + 1) * T], dvec[:, g:g + 1],
                               ps_op[:, i * 512:(i + 1) * 512],
                               ALU.mult, ALU.add, waits=[e_stop])
                    yac_user[i] = e_yf
                    tt(yg[:, g * T:(g + 1) * T], yac[:, i * T:(i + 1) * T],
                       yg[:, g * T:(g + 1) * T], ALU.mult,
                       waits=[e_yf])
        P.barrier()

        if not is_own:
            return
        for t2 in range(TS):
            last = None
            for g in range(G):
                for hh in range(2):
                    last = mm(ps_op[:, hh * 512:(hh + 1) * 512],
                              yg[:, g * T + t2 * PD:g * T + (t2 + 1) * PD],
                              outwT[:, g * H + hh * 512:g * H + (hh + 1) * 512],
                              start=(g == 0), stop=(g == G - 1),
                              waits=([P.now("scalar"), P.now("vector")]
                                     if g == 0 else []),
                              inc=(1 if (g == G - 1 and hh == 1) else 0))
            row0 = c * T + t2 * PD
            htile = hidt[t2 % 2][:, :]
            e_h = ldma(htile, hid_d[row0:row0 + PD, :],
                       waits=[P.now("scalar"), P.now("vector")],
                       stream="ld_h%d" % (t2 % 2))
            mu, r1 = m1c[:, t2:t2 + 1], r1c[:, t2:t2 + 1]
            e_mb = tt(tcol[:, 0:1], mu, r1, ALU.mult, waits=[P.now("vector")])
            e_mb2 = ts(tcol[:, 0:1], tcol[:, 0:1], -1.0, ALU.mult,
                       waits=[e_mb])
            e_l1 = act(attn[:, :], htile, AF.Identity, scale=r1,
                       bias=tcol[:, 0:1], waits=[e_mb2, e_h, P.now("scalar")])
            e_a1 = tt(attn[:, :], attn[:, :], ps_op[:, :], ALU.add,
                      waits=[e_l1, last])
            e_a2 = tt(attn[:, :], attn[:, :], outb_t[:, :], ALU.add,
                      waits=[e_a1])
            e_n2 = layernorm(attn[:, :], attn[:, :], [e_a2])
            e_x1 = tt(x1t[:, :], htile, attn[:, :], ALU.add,
                      waits=[e_n2, P.now("st")])
            sdma(x1sp_d[row0:row0 + PD, :], x1t[:, :], waits=[e_x1])
            e_n3 = layernorm(x1t[:, :], xnb[:, :], [e_x1])
            for h2 in range(HC):
                psv = ps_tr[:, (h2 % 2) * PD:(h2 % 2 + 1) * PD]
                e_tr = transpose(psv, xnb[:, h2 * PD:(h2 + 1) * PD],
                                 waits=[e_n3, P.now("vector")])
                vcopy(xnTs[:, h2 * PD:(h2 + 1) * PD], psv,
                      waits=[e_tr] + ([P.now("st")] if h2 == 0 else []))
            sdma(xnTsp_d[:, row0:row0 + PD].rearrange(
                "(h p) q -> p h q", p=PD),
                xnTs.rearrange("p (h q) -> p h q", h=HC),
                waits=[P.now("vector")])

    for c in range(NCH):
        P.begin_branch(c, 0)
        emit_chunk(c, True)
        P.end_branch()
        P.begin_branch(c, 1)
        emit_chunk(c, False)
        P.end_branch()
        P.barrier()

    P.barrier()
    sweep_es.close()

    ffn_es = contextlib.ExitStack()
    FW = lambda name, shape, dt: ffn_es.enter_context(nc.sbuf_tensor(name, shape, dt))
    wg_sb = FW("wg_sb", [PD, HC * FF], BF16)
    wu_sb = FW("wu_sb", [PD, HC * FF], BF16)
    wd_sb = FW("wd_sb", [PD, FC * H], BF16)

    DQ = 1.0 / 127.0
    fst_es = contextlib.ExitStack()
    fstage = [fst_es.enter_context(nc.sbuf_tensor(f"fstage{i}", [PD, FF], I32))
              for i in range(4)]
    cast_ev = [None] * 4
    WLQ = ("ld_f", "ld_x", "ld_s", "ld_h0")

    def wcast(dst, src, cols, k):
        sl = fstage[k % 4]
        e_l = ldma(sl[:, 0:cols], src,
                   waits=([cast_ev[k % 4]] if cast_ev[k % 4] else []),
                   stream=WLQ[k % 4])
        cast_ev[k % 4] = act(dst, sl[:, 0:cols], AF.Identity, scale=DQ,
                             waits=[e_l])
        return cast_ev[k % 4]

    wk = 0
    for hc2 in range(HC):
        e_wg = wcast(wg_sb[:, hc2 * FF:(hc2 + 1) * FF],
                     wgT_d[hc2 * PD:(hc2 + 1) * PD, :], FF, wk)
        wk += 1
    for hc2 in range(HC):
        e_wu = wcast(wu_sb[:, hc2 * FF:(hc2 + 1) * FF],
                     wuT_d[hc2 * PD:(hc2 + 1) * PD, :], FF, wk)
        wk += 1
    for f in range(FC):
        e_wd = wcast(wd_sb[:, f * H:(f + 1) * H],
                     wdT_d[f * PD:(f + 1) * PD, :], H, wk)
        wk += 1
    fst_es.close()

    frhs2 = [FW(f"frhs{i}", [PD, HC * T2], BF16) for i in range(2)]
    hbuf = FW("hbuf", [PD, FC * T2], BF16)
    sgr2 = [FW(f"sgr{i}", [PD, T2], F32) for i in range(2)]
    f_x1t = FW("f_x1t", [PD, H], F32)
    f_sq = FW("f_sq", [PD, H], F32)
    f_ev = FW("f_ev", [PD, H], F32)

    last_silu = [None, None]
    last_tt = [None, None]
    frhs_user = [None, None]
    fx1_user = None
    evac_ev = [None, None]
    sd_ev = None
    down_prev = None
    eo_last = None
    n_half = L // T2
    for j in range(n_half):
        row0 = j * T2
        if "ffn" in GUARDS:
            P.begin_branch(j)
        fr = frhs2[j % 2]
        wf = [frhs_user[j % 2]] if frhs_user[j % 2] else []
        for hc2 in range(HC):
            ldma(fr[:, hc2 * T2:(hc2 + 1) * T2],
                 xnTsp_d[hc2 * PD:(hc2 + 1) * PD, row0:row0 + T2],
                 waits=(wf if hc2 == 0 else []), stream="sync")
        e_r = P.now("sync")
        tts = []
        for fo in range(FC):
            bp = fo % 2
            bankG = (ps_a, ps_xp)[bp]
            bankU = (ps_b, ps_bb)[bp]
            wg0 = [e_r, e_wg] if fo == 0 else []
            if last_silu[bp]:
                wg0 = wg0 + [last_silu[bp]]
            lg = None
            for hc2 in range(HC):
                lg = mm(bankG[:, 0:T2],
                        wg_sb[:, hc2 * FF + fo * PD:hc2 * FF + (fo + 1) * PD],
                        fr[:, hc2 * T2:(hc2 + 1) * T2],
                        start=(hc2 == 0), stop=(hc2 == HC - 1),
                        waits=(wg0 if hc2 == 0 else []),
                        inc=(1 if hc2 == HC - 1 else 0))
            wu0 = [e_wu] if fo == 0 else []
            if last_tt[bp]:
                wu0 = wu0 + [last_tt[bp]]
            lu = None
            for hc2 in range(HC):
                lu = mm(bankU[:, 0:T2],
                        wu_sb[:, hc2 * FF + fo * PD:hc2 * FF + (fo + 1) * PD],
                        fr[:, hc2 * T2:(hc2 + 1) * T2],
                        start=(hc2 == 0), stop=(hc2 == HC - 1),
                        waits=(wu0 if hc2 == 0 else []),
                        inc=(1 if hc2 == HC - 1 else 0))
            frhs_user[j % 2] = lu
            e_sg = act(sgr2[bp][:, :], bankG[:, 0:T2], AF.Silu,
                       waits=[lg] + ([last_tt[bp]] if last_tt[bp] else []))
            last_silu[bp] = e_sg
            wt = [e_sg, lu]
            if fo == 0 and down_prev:
                wt.append(down_prev)
            last_tt[bp] = tt(hbuf[:, fo * T2:(fo + 1) * T2], sgr2[bp][:, :],
                             bankU[:, 0:T2], ALU.mult, waits=wt)
            tts.append(last_tt[bp])
        for i in range(TS2):
            sl = i % 2
            e_x1 = ldma(f_x1t[:, :],
                        x1sp_d[row0 + i * PD:row0 + (i + 1) * PD, :],
                        waits=([fx1_user] if fx1_user else []),
                        stream="ld_h1")
            halves = ((ps_op[:, 0:512], ps_op[:, 512:1024]) if sl == 0
                      else (ps_bb2[:, 0:512], ps_f[:, 0:512]))
            m_hh = [None, None]
            for fo in range(FC):
                for hh in range(2):
                    w = []
                    if fo == 0 and hh == 0:
                        w = [e_wd] + ([evac_ev[sl]] if evac_ev[sl] else [])
                    if hh == 0 and i == 0:
                        w = w + [tts[fo]]
                    m_hh[hh] = mm(halves[hh],
                                  hbuf[:, fo * T2 + i * PD:fo * T2 + (i + 1) * PD],
                                  wd_sb[:, fo * H + hh * 512:fo * H + (hh + 1) * 512],
                                  start=(fo == 0), stop=(fo == FC - 1),
                                  waits=w,
                                  inc=(1 if fo == FC - 1 else 0))
            b = 1 + 12 * sl
            e_ev1 = act(f_ev[:, 0:512], halves[0], AF.Copy,
                        waits=[m_hh[0]] + ([eo_last] if eo_last else []))
            e_ev2 = act(f_ev[:, 512:1024], halves[1], AF.Copy, waits=[m_hh[1]])
            evac_ev[sl] = e_ev2
            e_s = vred(tcol[:, b:b + 1], f_ev[:, :], waits=[e_ev2])
            e_q = act(f_sq[:, :], f_ev[:, :], AF.Square,
                      accum_out=tcol[:, b + 1:b + 2],
                      waits=([sd_ev] if sd_ev else []))
            e_mu = ts(tcol[:, b + 2:b + 3], tcol[:, b:b + 1], 1.0 / H,
                      ALU.mult, waits=[e_s])
            e_e2 = ts(tcol[:, b + 3:b + 4], tcol[:, b + 1:b + 2], 1.0 / H,
                      ALU.mult, waits=[e_q])
            e_m2 = tt(tcol[:, b + 4:b + 5], tcol[:, b + 2:b + 3],
                      tcol[:, b + 2:b + 3], ALU.mult, waits=[e_mu])
            e_v = tt(tcol[:, b + 5:b + 6], tcol[:, b + 3:b + 4],
                     tcol[:, b + 4:b + 5], ALU.subtract, waits=[e_e2, e_m2])
            e_ln = act(tcol[:, b + 6:b + 7], tcol[:, b + 5:b + 6], AF.Ln,
                       bias=EPS, waits=[e_v])
            e_rs = act(tcol[:, b + 7:b + 8], tcol[:, b + 6:b + 7], AF.Exp,
                       scale=-0.5, waits=[e_ln])
            e_mn = ts(tcol[:, b + 8:b + 9], tcol[:, b + 2:b + 3], -1.0,
                      ALU.mult, waits=[e_mu])
            e_bc = tt(tcol[:, b + 9:b + 10], tcol[:, b + 8:b + 9],
                      tcol[:, b + 7:b + 8], ALU.mult, waits=[e_rs, e_mn])
            e_n4 = act(f_ev[:, :], f_ev[:, :], AF.Identity,
                       scale=tcol[:, b + 7:b + 8], bias=tcol[:, b + 9:b + 10],
                       waits=[e_bc])
            e_o = tt(f_sq[:, :], f_x1t[:, :], f_ev[:, :], ALU.add,
                     waits=[e_n4, e_x1])
            sd_ev = sdma(out_d[row0 + i * PD:row0 + (i + 1) * PD, :],
                         f_sq[:, :], waits=[e_o])
            fx1_user = e_o
            eo_last = e_o
            if i == TS2 - 1:
                down_prev = m_hh[1]
        if "ffn" in GUARDS:
            P.end_branch()

    P.op("gpsimd", None, waits=[P.now("st")], inc=0)
    P.barrier()
    ffn_es.close()

    with es, nc.Block() as block:
        P.emit(block)
    return nc


def _prep_weights(inp):
    g = lambda k: np.ascontiguousarray(inp[k])
    w = {}
    w["inwT_dr"] = np.ascontiguousarray(g("L0_attn_in_w").T)
    w["outwT_dr"] = np.ascontiguousarray(g("L0_attn_out_w").T)
    w["xpwT_dr"] = np.ascontiguousarray(g("L0_attn_xp_w").T)
    w["dtpwT_dr"] = np.ascontiguousarray(g("L0_attn_dtp_w").T)
    w["lnw_dr"] = np.ascontiguousarray(g("L0_attn_ln_w").reshape(HC, PD).T)
    w["lnb_dr"] = np.ascontiguousarray(g("L0_attn_ln_b").reshape(HC, PD).T)
    w["inb_dr"] = np.ascontiguousarray(g("L0_attn_in_b").reshape(2 * G, PD).T)
    w["cvw_dr"] = np.ascontiguousarray(
        g("L0_attn_conv_w").reshape(G, PD, KCV).transpose(1, 0, 2).reshape(PD, G * KCV))
    w["cvb_dr"] = np.ascontiguousarray(g("L0_attn_conv_b").reshape(G, PD).T)
    w["dtb_dr"] = np.ascontiguousarray(g("L0_attn_dtp_b").reshape(G, PD).T)
    w["alog_dr"] = np.ascontiguousarray(
        g("L0_attn_A_log").reshape(G, PD, NST).transpose(1, 0, 2).reshape(PD, G * NST))
    w["dvec_dr"] = np.ascontiguousarray(g("L0_attn_D").reshape(G, PD).T)
    w["outb_dr"] = np.ascontiguousarray(g("L0_attn_out_b").reshape(1, H))
    w["wgT_dr"] = np.ascontiguousarray(g("L0_ffn_w_gate").T)
    w["wuT_dr"] = np.ascontiguousarray(g("L0_ffn_w_up").T)
    w["wdT_dr"] = np.ascontiguousarray(g("L0_ffn_w_down").T)
    return w


def kernel(**inputs):
    L = inputs["L0_attn_hidden"].shape[1]
    NCH = L // 512
    w = _prep_weights(inputs)
    nc = build(L)
    bounds = [0, NCH // 4, NCH // 2, 3 * NCH // 4, NCH]
    in_maps = []
    for core in range(8):
        b, q = core // 4, core % 4
        own = 0
        for c in range(bounds[q], bounds[q + 1]):
            own |= 1 << c
        scanonly = 0
        for c in range(0, bounds[q]):
            scanonly |= 1 << c
        m = dict(w)
        m["hidden"] = np.ascontiguousarray(
            np.asarray(inputs["L0_attn_hidden"][b], dtype=np.float32))
        m["cfg_dr"] = np.array([[own, scanonly, 0, 0]],
                               dtype=np.uint32).view(np.int32)
        in_maps.append(m)
    res = run_bass_kernel_spmd(nc, in_maps, core_ids=list(range(8)))
    out = np.empty((2, L, H), np.float32)
    for core in range(8):
        b, q = core // 4, core % 4
        r0, r1 = bounds[q] * 512, bounds[q + 1] * 512
        out[b, r0:r1] = res.results[core]["out"][r0:r1]
    return out
